# version 05
# speedup vs baseline: 1.1986x; 1.0645x over previous
import contextlib
import numpy as np
import ml_dtypes
import concourse.bass as bass
import concourse.mybir as mybir
from concourse.bass_utils import run_bass_kernel_spmd

F32 = mybir.dt.float32
BF16 = mybir.dt.bfloat16
AF = mybir.ActivationFunctionType
ALU = mybir.AluOpType
AX = mybir.AxisListType

NCORES = 8
D = 2048
NTOK = 2048
BAND = 2560
FRONT = 256
KVLEN = BAND + FRONT

BF = ml_dtypes.bfloat16

ENGS = ("sync", "tensor", "vector", "scalar", "gpsimd")


class Walk:

    def __init__(self, nc, sems, en):
        self.nc = nc
        self.sems = sems
        self.en = en
        self.c = {}
        self.last = {}
        self.dirty = {}
        self.lastinc = {}
        self.ftarget = {}

    def bump(self, name, n=1):
        self.c[name] = self.c.get(name, 0) + n
        return self.c[name]

    def val(self, name):
        return self.c.get(name, 0)

    def op(self, eng, method, *a, **k):
        self.dirty[eng] = True
        self.lastinc[eng] = None
        if self.en == eng:
            ins = getattr(getattr(self.nc, eng), method)(*a, **k)
            self.last[eng] = ins
            return ins
        return None

    def inc_last(self, eng, sem):
        assert self.lastinc.get(eng) is None
        v = self.bump(sem)
        self.lastinc[eng] = (sem, v)
        if self.en == eng:
            self.last[eng].then_inc(self.sems[sem], 1)
        return v

    def dma(self, eng, out, in_, sem, transpose=False):
        self.dirty[eng] = True
        v = self.bump(sem, 16)
        self.lastinc[eng] = (sem, v)
        if self.en == eng:
            kw = {"transpose": True} if transpose else {}
            ins = getattr(self.nc, eng).dma_start(out=out, in_=in_, **kw)
            ins.then_inc(self.sems[sem], 16)
            self.last[eng] = ins
        return v

    def wait(self, eng, sem, v=None):
        if v is None:
            v = self.val(sem)
        if self.en == eng:
            getattr(self.nc, eng).wait_ge(self.sems[sem], v)

    def fence(self, prods, conss):
        if isinstance(prods, str):
            prods = (prods,)
        if isinstance(conss, str):
            conss = (conss,)
        for p in prods:
            if self.dirty.get(p):
                li = self.lastinc.get(p)
                if li is not None:
                    self.ftarget[p] = li
                else:
                    v = self.bump("F_" + p)
                    if self.en == p:
                        self.last[p].then_inc(self.sems["F_" + p], 1)
                    self.lastinc[p] = ("F_" + p, v)
                    self.ftarget[p] = ("F_" + p, v)
                self.dirty[p] = False
        for ce in conss:
            for p in prods:
                t = self.ftarget.get(p)
                if t is not None and self.en == ce:
                    getattr(self.nc, ce).wait_ge(self.sems[t[0]], t[1])

    def pipe(self, n, load, mm, evac, evac_eng, nbuf=2, npsum=2,
             dma_eng="gpsimd"):
        if isinstance(dma_eng, str):
            dma_eng = (dma_eng,)
        pp0 = self.val("PP")
        pp_after = [0] * n
        sc_after = [0] * n
        for j in range(n):
            if load is not None:
                if nbuf == 0:
                    self.fence("tensor", dma_eng)
                elif j >= nbuf:
                    for de in dma_eng:
                        self.wait(de, "PP", pp_after[j - nbuf])
                else:
                    for de in dma_eng:
                        self.wait(de, "PP", pp0)
                load(j)
                self.wait("tensor", "WT")
                self.wait("tensor", "LD")
                self.wait("tensor", "LS")
            if j >= npsum:
                self.wait("tensor", "SC", sc_after[j - npsum])
            mm(j)
            pp_after[j] = self.inc_last("tensor", "PP")
            self.wait("scalar", "PP", pp_after[j])
            self.wait("vector", "PP", pp_after[j])
            if j >= 2:
                self.wait("scalar", "SC", sc_after[j - 2])
                self.wait("vector", "SC", sc_after[j - 2])
            evac(j)
            sc_after[j] = self.inc_last(evac_eng, "SC")
        self.wait("tensor", "SC", sc_after[n - 1])


class Alloc:

    def __init__(self, nc, st, nb_cols_bf16, nf_cols_f32):
        self.ab = st.enter_context(
            nc.sbuf_tensor("arena_b", [128, nb_cols_bf16], BF16))
        self.af = st.enter_context(
            nc.sbuf_tensor("arena_f", [128, nf_cols_f32], F32))
        self.ob = 0
        self.of = 0
        self.nb = nb_cols_bf16
        self.nf = nf_cols_f32

    def b(self, w):
        assert self.ob + w <= self.nb, f"bf16 arena overflow {self.ob}+{w}"
        v = self.ab[:, self.ob:self.ob + w]
        self.ob += w
        return v

    def f(self, w):
        assert self.of + w <= self.nf, f"f32 arena overflow {self.of}+{w}"
        v = self.af[:, self.of:self.of + w]
        self.of += w
        return v

    def save(self):
        return (self.ob, self.of)

    def restore(self, s):
        self.ob, self.of = s


def make_sems(nc, st, extra=()):
    names = ["LD", "WT", "ST", "GD", "CC", "PP", "SC", "PC", "LS"] + ["F_" + e for e in ENGS] + list(extra)
    return {n: st.enter_context(nc.semaphore(n)) for n in names}


def run_walks(nc, st, emit):
    sems = make_sems(nc, st)
    with nc.Block() as blk:
        @blk.sync
        def _(e):
            emit(Walk(nc, sems, "sync"))

        @blk.tensor
        def _(e):
            emit(Walk(nc, sems, "tensor"))

        @blk.vector
        def _(e):
            emit(Walk(nc, sems, "vector"))

        @blk.scalar
        def _(e):
            emit(Walk(nc, sems, "scalar"))

        @blk.gpsimd
        def _(e):
            emit(Walk(nc, sems, "gpsimd"))
    return nc



def ln_tile_g(w, env, src_chunks, dst_chunks, T):
    nch = len(src_chunks)
    nd = nch * 128
    ones_b = env["ones_b"]; onesr = env["onesr"]; sqbuf = env["sqbuf"]
    rows = env["rows"]; bcA = env["bcA"]; bcB = env["bcB"]; tmpb = env["tmpb"]
    ps_st = env["ps_st"]; ps_bA = env["ps_bA"]; ps_bB = env["ps_bB"]
    for c in range(nch):
        w.op("tensor", "matmul", ps_st[0:1, :T], ones_b[:, 0:1],
             src_chunks[c], start=(c == 0), stop=(c == nch - 1))
    for c in range(nch):
        sq = sqbuf[:, (c % 2) * 512:(c % 2) * 512 + T]
        if c >= 2:
            w.fence("tensor", "scalar")
        w.op("scalar", "activation", sq, src_chunks[c], AF.Square)
        w.fence("scalar", "tensor")
        w.op("tensor", "matmul", ps_st[32:33, :T], ones_b[:, 0:1], sq,
             start=(c == 0), stop=(c == nch - 1))
    w.fence("tensor", "vector")
    m = rows[0:1, 0:T]
    va = rows[0:1, 512:512 + T]
    aa = rows[0:1, 1024:1024 + T]
    bb = rows[0:1, 1536:1536 + T]
    w.op("vector", "tensor_scalar", m, ps_st[0:1, :T], 1.0 / nd, None,
         op0=ALU.mult)
    w.op("vector", "tensor_scalar", va, ps_st[32:33, :T], 1.0 / nd,
         None, op0=ALU.mult)
    w.fence("vector", "vector")
    w.op("vector", "tensor_tensor", aa, m, m, op=ALU.mult)
    w.fence("vector", "vector")
    w.op("vector", "tensor_tensor", va, va, aa, op=ALU.subtract)
    w.fence("vector", "vector")
    w.op("vector", "tensor_scalar", va, va, 1e-5, None, op0=ALU.add)
    w.fence("vector", "scalar")
    w.op("scalar", "activation", aa, va, AF.Sqrt)
    w.fence("scalar", "vector")
    w.op("vector", "reciprocal", aa, aa)
    w.fence("vector", "vector")
    w.op("vector", "tensor_tensor", bb, m, aa, op=ALU.mult)
    w.fence("vector", "tensor")
    w.op("tensor", "matmul", ps_bA[:, :T], onesr[0:1, :128], aa,
         start=True, stop=True)
    w.op("tensor", "matmul", ps_bB[:, :T], onesr[0:1, :128], bb,
         start=True, stop=True)
    w.fence("tensor", "scalar")
    w.op("scalar", "copy", bcA[:, :T], ps_bA[:, :T])
    w.op("scalar", "copy", bcB[:, :T], ps_bB[:, :T])
    w.fence("scalar", "vector")
    for c in range(nch):
        w.op("vector", "tensor_tensor", tmpb[:, :T], src_chunks[c],
             bcA[:, :T], op=ALU.mult)
        w.fence("vector", "vector")
        w.op("vector", "tensor_tensor", dst_chunks[c], tmpb[:, :T],
             bcB[:, :T], op=ALU.subtract)
        w.fence("vector", "vector")
    w.fence("vector", ("tensor", "scalar"))


def build_l0a(dbg=False):
    nc = bass.Bass("TRN2", target_bir_lowering=False)
    TQ = 512

    xT_band = nc.dram_tensor("xT_band", [D, BAND], F32, kind="ExternalInput").ap()
    xT_front = nc.dram_tensor("xT_front", [D, FRONT], F32, kind="ExternalInput").ap()
    wq = nc.dram_tensor("wq", [D, D], F32, kind="ExternalInput").ap()
    wk = nc.dram_tensor("wk", [D, 512], F32, kind="ExternalInput").ap()
    wv = nc.dram_tensor("wv", [D, 512], F32, kind="ExternalInput").ap()
    wo = nc.dram_tensor("wo", [D, D], F32, kind="ExternalInput").ap()
    cs_band = nc.dram_tensor("cs_band", [128, BAND], F32, kind="ExternalInput").ap()
    cs_front = nc.dram_tensor("cs_front", [128, FRONT], F32, kind="ExternalInput").ap()
    cs_q = nc.dram_tensor("cs_q", [128, NTOK], F32, kind="ExternalInput").ap()
    cs2_band = nc.dram_tensor("cs2_band", [128, BAND], F32, kind="ExternalInput").ap()
    cs2_front = nc.dram_tensor("cs2_front", [128, FRONT], F32, kind="ExternalInput").ap()
    cs2_q = nc.dram_tensor("cs2_q", [128, NTOK], F32, kind="ExternalInput").ap()
    masks = nc.dram_tensor("masks", [16, 128, 896], F32, kind="ExternalInput").ap()
    identb = nc.dram_tensor("identb", [128, 128], BF16, kind="ExternalInput").ap()
    x1T = nc.dram_tensor("x1T", [D, NTOK], F32, kind="ExternalOutput").ap()
    if dbg:
        h_dbg = nc.dram_tensor("h_dbg", [D, NTOK], F32, kind="ExternalOutput").ap()
        k_dbg = nc.dram_tensor("k_dbg", [512, KVLEN], F32, kind="ExternalOutput").ap()
        vt_dbg = nc.dram_tensor("vt_dbg", [512, KVLEN], F32, kind="ExternalOutput").ap()
        q_dbg = nc.dram_tensor("q_dbg", [D, 512], F32, kind="ExternalOutput").ap()
        ao_dbg = nc.dram_tensor("ao_dbg", [D, 512], F32, kind="ExternalOutput").ap()
        ap_dbg = nc.dram_tensor("ap_dbg", [D, 512], F32, kind="ExternalOutput").ap()

    st = contextlib.ExitStack()
    al = Alloc(nc, st, nb_cols_bf16=91000, nf_cols_f32=5700)
    hq = [al.b(NTOK) for _ in range(16)]
    ht = [al.b(512) for _ in range(16)]
    K = [al.b(KVLEN) for _ in range(4)]
    VT = [al.b(KVLEN) for _ in range(4)]
    q2 = [al.b(TQ) for _ in range(16)]
    ao2 = [al.b(TQ) for _ in range(16)]
    wbuf = [al.b(D) for _ in range(2)]
    sqbuf = al.b(1024)
    bcA = al.b(512)
    bcB = al.b(512)
    tmpb = al.b(512)
    cs_t = al.b(512)
    cs_t2 = al.b(512)
    scr = al.b(1024)
    Pt = al.b(896)
    PTs = al.b(896)
    idb = al.b(128)
    ones_b = al.b(2)
    xt = [al.f(512) for _ in range(2)]
    rows = al.f(2048)
    msk = al.f(896)
    scm = al.f(896)
    cols = al.f(8)
    outst = al.f(512)
    onesr = al.f(128)

    ps_qkv = st.enter_context(nc.psum_tensor("ps_qkv", [128, 512], F32))
    ps_sc = st.enter_context(nc.psum_tensor("ps_sc", [128, 896], F32))
    ps_att = st.enter_context(nc.psum_tensor("ps_att", [128, 128], F32))
    ps_PT = st.enter_context(nc.psum_tensor("ps_PT", [128, 896], BF16))
    ps_st = st.enter_context(nc.psum_tensor("ps_st", [64, 512], F32))
    ps_bA = st.enter_context(nc.psum_tensor("ps_bA", [128, 512], F32))
    ps_bB = st.enter_context(nc.psum_tensor("ps_bB", [128, 512], F32))
    env = dict(ones_b=ones_b, onesr=onesr, sqbuf=sqbuf, rows=rows, bcA=bcA,
               bcB=bcB, tmpb=tmpb, ps_st=ps_st, ps_bA=ps_bA, ps_bB=ps_bB)

    def emit(w):
        w.op("vector", "memset", ones_b[:, 0:1], 1.0)
        w.op("vector", "memset", onesr[0:1, :128], 1.0)
        w.fence("vector", ("tensor", "scalar"))
        w.dma("sync", idb, identb, "LD")

        def ln_tile(src_chunks, dst_chunks, T):
            ln_tile_g(w, env, src_chunks, dst_chunks, T)

        def rope_from_psum(ps, dst, T):
            a = scr[0:64, 0:T]
            b = scr[0:64, 512:512 + T]
            a2 = scr[64:128, 0:T]
            b2 = scr[64:128, 512:512 + T]
            w.op("vector", "tensor_tensor", a, ps[0:64, :T], cs_t[0:64, :T],
                 op=ALU.mult)
            w.op("vector", "tensor_tensor", b, ps[64:128, :T],
                 cs_t[64:128, :T], op=ALU.mult)
            w.op("vector", "tensor_tensor", a2, ps[0:64, :T],
                 cs_t2[0:64, :T], op=ALU.mult)
            w.op("vector", "tensor_tensor", b2, ps[64:128, :T],
                 cs_t2[64:128, :T], op=ALU.mult)
            w.fence("vector", "vector")
            w.op("vector", "tensor_tensor", dst[0:64, :T], a, b,
                 op=ALU.subtract)
            w.op("vector", "tensor_tensor", dst[64:128, :T], a2, b2,
                 op=ALU.add)

        def stream_wcol(dram_w, j, par):
            w.fence("tensor", "gpsimd")
            w.dma("gpsimd", wbuf[par].rearrange("p (c j) -> p c j", c=16),
                  dram_w[:, j * 128:(j + 1) * 128].rearrange(
                      "(c p) j -> p c j", p=128), "WT")
            w.wait("tensor", "WT")

        for kvt in range(6):
            T = 512 if kvt < 5 else FRONT
            if kvt < 5:
                src_ap = xT_band
                col0 = kvt * 512
                csrc = (cs_band, cs2_band)
            else:
                src_ap = xT_front
                col0 = 0
                csrc = (cs_front, cs2_front)
            if 1 <= kvt <= 4:
                dst = [hq[c][:, (kvt - 1) * 512:(kvt - 1) * 512 + T]
                       for c in range(16)]
            else:
                dst = [ht[c][:, :T] for c in range(16)]
            w.fence(("tensor", "vector", "scalar"), "gpsimd")
            for c in range(16):
                w.dma("gpsimd", dst[c],
                      src_ap[c * 128:(c + 1) * 128, col0:col0 + T], "WT")
            w.wait("tensor", "WT")
            w.wait("scalar", "WT")
            w.wait("vector", "WT")
            ln_tile(dst, dst, T)
            w.fence("vector", "gpsimd")
            w.dma("gpsimd", cs_t[:, :T], csrc[0][:, col0:col0 + T], "WT")
            w.dma("gpsimd", cs_t2[:, :T], csrc[1][:, col0:col0 + T], "WT")
            w.wait("vector", "WT")
            koff = kvt * 512 if kvt < 5 else BAND
            for j in range(4):
                stream_wcol(wk, j, j % 2)
                for kc in range(16):
                    w.op("tensor", "matmul", ps_qkv[:, :T],
                         wbuf[j % 2][:, kc * 128:(kc + 1) * 128], dst[kc],
                         start=(kc == 0), stop=(kc == 15))
                w.fence("tensor", "vector")
                rope_from_psum(ps_qkv, K[j][:, koff:koff + T], T)
                w.fence("vector", "tensor")
            for j in range(4):
                stream_wcol(wv, j, j % 2)
                for kc in range(16):
                    w.op("tensor", "matmul", ps_qkv[:, :T],
                         wbuf[j % 2][:, kc * 128:(kc + 1) * 128], dst[kc],
                         start=(kc == 0), stop=(kc == 15))
                w.fence("tensor", "scalar")
                w.op("scalar", "copy", tmpb[:, :T], ps_qkv[:, :T])
                w.fence("scalar", "tensor")
                w.wait("tensor", "LD", 16)
                for t in range(T // 128):
                    w.op("tensor", "transpose",
                         ps_PT[:, t * 128:(t + 1) * 128],
                         tmpb[:, t * 128:(t + 1) * 128], idb)
                w.fence("tensor", "vector")
                w.op("vector", "tensor_copy",
                     VT[j][:, koff:koff + T], ps_PT[:, :T])
                w.fence("vector", "tensor")

        if dbg:
            w.fence(("vector", "tensor", "scalar"), "gpsimd")
            for c in range(16):
                w.dma("gpsimd", h_dbg[c * 128:(c + 1) * 128, :], hq[c], "GD")
            for j in range(4):
                w.dma("gpsimd", k_dbg[j * 128:(j + 1) * 128, :], K[j], "GD")
                w.dma("gpsimd", vt_dbg[j * 128:(j + 1) * 128, :], VT[j], "GD")

        for qt in range(4):
            w.fence("vector", "gpsimd")
            w.dma("gpsimd", cs_t[:, :512], cs_q[:, qt * 512:(qt + 1) * 512],
                  "WT")
            w.dma("gpsimd", cs_t2[:, :512],
                  cs2_q[:, qt * 512:(qt + 1) * 512], "WT")
            for j in range(16):
                stream_wcol(wq, j, j % 2)
                for kc in range(16):
                    w.op("tensor", "matmul", ps_qkv[:, :512],
                         wbuf[j % 2][:, kc * 128:(kc + 1) * 128],
                         hq[kc][:, qt * 512:(qt + 1) * 512],
                         start=(kc == 0), stop=(kc == 15))
                w.fence("tensor", "vector")
                w.wait("vector", "WT")
                rope_from_psum(ps_qkv, q2[j], 512)
                w.fence("vector", "tensor")
            if dbg and qt == 0:
                w.fence("vector", "gpsimd")
                for j in range(16):
                    w.dma("gpsimd", q_dbg[j * 128:(j + 1) * 128, :], q2[j],
                          "GD")
            for h in range(16):
                g = h // 4
                for qb in range(4):
                    Qg = qt * 4 + qb
                    w.fence("vector", "sync")
                    w.dma("sync", msk, masks[Qg], "LD")
                    b0 = Qg * 128
                    w.op("tensor", "matmul", ps_sc[:, 0:512],
                         q2[h][:, qb * 128:(qb + 1) * 128],
                         K[g][:, b0:b0 + 512], start=True, stop=True)
                    w.op("tensor", "matmul", ps_sc[:, 512:640],
                         q2[h][:, qb * 128:(qb + 1) * 128],
                         K[g][:, b0 + 512:b0 + 640], start=True, stop=True)
                    w.op("tensor", "matmul", ps_sc[:, 640:896],
                         q2[h][:, qb * 128:(qb + 1) * 128],
                         K[g][:, BAND:BAND + 256], start=True, stop=True)
                    w.fence("tensor", "vector")
                    w.wait("vector", "LD")
                    w.op("vector", "scalar_tensor_tensor", scm, ps_sc[:, :],
                         0.08838834764831845, msk, op0=ALU.mult, op1=ALU.add)
                    w.fence("vector", "vector")
                    w.op("vector", "reduce_max", cols[:, 0:1], scm, axis=AX.X)
                    w.fence("vector", "vector")
                    w.op("vector", "tensor_scalar", cols[:, 1:2],
                         cols[:, 0:1], -1.0, None, op0=ALU.mult)
                    w.fence("vector", ("scalar", "tensor"))
                    w.op("scalar", "activation", Pt, scm, AF.Exp,
                         bias=cols[:, 1:2], accum_out=cols[:, 2:3])
                    w.fence("scalar", "vector")
                    w.op("vector", "reciprocal", cols[:, 3:4], cols[:, 2:3])
                    w.fence("vector", "vector")
                    w.op("vector", "tensor_scalar", Pt, Pt, cols[:, 3:4],
                         None, op0=ALU.mult)
                    w.fence("vector", "tensor")
                    for s in range(7):
                        w.op("tensor", "transpose",
                             ps_PT[:, s * 128:(s + 1) * 128],
                             Pt[:, s * 128:(s + 1) * 128], idb)
                    w.fence("tensor", "vector")
                    w.op("vector", "tensor_copy", PTs, ps_PT[:, :896])
                    w.fence("vector", "tensor")
                    for s in range(7):
                        gk = Qg + s if s < 5 else 20 + (s - 5)
                        w.op("tensor", "matmul", ps_att[:, :],
                             VT[g][:, gk * 128:(gk + 1) * 128],
                             PTs[:, s * 128:(s + 1) * 128],
                             start=(s == 0), stop=(s == 6))
                    w.fence("tensor", "scalar")
                    w.op("scalar", "copy", ao2[h][:, qb * 128:(qb + 1) * 128],
                         ps_att[:, :])
                    w.fence("scalar", "tensor")
            if dbg and qt == 0:
                w.fence("scalar", "gpsimd")
                for j in range(16):
                    w.dma("gpsimd", ao_dbg[j * 128:(j + 1) * 128, :], ao2[j],
                          "GD")
            for j in range(16):
                stream_wcol(wo, j, j % 2)
                for kc in range(16):
                    w.op("tensor", "matmul", ps_qkv[:, :512],
                         wbuf[j % 2][:, kc * 128:(kc + 1) * 128], ao2[kc],
                         start=(kc == 0), stop=(kc == 15))
                w.fence("tensor", "scalar")
                w.op("scalar", "copy", q2[j], ps_qkv[:, :512])
                w.fence("scalar", ("tensor", "vector"))
            if dbg and qt == 0:
                w.fence("scalar", "gpsimd")
                for j in range(16):
                    w.dma("gpsimd", ap_dbg[j * 128:(j + 1) * 128, :], q2[j],
                          "GD")
            ln_tile(q2, q2, 512)
            for c in range(16):
                par = c % 2
                if c >= 2:
                    w.fence("vector", "sync")
                w.dma("sync", xt[par][:, :512],
                      xT_band[c * 128:(c + 1) * 128,
                              512 + qt * 512:512 + (qt + 1) * 512], "LD")
                w.wait("vector", "LD")
                w.wait("vector", "ST")
                w.op("vector", "tensor_tensor", outst, xt[par][:, :512],
                     q2[c], op=ALU.add)
                w.fence("vector", "sync")
                w.dma("sync", x1T[c * 128:(c + 1) * 128,
                                  qt * 512:(qt + 1) * 512], outst, "ST")
        w.wait("sync", "ST")
        w.wait("sync", "LD")
        w.wait("sync", "WT")
        w.wait("sync", "GD")

    run_walks(nc, st, emit)
    st.close()
    return nc


def rope_perm():
    return np.r_[np.arange(0, 128, 2), np.arange(1, 128, 2)]


def cs_table(S=8192):
    inv = 1.0 / 10000.0 ** (np.arange(0, 128, 2).astype(np.float32) / 128)
    t = np.arange(S, dtype=np.float32)
    ang = np.outer(inv, t)
    return np.concatenate([np.cos(ang), np.sin(ang)], axis=0).astype(np.float32)


def build_masks(tok0):
    M = np.empty((16, 128, 896), dtype=np.float32)
    ii = np.arange(128)
    for Qg in range(16):
        iabs = tok0 + Qg * 128 + ii[:, None]
        for s in range(7):
            if s < 5:
                jabs = tok0 - 512 + (Qg + s) * 128 + ii[None, :]
            else:
                jabs = (s - 5) * 128 + ii[None, :]
            allowed = ((jabs <= iabs) & (jabs > iabs - 512)) | \
                      ((jabs < 256) & (jabs <= iabs - 512))
            allowed &= (jabs >= 0)
            if s >= 5:
                fs0 = (s - 5) * 128
                bandlo = tok0 + (Qg - 4) * 128
                bandhi = tok0 + (Qg + 1) * 128
                if bandlo <= fs0 and bandhi >= fs0 + 128:
                    allowed = np.zeros_like(allowed)
            M[Qg, :, s * 128:(s + 1) * 128] = np.where(allowed, 0.0, -1e30)
    return M


def l0a_inputs(x_full, wq, wk, wv, wo):
    perm = rope_perm()
    permQ = np.concatenate([h * 128 + perm for h in range(16)])
    permK = np.concatenate([h * 128 + perm for h in range(4)])
    wqP = np.ascontiguousarray(np.asarray(wq)[:, permQ])
    wkP = np.ascontiguousarray(np.asarray(wk)[:, permK])
    wv = np.ascontiguousarray(np.asarray(wv))
    wo = np.ascontiguousarray(np.asarray(wo))
    cs = cs_table()
    ident = np.eye(128).astype(BF)
    ins = []
    for c in range(NCORES):
        b, q = c // 4, c % 4
        tok0 = q * NTOK
        xb = np.zeros((D, BAND), dtype=np.float32)
        lo = max(0, tok0 - 512)
        xb[:, 512 - (tok0 - lo):] = x_full[b, lo:tok0 + NTOK].T
        xf = np.ascontiguousarray(x_full[b, 0:FRONT].T)
        csb = np.zeros((128, BAND), dtype=np.float32)
        csb[:, 512 - (tok0 - lo):] = cs[:, lo:tok0 + NTOK]
        csb[0:64, :512 - (tok0 - lo)] = 1.0
        sw = lambda a: np.ascontiguousarray(np.concatenate(
            [a[64:128], a[0:64]], axis=0))
        csb_c = np.ascontiguousarray(csb)
        csf = np.ascontiguousarray(cs[:, 0:FRONT])
        csq = np.ascontiguousarray(cs[:, tok0:tok0 + NTOK])
        ins.append({
            "xT_band": np.ascontiguousarray(xb),
            "xT_front": xf,
            "wq": wqP, "wk": wkP, "wv": wv, "wo": wo,
            "cs_band": csb_c, "cs2_band": sw(csb_c),
            "cs_front": csf, "cs2_front": sw(csf),
            "cs_q": csq, "cs2_q": sw(csq),
            "masks": build_masks(tok0),
            "identb": ident,
        })
    return ins


def build_l0f(dbg=False):
    nc = bass.Bass("TRN2", target_bir_lowering=False)
    TT = 512

    xT = nc.dram_tensor("xT", [D, NTOK], F32, kind="ExternalInput").ap()
    W1 = nc.dram_tensor("W1", [D, D], F32, kind="ExternalInput").ap()
    b1r = nc.dram_tensor("b1r", [128, 16], F32, kind="ExternalInput").ap()
    W2 = nc.dram_tensor("W2", [D, 8], F32, kind="ExternalInput").ap()
    b2c = nc.dram_tensor("b2c", [8, 1], F32, kind="ExternalInput").ap()
    VP = nc.dram_tensor("VP", [4, D, 128], F32, kind="ExternalInput").ap()
    UP = nc.dram_tensor("UP", [4, 128, D], F32, kind="ExternalInput").ap()
    SEL = nc.dram_tensor("SEL", [8, 512], F32, kind="ExternalInput").ap()
    identf = nc.dram_tensor("identf", [128, 128], F32, kind="ExternalInput").ap()
    x2T = nc.dram_tensor("x2T", [D, NTOK], F32, kind="ExternalOutput").ap()
    if dbg:
        hid_dbg = nc.dram_tensor("hid_dbg", [D, TT], F32, kind="ExternalOutput").ap()
        al_dbg = nc.dram_tensor("al_dbg", [8, TT], F32, kind="ExternalOutput").ap()
        vx_dbg = nc.dram_tensor("vx_dbg", [512, TT], F32, kind="ExternalOutput").ap()
        o_dbg = nc.dram_tensor("o_dbg", [D, TT], F32, kind="ExternalOutput").ap()

    st = contextlib.ExitStack()
    al = Alloc(nc, st, nb_cols_bf16=62000, nf_cols_f32=6000)
    src = [al.b(TT) for _ in range(16)]
    hid = [al.b(TT) for _ in range(16)]
    o2 = [al.b(TT) for _ in range(16)]
    vx = [al.b(TT) for _ in range(4)]
    wsc = [al.b(TT) for _ in range(4)]
    wbuf = [al.b(D) for _ in range(2)]
    vpw = [al.b(D) for _ in range(4)]
    upw = [al.b(D) for _ in range(4)]
    w2w = al.b(8 * 16)
    sqbuf = al.b(1024)
    bcA = al.b(512)
    bcB = al.b(512)
    tmpb = al.b(512)
    idb = al.b(128)
    ones_b = al.b(2)
    xt = [al.f(512) for _ in range(2)]
    rows = al.f(2048)
    onesr = al.f(128)
    idf = al.f(128)
    sa = al.f(512)
    at = al.f(32)
    am = al.f(4)
    asum = al.f(4)
    outst = al.f(512)
    alf = al.f(512)
    b1s = al.f(16)
    b2s = al.f(1)

    ps_mm = st.enter_context(nc.psum_tensor("ps_mm", [128, 512], F32))
    ps_a = st.enter_context(nc.psum_tensor("ps_a", [8, 512], F32))
    ps_at = st.enter_context(nc.psum_tensor("ps_at", [128, 32], F32))
    ps_af = st.enter_context(nc.psum_tensor("ps_af", [8, 512], F32))
    ps_bc = st.enter_context(nc.psum_tensor("ps_bc", [128, 512], F32))
    ps_st = st.enter_context(nc.psum_tensor("ps_st", [64, 512], F32))
    ps_bA = st.enter_context(nc.psum_tensor("ps_bA", [128, 512], F32))
    ps_bB = st.enter_context(nc.psum_tensor("ps_bB", [128, 512], F32))
    env = dict(ones_b=ones_b, onesr=onesr, sqbuf=sqbuf, rows=rows, bcA=bcA,
               bcB=bcB, tmpb=tmpb, ps_st=ps_st, ps_bA=ps_bA, ps_bB=ps_bB)

    def emit(w):
        w.op("vector", "memset", ones_b[:, 0:1], 1.0)
        w.op("vector", "memset", onesr[0:1, :128], 1.0)
        w.fence("vector", ("tensor", "scalar"))
        for m in range(4):
            w.dma("gpsimd", vpw[m].rearrange("p (c j) -> p c j", c=16),
                  VP[m].rearrange("(c p) j -> p c j", p=128), "WT")
            w.dma("gpsimd", upw[m], UP[m], "WT")
        w.dma("gpsimd", w2w.rearrange("p (c j) -> p c j", c=16),
              W2.rearrange("(c p) j -> p c j", p=128), "WT")
        w.dma("sync", b1s[:, 0:16], b1r, "LD")
        w.dma("sync", b2s[0:8, 0:1], b2c, "LD")
        w.dma("sync", idf, identf, "LD")
        w.dma("sync", selw[0:8, :], SEL, "LD")
        w.wait("tensor", "WT")
        w.wait("tensor", "LD")
        w.wait("scalar", "LD")

        def ln_tile(srcs, dsts, T):
            ln_tile_g(w, env, srcs, dsts, T)

        for tt in range(4):
            c0 = tt * TT
            w.fence(("tensor", "vector", "scalar"), "gpsimd")
            for c in range(16):
                w.dma("gpsimd", src[c], xT[c * 128:(c + 1) * 128,
                                           c0:c0 + TT], "WT")
            w.wait("tensor", "WT")
            w.wait("scalar", "WT")
            w.wait("vector", "WT")
            ln_tile(src, src, TT)
            for j in range(16):
                w.fence("tensor", "gpsimd")
                w.dma("gpsimd", wbuf[j % 2].rearrange("p (c j) -> p c j", c=16),
                      W1[:, j * 128:(j + 1) * 128].rearrange(
                          "(c p) j -> p c j", p=128), "WT")
                w.wait("tensor", "WT")
                for kc in range(16):
                    w.op("tensor", "matmul", ps_mm[:, :],
                         wbuf[j % 2][:, kc * 128:(kc + 1) * 128], src[kc],
                         start=(kc == 0), stop=(kc == 15))
                w.fence("tensor", "scalar")
                w.op("scalar", "activation", hid[j], ps_mm[:, :], AF.Relu,
                     bias=b1s[:, j:j + 1])
                w.fence("scalar", "tensor")
            for kc in range(16):
                w.op("tensor", "matmul", ps_a[:, :],
                     w2w[:, kc * 8:(kc + 1) * 8], hid[kc],
                     start=(kc == 0), stop=(kc == 15))
            w.fence("tensor", "scalar")
            w.op("scalar", "activation", sa[0:8, :], ps_a[:, :], AF.Identity,
                 bias=b2s[0:8, 0:1])
            w.fence("scalar", "tensor")
            for b in range(4):
                w.op("tensor", "transpose", ps_at[:, b * 8:(b + 1) * 8],
                     sa[0:8, b * 128:(b + 1) * 128], idf[0:8, 0:8])
            w.fence("tensor", "vector")
            atv = at[:, 0:32].rearrange("p (b e) -> p b e", e=8)
            w.op("vector", "tensor_copy", at[:, 0:32], ps_at[:, :])
            w.fence("vector", "vector")
            w.op("vector", "reduce_max", am[:, 0:4],
                 atv, axis=AX.X)
            w.fence("vector", "vector")
            amb = am[:, 0:4, None].broadcast_to([128, 4, 8])
            w.op("vector", "tensor_tensor", atv, atv, amb, op=ALU.subtract)
            w.fence("vector", "scalar")
            w.op("scalar", "activation", at[:, 0:32], at[:, 0:32], AF.Exp)
            w.fence("scalar", "vector")
            w.op("vector", "tensor_reduce", asum[:, 0:4], atv, axis=AX.X,
                 op=ALU.add)
            w.fence("vector", "vector")
            w.op("vector", "reciprocal", asum[:, 0:4], asum[:, 0:4])
            w.fence("vector", "vector")
            asb = asum[:, 0:4, None].broadcast_to([128, 4, 8])
            w.op("vector", "tensor_tensor", atv, atv, asb, op=ALU.mult)
            w.fence("vector", "tensor")
            for b in range(4):
                w.op("tensor", "transpose", ps_af[:, b * 128:(b + 1) * 128],
                     at[:, b * 8:(b + 1) * 8], idf)
            w.fence("tensor", "vector")
            w.op("vector", "tensor_copy", alf[0:8, :], ps_af[:, :])
            w.fence("vector", "tensor")
            if dbg and tt == 0:
                w.fence("vector", "gpsimd")
                w.dma("gpsimd", al_dbg, alf[0:8, :], "GD")
                for j in range(16):
                    w.dma("gpsimd", hid_dbg[j * 128:(j + 1) * 128, :],
                          hid[j], "GD")
            for m in range(4):
                for kc in range(16):
                    w.op("tensor", "matmul", ps_mm[:, :],
                         vpw[m][:, kc * 128:(kc + 1) * 128], src[kc],
                         start=(kc == 0), stop=(kc == 15))
                w.fence("tensor", "scalar")
                w.op("scalar", "copy", vx[m], ps_mm[:, :])
                w.fence("scalar", "tensor")
            for m in range(4):
                w.op("tensor", "matmul", ps_bc[:, :],
                     selw[0:8, m * 128:(m + 1) * 128],
                     alf[0:8, :], start=True, stop=True)
                w.fence("tensor", "vector")
                w.op("vector", "tensor_tensor", wsc[m], vx[m], ps_bc[:, :],
                     op=ALU.mult)
                w.fence("vector", "tensor")
            if dbg and tt == 0:
                w.fence(("vector", "scalar"), "gpsimd")
                for m in range(4):
                    w.dma("gpsimd", vx_dbg[m * 128:(m + 1) * 128, :], vx[m],
                          "GD")
            for j in range(16):
                for m in range(4):
                    w.op("tensor", "matmul", ps_mm[:, :],
                         upw[m][:, j * 128:(j + 1) * 128], wsc[m],
                         start=(m == 0), stop=(m == 3))
                w.fence("tensor", "scalar")
                w.op("scalar", "copy", o2[j], ps_mm[:, :])
                w.fence("scalar", ("tensor", "vector"))
            if dbg and tt == 0:
                w.fence("scalar", "gpsimd")
                for j in range(16):
                    w.dma("gpsimd", o_dbg[j * 128:(j + 1) * 128, :], o2[j],
                          "GD")
            ln_tile(o2, o2, TT)
            for c in range(16):
                par = c % 2
                if c >= 2:
                    w.fence("vector", "sync")
                w.dma("sync", xt[par][:, :], xT[c * 128:(c + 1) * 128,
                                               c0:c0 + TT], "LD")
                w.wait("vector", "LD")
                w.wait("vector", "ST")
                w.op("vector", "tensor_tensor", outst, xt[par][:, :], o2[c],
                     op=ALU.add)
                w.fence("vector", "sync")
                w.dma("sync", x2T[c * 128:(c + 1) * 128, c0:c0 + TT],
                      outst, "ST")
        w.wait("sync", "ST")
        w.wait("sync", "LD")
        w.wait("sync", "WT")
        w.wait("sync", "GD")

    selw = al.f(512)
    run_walks(nc, st, emit)
    st.close()
    return nc


def l0f_inputs(x2_like_core_slices, U, V, W1, b1, W2, b2):
    U_T = np.ascontiguousarray(np.asarray(U).transpose(0, 2, 1))
    V_T = np.ascontiguousarray(np.asarray(V).transpose(0, 2, 1))
    VP = np.stack([np.concatenate([V_T[2 * m], V_T[2 * m + 1]], axis=1)
                   for m in range(4)])
    UP = np.stack([np.concatenate([U_T[2 * m], U_T[2 * m + 1]], axis=0)
                   for m in range(4)])
    b1r = np.ascontiguousarray(np.asarray(b1).reshape(16, 128).T)
    b2c = np.asarray(b2).reshape(8, 1).astype(np.float32)
    SEL = np.zeros((8, 512), dtype=np.float32)
    for m in range(4):
        SEL[2 * m, m * 128:m * 128 + 64] = 1.0
        SEL[2 * m + 1, m * 128 + 64:(m + 1) * 128] = 1.0
    identf = np.eye(128, dtype=np.float32)
    ins = []
    for c in range(NCORES):
        ins.append({
            "xT": x2_like_core_slices[c],
            "W1": np.ascontiguousarray(np.asarray(W1)),
            "b1r": b1r, "W2": np.ascontiguousarray(np.asarray(W2)),
            "b2c": b2c,
            "VP": VP.astype(np.float32), "UP": UP.astype(np.float32),
            "SEL": SEL, "identf": identf,
        })
    return ins


def build_l1a_a(dbg=False):
    nc = bass.Bass("TRN2", target_bir_lowering=False)
    TT = 512

    xT = nc.dram_tensor("xT", [D, NTOK], F32, kind="ExternalInput").ap()
    xh = nc.dram_tensor("xh", [D, 4], F32, kind="ExternalInput").ap()
    wpT = nc.dram_tensor("wpT", [D, 4416], F32, kind="ExternalInput").ap()
    cwr = nc.dram_tensor("cwr", [128, 136], F32, kind="ExternalInput").ap()
    cbr = nc.dram_tensor("cbr", [128, 34], F32, kind="ExternalInput").ap()
    dtb = nc.dram_tensor("dtb", [64, 1], F32, kind="ExternalInput").ap()
    xBCT = nc.dram_tensor("xBCT", [4352, NTOK], BF16, kind="ExternalOutput").ap()
    dt2T = nc.dram_tensor("dt2T", [64, NTOK], F32, kind="ExternalOutput").ap()

    st = contextlib.ExitStack()
    al = Alloc(nc, st, nb_cols_bf16=30000, nf_cols_f32=8000)
    src = [al.b(TT) for _ in range(16)]
    hh = [al.b(4) for _ in range(16)]
    wbuf = [al.b(D) for _ in range(2)]
    projx = al.b(515 + 1)
    xbc = al.b(TT)
    sqbuf = al.b(1024)
    bcA = al.b(512)
    bcB = al.b(512)
    tmpb = al.b(512)
    ones_b = al.b(2)
    rows = al.f(2048)
    onesr = al.f(128)
    cw = al.f(136)
    cb = al.f(34)
    dtbc = al.f(1)
    tails = al.f(110)
    acc = al.f(512)
    dta = al.f(512)
    dtx = al.f(512 * 3)
    ps_mm = st.enter_context(nc.psum_tensor("ps_mm", [128, 512], F32))
    ps_h = st.enter_context(nc.psum_tensor("ps_h", [128, 8], F32))
    ps_st = st.enter_context(nc.psum_tensor("ps_st", [64, 512], F32))
    ps_bA = st.enter_context(nc.psum_tensor("ps_bA", [128, 512], F32))
    ps_bB = st.enter_context(nc.psum_tensor("ps_bB", [128, 512], F32))
    env = dict(ones_b=ones_b, onesr=onesr, sqbuf=sqbuf, rows=rows, bcA=bcA,
               bcB=bcB, tmpb=tmpb, ps_st=ps_st, ps_bA=ps_bA, ps_bB=ps_bB)

    def emit(w):
        w.op("vector", "memset", ones_b[:, 0:1], 1.0)
        w.op("vector", "memset", onesr[0:1, :128], 1.0)
        w.fence("vector", ("tensor", "scalar"))
        w.dma("sync", cw, cwr, "LD")
        w.dma("sync", cb, cbr, "LD")
        w.dma("sync", dtbc[0:64, 0:1], dtb, "LD")
        w.fence(("tensor", "vector", "scalar"), "gpsimd")
        for c in range(16):
            w.dma("gpsimd", hh[c], xh[c * 128:(c + 1) * 128, :], "WT")
        w.wait("tensor", "WT")
        w.wait("scalar", "WT")
        w.wait("vector", "WT")
        ln_tile_g(w, env, [t[:, 0:4] for t in hh], [t[:, 0:4] for t in hh], 4)
        w.wait("scalar", "LD")
        w.wait("vector", "LD")

        for tt in range(4):
            c0 = tt * TT
            w.fence(("tensor", "vector", "scalar"), "gpsimd")
            for c in range(16):
                w.dma("gpsimd", src[c], xT[c * 128:(c + 1) * 128,
                                           c0:c0 + TT], "WT")
            w.wait("tensor", "WT")
            w.wait("scalar", "WT")
            w.wait("vector", "WT")
            ln_tile_g(w, env, src, src, TT)
            for j in range(35):
                wd = 128 if j < 34 else 64
                jc0 = j * 128
                w.fence("tensor", "gpsimd")
                w.dma("gpsimd",
                      wbuf[j % 2][:, :16 * wd].rearrange(
                          "p (c j) -> p c j", c=16),
                      wpT[:, jc0:jc0 + wd].rearrange(
                          "(c p) j -> p c j", p=128), "WT")
                w.wait("tensor", "WT")
                for kc in range(16):
                    w.op("tensor", "matmul", ps_mm[:wd if wd < 128 else 128, :TT],
                         wbuf[j % 2][:, kc * wd:(kc + 1) * wd], src[kc],
                         start=(kc == 0), stop=(kc == 15))
                if j < 34:
                    if tt == 0:
                        for kc in range(16):
                            w.op("tensor", "matmul", ps_h[:, 0:4],
                                 wbuf[j % 2][:, kc * 128:(kc + 1) * 128],
                                 hh[kc][:, 0:4],
                                 start=(kc == 0), stop=(kc == 15))
                    w.fence("tensor", "scalar")
                    w.op("scalar", "copy", projx[:, 3:515], ps_mm[:, :TT])
                    if tt == 0:
                        w.op("scalar", "copy", projx[:, 0:3], ps_h[:, 1:4])
                    w.fence("scalar", ("tensor", "vector"))
                    if tt > 0:
                        w.op("vector", "tensor_copy", projx[:, 0:3],
                             tails[:, j * 3:j * 3 + 3])
                        w.fence("vector", "vector")
                    w.op("vector", "tensor_scalar", acc, projx[:, 0:512],
                         cw[:, j * 4:j * 4 + 1], cb[:, j:j + 1],
                         op0=ALU.mult, op1=ALU.add)
                    for i in range(1, 4):
                        w.fence("vector", "vector")
                        w.op("vector", "scalar_tensor_tensor", acc,
                             projx[:, i:i + 512], cw[:, j * 4 + i:j * 4 + i + 1],
                             acc, op0=ALU.mult, op1=ALU.add)
                    w.fence("vector", "vector")
                    w.op("vector", "tensor_copy", tails[:, j * 3:j * 3 + 3],
                         projx[:, 512:515])
                    w.fence("vector", "scalar")
                    w.op("scalar", "activation", xbc, acc, AF.Silu)
                    w.fence("scalar", ("gpsimd", "vector"))
                    w.dma("gpsimd", xBCT[jc0:jc0 + 128, c0:c0 + TT], xbc,
                          "GD")
                else:
                    w.fence("tensor", "vector")
                    d0 = dtx[0:64, 0:512]
                    d1 = dtx[0:64, 512:1024]
                    d2 = dtx[0:64, 1024:1536]
                    w.op("vector", "tensor_scalar", d0, ps_mm[0:64, :TT],
                         dtbc[0:64, 0:1], None, op0=ALU.add)
                    w.fence("vector", "scalar")
                    w.op("scalar", "activation", d1, d0, AF.Abs)
                    w.fence("scalar", "scalar")
                    w.op("scalar", "activation", d1, d1, AF.Exp, scale=-1.0)
                    w.fence("scalar", "vector")
                    w.op("vector", "tensor_scalar", d1, d1, 1.0, None,
                         op0=ALU.add)
                    w.fence("vector", "scalar")
                    w.op("scalar", "activation", d2, d1, AF.Ln)
                    w.fence("scalar", "vector")
                    w.op("vector", "tensor_scalar", d0, d0, 0.0, None,
                         op0=ALU.max)
                    w.fence("vector", "vector")
                    w.op("vector", "tensor_tensor", dta[0:64, :], d0, d2,
                         op=ALU.add)
                    w.fence("vector", "sync")
                    w.dma("sync", dt2T[:, c0:c0 + TT], dta[0:64, :], "ST")
                w.fence(("vector", "scalar"), "tensor")
        w.wait("sync", "ST")
        w.wait("sync", "GD")
        w.wait("sync", "WT")
        w.wait("sync", "LD")

    run_walks(nc, st, emit)
    st.close()
    return nc


def build_l1a_b(dbg=False):
    nc = bass.Bass("TRN2", target_bir_lowering=False)
    CH = 256
    NCH = 32

    xsT = nc.dram_tensor("xsT", [1024, 8192], BF16, kind="ExternalInput").ap()
    BT_d = nc.dram_tensor("BT_d", [128, 8192], BF16, kind="ExternalInput").ap()
    CT_d = nc.dram_tensor("CT_d", [128, 8192], BF16, kind="ExternalInput").ap()
    dt2d = nc.dram_tensor("dt2d", [16, 8192], F32, kind="ExternalInput").ap()
    alog = nc.dram_tensor("alog", [16, 1], F32, kind="ExternalInput").ap()
    dbc_in = nc.dram_tensor("dbc_in", [128, 16], F32, kind="ExternalInput").ap()
    ehsel = nc.dram_tensor("ehsel", [16, 2048], F32, kind="ExternalInput").ap()
    trim = nc.dram_tensor("trim", [128, 128], BF16, kind="ExternalInput").ap()
    identf = nc.dram_tensor("identf", [128, 128], F32, kind="ExternalInput").ap()
    identb = nc.dram_tensor("identb", [128, 128], BF16, kind="ExternalInput").ap()
    yTok = nc.dram_tensor("yTok", [8192, 1024], F32, kind="ExternalOutput").ap()

    st = contextlib.ExitStack()
    al = Alloc(nc, st, nb_cols_bf16=20000, nf_cols_f32=12000)
    xs = [al.b(CH) for _ in range(8)]
    Bm = al.b(CH)
    Cm = al.b(CH)
    xT = [al.b(1024) for _ in range(2)]
    BTt = [al.b(128) for _ in range(2)]
    CBTm = [al.b(CH) for _ in range(2)]
    Pg = [al.b(4096) for _ in range(2)]
    xdt = [al.b(1024) for _ in range(2)]
    xdec = [al.b(1024) for _ in range(2)]
    tseg2b = al.b(1024)
    S_bf = al.b(1024)
    trimask = al.b(128)
    idb = al.b(128)
    ones_b = al.b(2)
    dt2c = al.f(256)
    dAf = al.f(256)
    cumf = al.f(256)
    cumT = [al.f(16) for _ in range(2)]
    dtT = [al.f(16) for _ in range(2)]
    decf = [al.f(16) for _ in range(2)]
    ect = [al.f(16) for _ in range(2)]
    sfac = [al.f(16) for _ in range(2)]
    rexp = al.f(16)
    acol = al.f(1)
    dbc = al.f(16)
    ehs = al.f(2048)
    rrow = al.f(16)
    S_in = al.f(1024)
    yst = [al.f(1024) for _ in range(2)]
    tseg = al.f(1024)
    onesr = al.f(128)
    idf = al.f(128)

    ps_tr = st.enter_context(nc.psum_tensor("ps_tr", [128, 1024], BF16))
    ps_cbt = st.enter_context(nc.psum_tensor("ps_cbt", [128, 512], F32))
    ps_seg = st.enter_context(nc.psum_tensor("ps_seg", [128, 1024], F32))
    ps_yd = st.enter_context(nc.psum_tensor("ps_yd", [128, 1024], F32))
    ps_yo = st.enter_context(nc.psum_tensor("ps_yo", [128, 1024], F32))

    def emit(w):
        w.op("vector", "memset", ones_b[:, 0:1], 1.0)
        w.op("vector", "memset", onesr[0:1, :128], 1.0)
        w.op("vector", "memset", S_in[:, :], 0.0)
        w.fence("vector", ("tensor", "scalar"))
        w.dma("sync", idf, identf, "LD")
        w.dma("sync", acol[0:16, 0:1], alog, "LD")
        w.dma("sync", dbc[:, 0:16], dbc_in, "LD")
        w.dma("sync", ehs[0:16, :], ehsel, "LD")
        w.dma("gpsimd", trimask, trim, "WT")
        w.dma("gpsimd", idb, identb, "WT")
        w.wait("scalar", "LD")
        w.op("scalar", "activation", acol[0:16, 0:1], acol[0:16, 0:1], AF.Exp)
        w.fence("scalar", "vector")
        w.op("vector", "tensor_scalar", acol[0:16, 0:1], acol[0:16, 0:1],
             -1.0, None, op0=ALU.mult)
        w.fence("vector", ("tensor", "scalar"))

        for ch in range(NCH):
            t0 = ch * CH
            w.fence(("tensor", "vector", "scalar"), "gpsimd")
            for c in range(8):
                w.dma("gpsimd", xs[c], xsT[c * 128:(c + 1) * 128,
                                           t0:t0 + CH], "WT")
            w.dma("gpsimd", Bm, BT_d[:, t0:t0 + CH], "WT")
            w.dma("gpsimd", Cm, CT_d[:, t0:t0 + CH], "WT")
            w.fence("vector", "sync")
            w.dma("sync", dt2c[0:16, :], dt2d[:, t0:t0 + CH], "LD")
            w.wait("vector", "LD")
            w.wait("tensor", "WT")
            w.wait("vector", "WT")
            w.op("vector", "tensor_scalar", dAf[0:16, :], dt2c[0:16, :],
                 acol[0:16, 0:1], None, op0=ALU.mult)
            w.fence("vector", "vector")
            w.op("vector", "memset", cumf[0:16, :], 0.0)
            w.fence("vector", "vector")
            w.op("vector", "tensor_tensor_scan", cumf[0:16, :], dAf[0:16, :],
                 cumf[0:16, :], 0.0, op0=ALU.add, op1=ALU.add)
            w.fence("vector", "tensor")
            for sx in range(2):
                w.op("tensor", "transpose", ps_cbt[:, sx * 16:16 + sx * 16],
                     cumf[0:16, sx * 128:(sx + 1) * 128], idf[0:16, 0:16])
                w.op("tensor", "transpose",
                     ps_cbt[:, 32 + sx * 16:48 + sx * 16],
                     dt2c[0:16, sx * 128:(sx + 1) * 128], idf[0:16, 0:16])
            w.op("tensor", "transpose", ps_cbt[0:1, 64:80],
                 cumf[0:16, 255:256], idf[0:16, 0:16])
            w.fence("tensor", "scalar")
            for sx in range(2):
                w.op("scalar", "copy", cumT[sx][:, 0:16],
                     ps_cbt[:, sx * 16:16 + sx * 16])
                w.op("scalar", "copy", dtT[sx][:, 0:16],
                     ps_cbt[:, 32 + sx * 16:48 + sx * 16])
            w.op("scalar", "copy", rrow[0:1, 0:16], ps_cbt[0:1, 64:80])
            w.fence("scalar", "tensor")
            w.op("tensor", "matmul", ps_cbt[:, 80:96], onesr[0:1, :128],
                 rrow[0:1, 0:16], start=True, stop=True)
            w.fence("tensor", ("vector", "scalar"))
            w.op("scalar", "activation", rexp[:, 0:16], ps_cbt[:, 80:96],
                 AF.Exp)
            for sx in range(2):
                w.op("vector", "tensor_tensor", tseg[:, sx * 16:sx * 16 + 16],
                     ps_cbt[:, 80:96], cumT[sx][:, 0:16], op=ALU.subtract)
            w.fence("vector", "scalar")
            for sx in range(2):
                w.op("scalar", "activation", decf[sx][:, 0:16],
                     tseg[:, sx * 16:sx * 16 + 16], AF.Exp)
                w.op("scalar", "activation", ect[sx][:, 0:16],
                     cumT[sx][:, 0:16], AF.Exp)
            w.fence("scalar", "vector")
            for sx in range(2):
                w.op("vector", "tensor_tensor", sfac[sx][:, 0:16],
                     dtT[sx][:, 0:16], decf[sx][:, 0:16], op=ALU.mult)
            w.fence("vector", "vector")
            for sx in range(2):
                for c in range(8):
                    w.op("tensor", "transpose",
                         ps_tr[:, c * 128:(c + 1) * 128],
                         xs[c][:, sx * 128:(sx + 1) * 128], idb)
                w.fence("tensor", "vector")
                w.op("vector", "tensor_copy", xT[sx], ps_tr[:, :])
                w.fence("vector", "tensor")
                w.op("tensor", "transpose", ps_tr[:, 0:128],
                     Bm[:, sx * 128:(sx + 1) * 128], idb)
                w.fence("tensor", "vector")
                w.op("vector", "tensor_copy", BTt[sx], ps_tr[:, 0:128])
                w.fence("vector", "tensor")
            for sx in range(2):
                dv = dtT[sx][:, 0:16, None].broadcast_to([128, 16, 64])
                sv = sfac[sx][:, 0:16, None].broadcast_to([128, 16, 64])
                xtv = xT[sx].rearrange("p (h q) -> p h q", h=16)
                w.op("vector", "tensor_tensor",
                     xdt[sx].rearrange("p (h q) -> p h q", h=16), xtv, dv,
                     op=ALU.mult)
                w.op("vector", "tensor_tensor",
                     xdec[sx].rearrange("p (h q) -> p h q", h=16), xtv, sv,
                     op=ALU.mult)
            w.fence("vector", "tensor")
            for sx in range(2):
                w.op("tensor", "matmul", ps_cbt[:, 128:384],
                     Bm[:, sx * 128:(sx + 1) * 128], Cm[:, :],
                     start=True, stop=True)
                w.fence("tensor", "vector")
                if sx == 0:
                    w.op("vector", "tensor_tensor", CBTm[0][:, 0:128],
                         ps_cbt[:, 128:256], trimask, op=ALU.mult)
                    w.op("vector", "tensor_copy", CBTm[0][:, 128:256],
                         ps_cbt[:, 256:384])
                else:
                    w.op("vector", "memset", CBTm[1][:, 0:128], 0.0)
                    w.op("vector", "tensor_tensor", CBTm[1][:, 128:256],
                         ps_cbt[:, 256:384], trimask, op=ALU.mult)
                w.fence("vector", "tensor")
            for sx in range(2):
                for gq in range(4):
                    for hh2 in range(4):
                        h = gq * 4 + hh2
                        w.op("tensor", "matmul",
                             ps_seg[:, hh2 * 256:(hh2 + 1) * 256],
                             ehs[0:16, h * 128:(h + 1) * 128],
                             cumf[0:16, :], start=True, stop=True)
                    w.fence("tensor", "vector")
                    cv = cumT[sx][:, gq * 4:(gq + 1) * 4, None].broadcast_to(
                        [128, 4, 256])
                    w.op("vector", "tensor_tensor",
                         tseg[:, 0:1024].rearrange("p (h t) -> p h t", h=4),
                         ps_seg[:, 0:1024].rearrange("p (h t) -> p h t", h=4),
                         cv, op=ALU.subtract)
                    w.fence("vector", "vector")
                    w.op("vector", "tensor_scalar", tseg[:, 0:1024],
                         tseg[:, 0:1024], 0.0, None, op0=ALU.min)
                    w.fence("vector", ("scalar", "tensor"))
                    w.op("scalar", "activation", tseg2b[:, 0:1024],
                         tseg[:, 0:1024], AF.Exp)
                    w.fence("scalar", "vector")
                    cbv = CBTm[sx][:, None, :].broadcast_to([128, 4, 256])
                    w.op("vector", "tensor_tensor",
                         Pg[sx][:, gq * 1024:(gq + 1) * 1024].rearrange(
                             "p (h t) -> p h t", h=4),
                         tseg2b[:, 0:1024].rearrange("p (h t) -> p h t", h=4),
                         cbv, op=ALU.mult)
                    w.fence("vector", ("tensor", "scalar"))
            w.op("vector", "tensor_copy", S_bf, S_in[:, :])
            w.fence("vector", "tensor")
            for tq in range(2):
                w.op("tensor", "matmul", ps_yo[:, 0:512],
                     Cm[:, tq * 128:(tq + 1) * 128], S_bf[:, 0:512],
                     start=True, stop=True)
                w.op("tensor", "matmul", ps_yo[:, 512:1024],
                     Cm[:, tq * 128:(tq + 1) * 128], S_bf[:, 512:1024],
                     start=True, stop=True)
                for h in range(16):
                    for sx in range(2):
                        w.op("tensor", "matmul",
                             ps_yd[:, h * 64:(h + 1) * 64],
                             Pg[sx][:, h * 256 + tq * 128:
                                    h * 256 + tq * 128 + 128],
                             xdt[sx][:, h * 64:(h + 1) * 64],
                             start=(sx == 0), stop=(sx == 1))
                w.fence("tensor", "vector")
                ev = ect[tq][:, 0:16, None].broadcast_to([128, 16, 64])
                w.op("vector", "tensor_tensor",
                     yst[tq].rearrange("p (h q) -> p h q", h=16),
                     ps_yo[:, :].rearrange("p (h q) -> p h q", h=16), ev,
                     op=ALU.mult)
                w.fence("vector", "vector")
                w.op("vector", "tensor_tensor", yst[tq], yst[tq],
                     ps_yd[:, :], op=ALU.add)
                w.fence("vector", "vector")
                dvv = dbc[:, 0:16, None].broadcast_to([128, 16, 64])
                w.op("vector", "tensor_tensor",
                     tseg[:, 0:1024].rearrange("p (h q) -> p h q", h=16),
                     xT[tq].rearrange("p (h q) -> p h q", h=16), dvv,
                     op=ALU.mult)
                w.fence("vector", "vector")
                w.op("vector", "tensor_tensor", yst[tq], yst[tq],
                     tseg[:, 0:1024], op=ALU.add)
                w.fence("vector", ("sync", "tensor"))
                w.dma("sync", yTok[t0 + tq * 128:t0 + (tq + 1) * 128, :],
                      yst[tq], "ST")
            for sx in range(2):
                w.op("tensor", "matmul", ps_yd[:, 0:512], BTt[sx],
                     xdec[sx][:, 0:512], start=(sx == 0), stop=(sx == 1))
            for sx in range(2):
                w.op("tensor", "matmul", ps_yd[:, 512:1024], BTt[sx],
                     xdec[sx][:, 512:1024], start=(sx == 0), stop=(sx == 1))
            w.fence("tensor", "vector")
            rv = rexp[:, 0:16, None].broadcast_to([128, 16, 64])
            w.op("vector", "tensor_tensor",
                 S_in[:, :].rearrange("p (h q) -> p h q", h=16),
                 S_in[:, :].rearrange("p (h q) -> p h q", h=16), rv,
                 op=ALU.mult)
            w.fence("vector", "vector")
            w.op("vector", "tensor_tensor", S_in[:, :], S_in[:, :],
                 ps_yd[:, :], op=ALU.add)
            w.fence("vector", "tensor")
        w.wait("sync", "ST")
        w.wait("sync", "LD")
        w.wait("sync", "WT")

    run_walks(nc, st, emit)
    st.close()
    return nc


def l1a_a_inputs(x2_slices, in_proj_w, conv_w, conv_b, dt_bias):
    wpT = np.ascontiguousarray(np.asarray(in_proj_w).T[:, 4096:8512])
    cw = np.asarray(conv_w)
    cwr = np.ascontiguousarray(
        cw.reshape(34, 128, 4).transpose(1, 0, 2).reshape(128, 136))
    cbr = np.ascontiguousarray(np.asarray(conv_b).reshape(34, 128).T)
    dtb = np.asarray(dt_bias).reshape(64, 1).astype(np.float32)
    ins = []
    for c in range(NCORES):
        b, q = c // 4, c % 4
        tok0 = q * NTOK
        xh = np.zeros((D, 4), dtype=np.float32)
        if tok0 > 0:
            xh[:, 1:4] = x2_slices[c - 1][:, NTOK - 3:NTOK]
        ins.append({"xT": x2_slices[c], "xh": xh, "wpT": wpT,
                    "cwr": cwr, "cbr": cbr, "dtb": dtb})
    return ins


def l1a_b_inputs(xBC_full, dt2_full, A_log, Dv):
    ident = np.eye(128)
    ehsel = np.zeros((16, 2048), dtype=np.float32)
    for h in range(16):
        ehsel[h, h * 128:(h + 1) * 128] = 1.0
    tri = np.tril(np.ones((128, 128), dtype=np.float32)).T
    ins = []
    for c in range(NCORES):
        b, q = c // 4, c % 4
        h0 = 16 * q
        ins.append({
            "xsT": np.ascontiguousarray(xBC_full[b][h0 * 64:h0 * 64 + 1024]),
            "BT_d": np.ascontiguousarray(xBC_full[b][4096:4224]),
            "CT_d": np.ascontiguousarray(xBC_full[b][4224:4352]),
            "dt2d": np.ascontiguousarray(dt2_full[b][h0:h0 + 16]),
            "alog": np.asarray(A_log)[h0:h0 + 16].reshape(16, 1).astype(np.float32),
            "dbc_in": np.tile(np.asarray(Dv)[h0:h0 + 16].astype(np.float32),
                              (128, 1)),
            "ehsel": ehsel,
            "trim": tri.astype(BF),
            "identf": ident.astype(np.float32),
            "identb": ident.astype(BF),
        })
    return ins


def build_l1a_c(dbg=False):
    nc = bass.Bass("TRN2", target_bir_lowering=False)
    TT = 512

    xT = nc.dram_tensor("xT", [D, NTOK], F32, kind="ExternalInput").ap()
    yT = nc.dram_tensor("yT", [4096, NTOK], F32, kind="ExternalInput").ap()
    gpT = nc.dram_tensor("gpT", [D, 4096], F32, kind="ExternalInput").ap()
    opT = nc.dram_tensor("opT", [4096, D], F32, kind="ExternalInput").ap()
    nwr = nc.dram_tensor("nwr", [128, 32], F32, kind="ExternalInput").ap()
    x3T = nc.dram_tensor("x3T", [D, NTOK], F32, kind="ExternalOutput").ap()

    st = contextlib.ExitStack()
    al = Alloc(nc, st, nb_cols_bf16=62000, nf_cols_f32=6000)
    src = [al.b(TT) for _ in range(16)]
    hg = [al.b(TT) for _ in range(32)]
    o2 = [al.b(TT) for _ in range(16)]
    wbuf = [al.b(D) for _ in range(2)]
    wbuf2 = [al.b(4096) for _ in range(2)]
    sg = al.b(TT)
    yb = al.b(TT)
    bcR = al.b(TT)
    sqbuf = al.b(1024)
    bcA = al.b(512)
    bcB = al.b(512)
    tmpb = al.b(512)
    ones_b = al.b(2)
    xt = [al.f(512) for _ in range(2)]
    rows = al.f(2048)
    onesr = al.f(128)
    nw = al.f(32)
    outst = al.f(512)

    ps_mm = st.enter_context(nc.psum_tensor("ps_mm", [128, 512], F32))
    ps_st = st.enter_context(nc.psum_tensor("ps_st", [64, 512], F32))
    ps_bA = st.enter_context(nc.psum_tensor("ps_bA", [128, 512], F32))
    ps_bB = st.enter_context(nc.psum_tensor("ps_bB", [128, 512], F32))
    env = dict(ones_b=ones_b, onesr=onesr, sqbuf=sqbuf, rows=rows, bcA=bcA,
               bcB=bcB, tmpb=tmpb, ps_st=ps_st, ps_bA=ps_bA, ps_bB=ps_bB)

    def emit(w):
        w.op("vector", "memset", ones_b[:, 0:1], 1.0)
        w.op("vector", "memset", onesr[0:1, :128], 1.0)
        w.fence("vector", ("tensor", "scalar"))
        w.dma("sync", nw[:, 0:32], nwr, "LD")
        w.wait("vector", "LD")

        for tt in range(4):
            c0 = tt * TT
            w.fence(("tensor", "vector", "scalar"), "gpsimd")
            for c in range(16):
                w.dma("gpsimd", src[c], xT[c * 128:(c + 1) * 128,
                                           c0:c0 + TT], "WT")
            w.wait("tensor", "WT")
            w.wait("scalar", "WT")
            w.wait("vector", "WT")
            ln_tile_g(w, env, src, src, TT)
            for j in range(32):
                w.fence("tensor", "gpsimd")
                w.dma("gpsimd", wbuf[j % 2].rearrange("p (c j) -> p c j", c=16),
                      gpT[:, j * 128:(j + 1) * 128].rearrange(
                          "(c p) j -> p c j", p=128), "WT")
                w.fence("vector", "gpsimd")
                w.dma("gpsimd", yb, yT[j * 128:(j + 1) * 128, c0:c0 + TT],
                      "WT")
                w.wait("tensor", "WT")
                for kc in range(16):
                    w.op("tensor", "matmul", ps_mm[:, :],
                         wbuf[j % 2][:, kc * 128:(kc + 1) * 128], src[kc],
                         start=(kc == 0), stop=(kc == 15))
                w.fence("tensor", "scalar")
                w.op("scalar", "activation", sg, ps_mm[:, :], AF.Silu)
                w.fence("scalar", ("vector", "tensor"))
                w.wait("vector", "WT")
                w.op("vector", "tensor_tensor", hg[j], yb, sg, op=ALU.mult)
                w.fence("vector", ("tensor", "gpsimd"))
            for j in range(32):
                sq = sqbuf[:, (j % 2) * 512:(j % 2) * 512 + TT]
                if j >= 2:
                    w.fence("tensor", "scalar")
                w.op("scalar", "activation", sq, hg[j], AF.Square)
                w.fence("scalar", "tensor")
                w.op("tensor", "matmul", ps_st[0:1, :TT], ones_b[:, 0:1], sq,
                     start=(j == 0), stop=(j == 31))
            w.fence("tensor", "vector")
            m = rows[0:1, 0:TT]
            w.op("vector", "tensor_scalar", m, ps_st[0:1, :TT], 1.0 / 4096,
                 None, op0=ALU.mult)
            w.fence("vector", "vector")
            w.op("vector", "tensor_scalar", m, m, 1e-5, None, op0=ALU.add)
            w.fence("vector", "scalar")
            w.op("scalar", "activation", m, m, AF.Sqrt)
            w.fence("scalar", "vector")
            w.op("vector", "reciprocal", m, m)
            w.fence("vector", "tensor")
            w.op("tensor", "matmul", ps_bA[:, :TT], onesr[0:1, :128], m,
                 start=True, stop=True)
            w.fence("tensor", "scalar")
            w.op("scalar", "copy", bcR, ps_bA[:, :TT])
            w.fence("scalar", "vector")
            for j in range(32):
                w.op("vector", "tensor_tensor", hg[j], hg[j], bcR,
                     op=ALU.mult)
                w.fence("vector", "vector")
                w.op("vector", "tensor_scalar", hg[j], hg[j],
                     nw[:, j:j + 1], None, op0=ALU.mult)
                w.fence("vector", "vector")
            w.fence("vector", "tensor")
            for j2 in range(16):
                w.fence("tensor", "gpsimd")
                w.dma("gpsimd",
                      wbuf2[j2 % 2].rearrange("p (c j) -> p c j", c=32),
                      opT[:, j2 * 128:(j2 + 1) * 128].rearrange(
                          "(c p) j -> p c j", p=128), "WT")
                w.wait("tensor", "WT")
                for kc in range(32):
                    w.op("tensor", "matmul", ps_mm[:, :],
                         wbuf2[j2 % 2][:, kc * 128:(kc + 1) * 128], hg[kc],
                         start=(kc == 0), stop=(kc == 31))
                w.fence("tensor", "scalar")
                w.op("scalar", "copy", o2[j2], ps_mm[:, :])
                w.fence("scalar", ("tensor", "vector"))
            ln_tile_g(w, env, o2, o2, TT)
            for c in range(16):
                par = c % 2
                if c >= 2:
                    w.fence("vector", "sync")
                w.dma("sync", xt[par][:, :], xT[c * 128:(c + 1) * 128,
                                               c0:c0 + TT], "LD")
                w.wait("vector", "LD")
                w.wait("vector", "ST")
                w.op("vector", "tensor_tensor", outst, xt[par][:, :], o2[c],
                     op=ALU.add)
                w.fence("vector", "sync")
                w.dma("sync", x3T[c * 128:(c + 1) * 128, c0:c0 + TT],
                      outst, "ST")
        w.wait("sync", "ST")
        w.wait("sync", "LD")
        w.wait("sync", "WT")

    run_walks(nc, st, emit)
    st.close()
    return nc


def l1a_c_inputs(x2_slices, y_slices, in_proj_w, norm_w, out_proj_w):
    gpT = np.ascontiguousarray(np.asarray(in_proj_w).T[:, 0:4096])
    opT = np.ascontiguousarray(np.asarray(out_proj_w).T)
    nwr = np.ascontiguousarray(np.asarray(norm_w).reshape(32, 128).T)
    ins = []
    for c in range(NCORES):
        ins.append({"xT": x2_slices[c], "yT": y_slices[c],
                    "gpT": gpT, "opT": opT, "nwr": nwr})
    return ins


def build_l1f(dbg=False):
    nc = bass.Bass("TRN2", target_bir_lowering=False)
    TT = 512
    S2048 = 1.0 / float(np.sqrt(2048.0))
    S8192 = 1.0 / float(np.sqrt(8192.0))

    xT = nc.dram_tensor("xT", [D, NTOK], F32, kind="ExternalInput").ap()
    WupT = nc.dram_tensor("WupT", [D, 8192], F32, kind="ExternalInput").ap()
    WgtT = nc.dram_tensor("WgtT", [D, 8192], F32, kind="ExternalInput").ap()
    WdnT = nc.dram_tensor("WdnT", [8192, D], F32, kind="ExternalInput").ap()
    suug = nc.dram_tensor("suug", [128, 16], F32, kind="ExternalInput").ap()
    sudp = nc.dram_tensor("sudp", [128, 64], F32, kind="ExternalInput").ap()
    wsup = nc.dram_tensor("wsup", [128, 64], F32, kind="ExternalInput").ap()
    wsgt = nc.dram_tensor("wsgt", [128, 64], F32, kind="ExternalInput").ap()
    wsdp = nc.dram_tensor("wsdp", [128, 16], F32, kind="ExternalInput").ap()
    h128 = nc.dram_tensor("h128", [128, 128], BF16, kind="ExternalInput").ap()
    x4T = nc.dram_tensor("x4T", [D, NTOK], F32, kind="ExternalOutput").ap()
    if dbg:
        z_dbg = nc.dram_tensor("z_dbg", [D, TT], F32, kind="ExternalOutput").ap()
        u_dbg = nc.dram_tensor("u_dbg", [8192, TT], F32, kind="ExternalOutput").ap()
        wf_dbg = nc.dram_tensor("wf_dbg", [8192, TT], F32, kind="ExternalOutput").ap()
        o_dbg = nc.dram_tensor("o_dbg", [D, TT], F32, kind="ExternalOutput").ap()

    st = contextlib.ExitStack()
    al = Alloc(nc, st, nb_cols_bf16=86000, nf_cols_f32=6000)
    srcO = al.b(32 * TT)
    src = [srcO[:, c * TT:(c + 1) * TT] for c in range(16)]
    o2 = [srcO[:, (16 + c) * TT:(17 + c) * TT] for c in range(16)]
    uscr = srcO
    zA = [al.b(TT) for _ in range(16)]
    zB = [al.b(TT) for _ in range(16)]
    ub = al.b(64 * TT)
    wbuf = [al.b(D) for _ in range(2)]
    wbufg = [al.b(D) for _ in range(2)]
    wbuf2 = [al.b(4096) for _ in range(2)]
    sgb = al.b(TT)
    h128b = al.b(128)
    sqbuf = al.b(1024)
    bcA = al.b(512)
    bcB = al.b(512)
    tmpb = al.b(512)
    ones_b = al.b(2)
    xt = [al.f(512) for _ in range(2)]
    rows = al.f(2048)
    onesr = al.f(128)
    suugf = al.f(16)
    sudpf = al.f(64)
    wsupf = al.f(64)
    wsgtf = al.f(64)
    wsdpf = al.f(16)
    outst = al.f(512)

    ps_up = st.enter_context(nc.psum_tensor("ps_up", [128, 512], F32))
    ps_gt = st.enter_context(nc.psum_tensor("ps_gt", [128, 512], F32))
    ps_h = st.enter_context(nc.psum_tensor("ps_h", [128, 512], F32))
    ps_st = st.enter_context(nc.psum_tensor("ps_st", [64, 512], F32))
    ps_bA = st.enter_context(nc.psum_tensor("ps_bA", [128, 512], F32))
    ps_bB = st.enter_context(nc.psum_tensor("ps_bB", [128, 512], F32))
    env = dict(ones_b=ones_b, onesr=onesr, sqbuf=sqbuf, rows=rows, bcA=bcA,
               bcB=bcB, tmpb=tmpb, ps_st=ps_st, ps_bA=ps_bA, ps_bB=ps_bB)

    def emit(w):
        w.op("vector", "memset", ones_b[:, 0:1], 1.0)
        w.op("vector", "memset", onesr[0:1, :128], 1.0)
        w.fence("vector", ("tensor", "scalar"))
        w.dma("sync", suugf[:, 0:16], suug, "LD")
        w.dma("sync", sudpf[:, 0:64], sudp, "LD")
        w.dma("sync", wsupf[:, 0:64], wsup, "LD")
        w.dma("sync", wsgtf[:, 0:64], wsgt, "LD")
        w.dma("sync", wsdpf[:, 0:16], wsdp, "LD")
        w.dma("gpsimd", h128b, h128, "WT")
        w.wait("vector", "LD")
        w.wait("scalar", "LD")
        w.wait("tensor", "WT")

        def butterfly(chunks_view, nch, scrv, width):
            import math
            nb = int(math.log2(nch))
            for b in range(nb):
                step = 1 << b
                nhi = nch // (2 * step)
                va = chunks_view.rearrange(
                    "p (hi two lo t) -> p hi two lo t", hi=nhi, two=2,
                    lo=step)
                a0 = va[:, :, 0, :, :]
                a1 = va[:, :, 1, :, :]
                sv = scrv.rearrange("p (hi lo t) -> p hi lo t", hi=nhi,
                                    lo=step)
                w.op("vector", "tensor_tensor", sv, a0, a1, op=ALU.subtract)
                w.fence("vector", "vector")
                w.op("vector", "tensor_tensor", a0, a0, a1, op=ALU.add)
                w.fence("vector", "vector")
                w.op("vector", "tensor_copy", a1, sv)
                w.fence("vector", "vector")

        for tt in range(4):
            c0 = tt * TT
            w.fence(("tensor", "vector", "scalar"), "gpsimd")
            for c in range(16):
                w.dma("gpsimd", src[c], xT[c * 128:(c + 1) * 128,
                                           c0:c0 + TT], "WT")
            w.wait("tensor", "WT")
            w.wait("scalar", "WT")
            w.wait("vector", "WT")
            ln_tile_g(w, env, src, src, TT)
            za, zb = zA, zB
            for c in range(16):
                w.op("vector", "tensor_scalar", za[c], src[c],
                     suugf[:, c:c + 1], S2048, op0=ALU.mult, op1=ALU.mult)
            w.fence("vector", "tensor")
            for c in range(16):
                w.op("tensor", "matmul", ps_h[:, :], h128b, za[c],
                     start=True, stop=True)
                w.fence("tensor", "scalar")
                w.op("scalar", "copy", zb[c], ps_h[:, :])
                w.fence("scalar", "tensor")
            w.fence("scalar", "vector")
            for b in range(4):
                step = 1 << b
                for i in range(16):
                    if i & step:
                        continue
                    j = i | step
                    w.op("vector", "tensor_tensor", za[i], zb[i], zb[j],
                         op=ALU.add)
                    w.op("vector", "tensor_tensor", za[j], zb[i], zb[j],
                         op=ALU.subtract)
                w.fence("vector", "vector")
                za, zb = zb, za
            z = zb
            w.fence("vector", ("tensor", "scalar"))
            if dbg and tt == 0:
                w.fence("vector", "gpsimd")
                for c in range(16):
                    w.dma("gpsimd", z_dbg[c * 128:(c + 1) * 128, :], z[c],
                          "GD")
            for j in range(64):
                par = j % 2
                w.fence("tensor", "gpsimd")
                w.dma("gpsimd", wbuf[par].rearrange("p (c j) -> p c j", c=16),
                      WupT[:, j * 128:(j + 1) * 128].rearrange(
                          "(c p) j -> p c j", p=128), "WT")
                w.dma("gpsimd", wbufg[par].rearrange("p (c j) -> p c j", c=16),
                      WgtT[:, j * 128:(j + 1) * 128].rearrange(
                          "(c p) j -> p c j", p=128), "WT")
                w.wait("tensor", "WT")
                for kc in range(16):
                    w.op("tensor", "matmul", ps_up[:, :],
                         wbuf[par][:, kc * 128:(kc + 1) * 128], z[kc],
                         start=(kc == 0), stop=(kc == 15))
                for kc in range(16):
                    w.op("tensor", "matmul", ps_gt[:, :],
                         wbufg[par][:, kc * 128:(kc + 1) * 128], z[kc],
                         start=(kc == 0), stop=(kc == 15))
                w.fence("tensor", "scalar")
                w.op("scalar", "activation", sgb, ps_gt[:, :], AF.Silu,
                     scale=wsgtf[:, j:j + 1])
                w.fence("scalar", "vector")
                uv = ub[:, j * TT:(j + 1) * TT]
                w.op("vector", "scalar_tensor_tensor", uv, ps_up[:, :],
                     wsupf[:, j:j + 1], sgb, op0=ALU.mult, op1=ALU.mult)
                w.fence("vector", "vector")
                w.op("vector", "tensor_scalar", uv, uv, sudpf[:, j:j + 1],
                     S8192, op0=ALU.mult, op1=ALU.mult)
                w.fence("vector", ("tensor", "scalar"))
            if dbg and tt == 0:
                w.fence("vector", "gpsimd")
                w.dma("gpsimd", u_dbg.rearrange("(a p) t -> p a t", p=128),
                      ub.rearrange("p (a t) -> p a t", a=64), "GD")
                w.wait("tensor", "GD")
                w.wait("scalar", "GD")
            for c in range(64):
                uv = ub[:, c * TT:(c + 1) * TT]
                w.op("tensor", "matmul", ps_h[:, :], h128b, uv,
                     start=True, stop=True)
                w.fence("tensor", "scalar")
                w.op("scalar", "copy", uv, ps_h[:, :])
                w.fence("scalar", ("tensor", "vector"))
            butterfly(ub[:, :], 64, uscr[:, :], TT)
            if dbg and tt == 0:
                w.fence("vector", "gpsimd")
                w.dma("gpsimd", wf_dbg.rearrange("(a p) t -> p a t", p=128),
                      ub.rearrange("p (a t) -> p a t", a=64), "GD")
                w.wait("tensor", "GD")
                w.wait("vector", "GD")
            w.fence("vector", "tensor")
            for j2 in range(16):
                par = j2 % 2
                w.fence("tensor", "gpsimd")
                w.dma("gpsimd",
                      wbuf2[par].rearrange("p (c j) -> p c j", c=32),
                      WdnT[0:4096, j2 * 128:(j2 + 1) * 128].rearrange(
                          "(c p) j -> p c j", p=128), "WT")
                w.wait("tensor", "WT")
                for kc in range(32):
                    w.op("tensor", "matmul", ps_up[:, :],
                         wbuf2[par][:, kc * 128:(kc + 1) * 128],
                         ub[:, kc * TT:(kc + 1) * TT],
                         start=(kc == 0), stop=False)
                w.fence("tensor", "gpsimd")
                w.dma("gpsimd",
                      wbuf2[par].rearrange("p (c j) -> p c j", c=32),
                      WdnT[4096:8192, j2 * 128:(j2 + 1) * 128].rearrange(
                          "(c p) j -> p c j", p=128), "WT")
                w.wait("tensor", "WT")
                for kc in range(32):
                    w.op("tensor", "matmul", ps_up[:, :],
                         wbuf2[par][:, kc * 128:(kc + 1) * 128],
                         ub[:, (32 + kc) * TT:(33 + kc) * TT],
                         start=False, stop=(kc == 31))
                w.fence("tensor", "vector")
                w.op("vector", "tensor_scalar", o2[j2], ps_up[:, :],
                     wsdpf[:, j2:j2 + 1], None, op0=ALU.mult)
                w.fence("vector", ("tensor", "scalar"))
            if dbg and tt == 0:
                w.fence("vector", "gpsimd")
                for j2 in range(16):
                    w.dma("gpsimd", o_dbg[j2 * 128:(j2 + 1) * 128, :],
                          o2[j2], "GD")
            ln_tile_g(w, env, o2, o2, TT)
            for c in range(16):
                par = c % 2
                if c >= 2:
                    w.fence("vector", "sync")
                w.dma("sync", xt[par][:, :], xT[c * 128:(c + 1) * 128,
                                               c0:c0 + TT], "LD")
                w.wait("vector", "LD")
                w.wait("vector", "ST")
                w.op("vector", "tensor_tensor", outst, xt[par][:, :], o2[c],
                     op=ALU.add)
                w.fence("vector", "sync")
                w.dma("sync", x4T[c * 128:(c + 1) * 128, c0:c0 + TT],
                      outst, "ST")
        w.wait("sync", "ST")
        w.wait("sync", "LD")
        w.wait("sync", "WT")
        w.wait("sync", "GD")

    run_walks(nc, st, emit)
    st.close()
    return nc


def l1f_inputs(x3_slices, SU_ug, SU_dp, Wscale_ug, Wscale_dp, W_up, W_gate,
               W_down):
    WupT = np.ascontiguousarray(np.asarray(W_up).T)
    WgtT = np.ascontiguousarray(np.asarray(W_gate).T)
    WdnT = np.ascontiguousarray(np.asarray(W_down).T)
    pack = lambda v, n: np.ascontiguousarray(
        np.asarray(v).reshape(n, 128).T.astype(np.float32))
    wsu = np.asarray(Wscale_ug)
    Hm = np.array([[1.0]])
    while Hm.shape[0] < 128:
        Hm = np.block([[Hm, Hm], [Hm, -Hm]])
    ins = []
    for c in range(NCORES):
        ins.append({
            "xT": x3_slices[c], "WupT": WupT, "WgtT": WgtT, "WdnT": WdnT,
            "suug": pack(SU_ug, 16), "sudp": pack(SU_dp, 64),
            "wsup": pack(wsu[:8192], 64), "wsgt": pack(wsu[8192:], 64),
            "wsdp": pack(Wscale_dp, 16), "h128": Hm.astype(BF),
        })
    return ins


BAND2 = 2688
KVLEN2 = BAND2 + FRONT


def build_fused(upto=99, dbg=False, p4chunks=8, p4parts=7, p8f=7, ugn=2):
    nc = bass.Bass("TRN2", target_bir_lowering=False)
    TT = 512
    S2048 = 1.0 / float(np.sqrt(2048.0))
    S8192 = 1.0 / float(np.sqrt(8192.0))

    def din(name, shape, dt=F32):
        return nc.dram_tensor(name, shape, dt, kind="ExternalInput").ap()

    x_band = din("x_band", [D, BAND2])
    x_front = din("x_front", [D, FRONT])
    wq = din("wq", [D, D]); wk = din("wk", [D, 512])
    wv = din("wv", [D, 512]); wo = din("wo", [D, D])
    cs_band = din("cs_band", [128, BAND2])
    cs2_band = din("cs2_band", [128, BAND2])
    cs_front = din("cs_front", [128, FRONT])
    cs2_front = din("cs2_front", [128, FRONT])
    cs_q = din("cs_q", [128, 2176]); cs2_q = din("cs2_q", [128, 2176])
    masks = din("masks", [17, 128, 896], BF16)
    identb = din("identb", [128, 128], BF16)
    identf = din("identf", [128, 128], F32)
    W1 = din("W1", [D, D]); b1r = din("b1r", [128, 16])
    W2 = din("W2", [D, 8]); b2c = din("b2c", [8, 1])
    VPd = din("VP", [4, D, 128]); UPd = din("UP", [4, 128, D])
    SELd = din("SEL", [8, 512])
    wpT = din("wpT", [D, 4416])
    cwr = din("cwr", [128, 136]); cbr = din("cbr", [128, 34])
    dtb = din("dtb", [64, 1])
    hmaskd = din("hmask", [128, 1])
    alogd = din("alog", [64, 1])
    dbc_in = din("dbc_in", [128, 64])
    ehseld = din("ehsel", [64, 8192])
    trim = din("trim", [128, 128], BF16)
    mv1d = din("mv1", [128, 3]); mv2d = din("mv2", [128, 3])
    gpT = din("gpT", [D, 4096]); opT = din("opT", [4096, D])
    nwr = din("nwr", [128, 32])
    WupT = din("WupT", [D, 8192]); WgtT = din("WgtT", [D, 8192])
    WdnT = din("WdnT", [8192, D])
    suugd = din("suug", [128, 16]); sudpd = din("sudp", [128, 64])
    wsupd = din("wsup", [128, 64]); wsgtd = din("wsgt", [128, 64])
    wsdpd = din("wsdp", [128, 16])
    h128d = din("h128", [128, 128], BF16)
    outT = nc.dram_tensor("outT", [D, NTOK], F32, kind="ExternalOutput").ap()
    if dbg:
        ydbg = nc.dram_tensor("ydbg", [NTOK, 4096], F32,
                              kind="ExternalOutput").ap()
        xsdbg = nc.dram_tensor("xsdbg", [4096, NTOK], F32,
                               kind="ExternalOutput").ap()
        pdbg = nc.dram_tensor("pdbg", [8, 128, 4096], F32,
                              kind="ExternalOutput").ap()
        u8dbg = nc.dram_tensor("u8dbg", [8192, 512], F32,
                               kind="ExternalOutput").ap()
    xs_scr = nc.dram_tensor("xs_scr", [4096, NTOK], BF16).ap()
    wpT_b = nc.dram_tensor("wpT_b", [D, 4416], BF16).ap()
    gpT_b = nc.dram_tensor("gpT_b", [D, 4096], BF16).ap()
    opT_b = nc.dram_tensor("opT_b", [4096, D], BF16).ap()
    WupT_b = nc.dram_tensor("WupT_b", [D, 8192], BF16).ap()
    WgtT_b = nc.dram_tensor("WgtT_b", [D, 8192], BF16).ap()
    WdnT_b = nc.dram_tensor("WdnT_b", [8192, D], BF16).ap()
    y_scr = nc.dram_tensor("y_scr", [NTOK, 4096], BF16).ap()
    sx_in = nc.dram_tensor("sx_in", [128, 4160], F32)
    sx_out = nc.dram_tensor("sx_out", [512, 4160], F32)

    st = contextlib.ExitStack()

    class Alloc2:
        def __init__(self, nc, st, nb):
            self.ab = st.enter_context(
                nc.sbuf_tensor("arena", [128, nb], BF16))
            self.ob = 0
            self.nb = nb

        def b(self, w):
            self.ob = (self.ob + 15) & ~15
            assert self.ob + w <= self.nb, f"arena overflow {self.ob}+{w}"
            v = self.ab[:, self.ob:self.ob + w]
            self.ob += w
            return v

        def f(self, w):
            self.ob = (self.ob + 15) & ~15
            v = self.ab[:, self.ob:self.ob + 2 * w].bitcast(F32)
            self.ob += 2 * w
            return v

        def save(self):
            return self.ob

        def restore(self, s):
            self.ob = s

    al = Alloc2(nc, st, 106400)
    resid = [al.b(NTOK) for _ in range(16)]
    idb = al.b(128)
    trimask = al.b(128)
    h128b = al.b(128)
    ones_b = al.b(2)
    sqbuf = al.b(1024)
    bcA = al.b(512)
    bcB = al.b(512)
    tmpb = al.b(512)
    onesr_f = al.f(128)
    idf = al.f(128)
    rows = al.f(2048)
    mPerm = al.save()
    x2h = [al.b(128) for _ in range(16)]
    mA = al.save()
    Bres = al.b(NTOK)
    Cres = al.b(NTOK)
    pc_all = al.b(8 * 64)
    ect_all = al.b(8 * 2 * 64)
    dt2tok = al.f(16 * 64)
    S_in = al.f(4096)
    pc_cur = al.f(64)
    mB = al.save()

    ps_f = st.enter_context(nc.psum_tensor("ps_f", [128, 3584], F32))
    ps_b = st.enter_context(nc.psum_tensor("ps_b", [128, 1024], BF16))
    SL = [ps_f[:, i * 512:(i + 1) * 512] for i in range(7)]
    env = dict(ones_b=ones_b, onesr=onesr_f, sqbuf=sqbuf, rows=rows, bcA=bcA,
               bcB=bcB, tmpb=tmpb, ps_st=ps_f[0:64, 3072:3584],
               ps_bA=SL[4], ps_bB=SL[5])

    al.restore(mA)
    hb = [al.b(TT) for _ in range(16)]
    K1 = [al.b(KVLEN2) for _ in range(4)]
    VT1 = [al.b(KVLEN2) for _ in range(4)]
    q2 = [al.b(TT) for _ in range(16)]
    ao2 = [al.b(TT) for _ in range(16)]
    wbuf1 = [al.b(D) for _ in range(2)]
    cs_t = al.b(512)
    cs_t2 = al.b(512)
    scr1 = al.b(1024)
    Pt = al.b(896)
    PTs = al.b(896)
    mskb = al.b(896)
    scm = al.f(896)
    cols1 = al.f(8)

    al.restore(mA)
    hb2 = [al.b(TT) for _ in range(16)]
    hid = [al.b(TT) for _ in range(16)]
    o22 = [al.b(TT) for _ in range(16)]
    vx = [al.b(TT) for _ in range(4)]
    wsc = [al.b(TT) for _ in range(4)]
    wbuf2f = [al.b(D) for _ in range(2)]
    vpw = [al.b(D) for _ in range(4)]
    upw = [al.b(D) for _ in range(4)]
    w2w = al.b(8 * 16)
    sa2 = al.f(512)
    at2 = al.f(32)
    am2 = al.f(4)
    asum2 = al.f(4)
    alf2 = al.f(512)
    b1s = al.f(16)
    b2s = al.f(1)
    selw = al.f(512)

    al.restore(mB)
    a_src = [al.b(TT) for _ in range(16)]
    a_wbuf = [al.b(D) for _ in range(2)]
    projx2 = [al.b(516) for _ in range(2)]
    xbc32 = [al.b(TT) for _ in range(2)]
    acc32f = 0
    hh3 = [al.b(4) for _ in range(16)]
    cwf = al.f(136)
    cbf = al.f(34)
    dtbc = al.f(1)
    tails = al.f(102)
    acc3 = al.f(512)
    acc32 = [al.f(512) for _ in range(2)]
    dtx3 = al.f(512)
    hmk = al.f(1)

    al.restore(mB)
    xT4 = [al.b(4096) for _ in range(2)]
    BTt = [al.b(128) for _ in range(2)]
    CBTm4 = [al.b(256) for _ in range(2)]
    Pg4 = [al.b(4096) for _ in range(2)]
    xdt4 = [al.b(1024) for _ in range(2)]
    xdec4 = [al.b(1024) for _ in range(2)]
    tseg2b = al.b(1024)
    S_bf4 = al.b(1024)
    dAf4 = al.f(256)
    cumf4 = al.f(256)
    cumT44 = [al.f(64) for _ in range(2)]
    decf44 = [al.f(64) for _ in range(2)]
    ect44 = [al.f(64) for _ in range(2)]
    sfac44 = [al.f(64) for _ in range(2)]
    rexp4 = al.f(64)
    acol4 = al.f(1)
    dbc44 = al.f(64)
    rrow4 = al.f(64)
    tseg4 = al.f(1024)
    ehs4 = al.f(2048)
    yst4 = al.f(1024)
    mP4 = al.save()

    al.restore(mB)
    xtmp5 = al.f(4096)
    lam5 = al.f(64)
    mv1s = al.f(3)
    mv2s = al.f(3)
    sc6 = al.b(4096)
    yrmw = al.b(1024)
    corr6 = al.f(1024)

    al.restore(mPerm)
    hb7 = [al.b(TT) for _ in range(16)]
    hg7 = [al.b(TT) for _ in range(32)]
    c_wbuf = [al.b(D) for _ in range(2)]
    c_wbuf2 = [al.b(4096) for _ in range(2)]
    sg7 = al.b(TT)
    sg7b = al.b(TT)
    yb7 = al.b(TT)
    yb7b = al.b(TT)
    bcR7 = al.b(TT)
    o27 = [al.b(TT) for _ in range(16)]
    nw7 = al.f(32)

    al.restore(mPerm)
    srcO8 = al.b(32 * TT)
    src8 = [srcO8[:, c * TT:(c + 1) * TT] for c in range(16)]
    o28 = [srcO8[:, (16 + c) * TT:(17 + c) * TT] for c in range(16)]
    uscr8 = srcO8
    ub8 = al.b(64 * TT)
    f_wbufA = [al.b(D) for _ in range(2)]
    _g = al.b(D)
    f_wbufG = [_g, _g]
    f_wbuf2 = [al.b(4096) for _ in range(2)]
    sgb8 = al.b(TT)
    sgb8b = al.b(TT)
    suugf = al.f(16)
    sudpf = al.f(64)
    wsupf = al.f(64)
    wsgtf = al.f(64)
    wsdpf = al.f(16)
    outst8 = al.f(512)

    NQT = [(-128, 0, 128, 0, "halo")] + [
        (qt * 512, 128 + qt * 512, 512, 1 + qt * 4, "own") for qt in range(4)]

    def emit(w):
        ALLE = ("sync", "tensor", "vector", "scalar", "gpsimd")

        def phase_barrier():
            w.fence(("tensor", "vector", "scalar"), ALLE)
            for sem in ("LD", "ST", "WT", "GD"):
                for e in ALLE:
                    w.wait(e, sem)

        def ln(srcs, dsts, T):
            ln_tile_g(w, env, srcs, dsts, T)

        w.op("vector", "memset", ones_b[:, 0:1], 1.0)
        w.op("vector", "memset", onesr_f[0:1, :128], 1.0)
        w.fence("vector", ("tensor", "scalar"))
        w.dma("sync", idf, identf, "LD")
        w.dma("gpsimd", idb, identb, "WT")
        w.dma("gpsimd", trimask, trim, "WT")
        w.dma("gpsimd", h128b, h128d, "WT")
        for c in range(16):
            w.dma("gpsimd", resid[c],
                  x_band[c * 128:(c + 1) * 128, 640:2688], "WT")
            w.dma("gpsimd", x2h[c],
                  x_band[c * 128:(c + 1) * 128, 512:640], "WT")
        w.wait("tensor", "WT")
        w.wait("tensor", "LD")
        w.wait("scalar", "WT")
        w.wait("vector", "WT")
        pc_jobs = []
        for (src_t, dst_t, rows_n) in ((wpT, wpT_b, D), (gpT, gpT_b, D),
                                       (opT, opT_b, 4096),
                                       (WupT, WupT_b, D), (WgtT, WgtT_b, D),
                                       (WdnT, WdnT_b, 8192)):
            step = 512
            for r0 in range(0, rows_n, step):
                pc_jobs.append((dst_t, src_t, r0, r0 + step))
        pc_total = 16 * len(pc_jobs)

        def drain_pc(k):
            for _ in range(k):
                if pc_jobs:
                    dst_t, src_t, a, b = pc_jobs.pop(0)
                    w.dma("gpsimd", dst_t[a:b, :], src_t[a:b, :], "PC")

        def drain_pc_all():
            drain_pc(len(pc_jobs))

        def rope_from_psum(ps, dst, T):
            a = scr1[0:64, 0:T]
            b = scr1[0:64, 512:512 + T]
            a2 = scr1[64:128, 0:T]
            b2 = scr1[64:128, 512:512 + T]
            w.op("vector", "tensor_tensor", a, ps[0:64, :T], cs_t[0:64, :T],
                 op=ALU.mult)
            w.op("vector", "tensor_tensor", b, ps[64:128, :T],
                 cs_t[64:128, :T], op=ALU.mult)
            w.op("vector", "tensor_tensor", a2, ps[0:64, :T],
                 cs_t2[0:64, :T], op=ALU.mult)
            w.op("vector", "tensor_tensor", b2, ps[64:128, :T],
                 cs_t2[64:128, :T], op=ALU.mult)
            w.fence("vector", "vector")
            w.op("vector", "tensor_tensor", dst[0:64, :T], a, b,
                 op=ALU.subtract)
            w.op("vector", "tensor_tensor", dst[64:128, :T], a2, b2,
                 op=ALU.add)

        def stream_wcol(dram_w, j, par, wb, nch=16):
            w.fence("tensor", "gpsimd")
            w.dma("gpsimd", wb[par][:, :nch * 128].rearrange(
                "p (c j) -> p c j", c=nch),
                dram_w[:, j * 128:(j + 1) * 128].rearrange(
                    "(c p) j -> p c j", p=128), "WT")
            w.wait("tensor", "WT")

        def p1(w):
            ps_qkv = SL[0]
            ps_sc = ps_f[:, 512:1408]
            ps_att = ps_f[:, 1536:1664]
            kvt_list = [(0, 128), (128, 512), (640, 512), (1152, 512),
                        (1664, 512), (2176, 512), (-1, 256)]
            for (koff, T) in kvt_list:
                drain_pc(4)
                if koff >= 0:
                    src_ap, col0 = x_band, koff
                    csrc = (cs_band, cs2_band)
                else:
                    src_ap, col0 = x_front, 0
                    csrc = (cs_front, cs2_front)
                    koff = BAND2
                dst = [hb[c][:, :T] for c in range(16)]
                w.fence(("tensor", "vector", "scalar"), "gpsimd")
                for c in range(16):
                    w.dma("gpsimd", dst[c],
                          src_ap[c * 128:(c + 1) * 128, col0:col0 + T], "WT")
                w.wait("tensor", "WT")
                w.wait("scalar", "WT")
                w.wait("vector", "WT")
                ln(dst, dst, T)
                w.fence("vector", "gpsimd")
                w.dma("gpsimd", cs_t[:, :T], csrc[0][:, col0:col0 + T], "WT")
                w.dma("gpsimd", cs_t2[:, :T], csrc[1][:, col0:col0 + T], "WT")
                w.wait("vector", "WT")
                PQ = (SL[0], SL[4])

                def k_load(j):
                    w.dma("gpsimd", wbuf1[j % 2].rearrange(
                        "p (c j) -> p c j", c=16),
                        wk[:, j * 128:(j + 1) * 128].rearrange(
                            "(c p) j -> p c j", p=128), "WT")

                def k_mm(j):
                    for kc in range(16):
                        w.op("tensor", "matmul", PQ[j % 2][:, :T],
                             wbuf1[j % 2][:, kc * 128:(kc + 1) * 128],
                             dst[kc], start=(kc == 0), stop=(kc == 15))

                def k_evac(j):
                    rope_from_psum(PQ[j % 2], K1[j][:, koff:koff + T], T)

                w.fence(("vector", "scalar"), "tensor")
                w.fence("tensor", "gpsimd")
                w.pipe(4, k_load, k_mm, k_evac, "vector")
                w.fence(("vector", "tensor"), ("tensor", "vector", "gpsimd"))
                for j in range(4):
                    stream_wcol(wv, j, j % 2, wbuf1)
                    for kc in range(16):
                        w.op("tensor", "matmul", ps_qkv[:, :T],
                             wbuf1[j % 2][:, kc * 128:(kc + 1) * 128],
                             dst[kc], start=(kc == 0), stop=(kc == 15))
                    w.fence("tensor", "scalar")
                    w.op("scalar", "copy", tmpb[:, :T], ps_qkv[:, :T])
                    w.fence("scalar", "tensor")
                    for t in range(T // 128):
                        w.op("tensor", "transpose",
                             ps_b[:, t * 128:(t + 1) * 128],
                             tmpb[:, t * 128:(t + 1) * 128], idb)
                    w.fence("tensor", "vector")
                    w.op("vector", "tensor_copy",
                         VT1[j][:, koff:koff + T], ps_b[:, :T])
                    w.fence("vector", "tensor")
            for (qoff, csoff, T, Qg0, kind) in NQT:
                drain_pc(8)
                bcol = qoff + 640
                hq = [hb[c][:, :T] for c in range(16)]
                w.fence(("tensor", "vector", "scalar"), "gpsimd")
                for c in range(16):
                    w.dma("gpsimd", hq[c],
                          x_band[c * 128:(c + 1) * 128, bcol:bcol + T], "WT")
                w.wait("tensor", "WT")
                w.wait("scalar", "WT")
                w.wait("vector", "WT")
                ln(hq, hq, T)
                w.fence("vector", "gpsimd")
                w.dma("gpsimd", cs_t[:, :T], cs_q[:, csoff:csoff + T], "WT")
                w.dma("gpsimd", cs_t2[:, :T], cs2_q[:, csoff:csoff + T], "WT")
                w.wait("vector", "WT")

                PQ = (SL[0], SL[4])

                def q_load(j):
                    w.dma("gpsimd", wbuf1[j % 2].rearrange(
                        "p (c j) -> p c j", c=16),
                        wq[:, j * 128:(j + 1) * 128].rearrange(
                            "(c p) j -> p c j", p=128), "WT")

                def q_mm(j):
                    for kc in range(16):
                        w.op("tensor", "matmul", PQ[j % 2][:, :T],
                             wbuf1[j % 2][:, kc * 128:(kc + 1) * 128],
                             hq[kc], start=(kc == 0), stop=(kc == 15))

                def q_evac(j):
                    rope_from_psum(PQ[j % 2], q2[j][:, :T], T)

                w.fence(("vector", "scalar"), "tensor")
                w.fence("tensor", "gpsimd")
                w.pipe(16, q_load, q_mm, q_evac, "vector")
                w.fence(("vector", "tensor"), ("tensor", "vector", "gpsimd"))
                for h in range(16):
                    g = h // 4
                    for qb in range(T // 128):
                        Qg = Qg0 + qb
                        w.fence("vector", "sync")
                        w.dma("sync", mskb, masks[Qg], "LD")
                        b0 = Qg * 128
                        w.op("tensor", "matmul", ps_sc[:, 0:512],
                             q2[h][:, qb * 128:(qb + 1) * 128],
                             K1[g][:, b0:b0 + 512], start=True, stop=True)
                        w.op("tensor", "matmul", ps_sc[:, 512:640],
                             q2[h][:, qb * 128:(qb + 1) * 128],
                             K1[g][:, b0 + 512:b0 + 640], start=True,
                             stop=True)
                        w.op("tensor", "matmul", ps_sc[:, 640:896],
                             q2[h][:, qb * 128:(qb + 1) * 128],
                             K1[g][:, BAND2:BAND2 + 256], start=True,
                             stop=True)
                        w.fence("tensor", "vector")
                        w.wait("vector", "LD")
                        w.op("vector", "scalar_tensor_tensor", scm, ps_sc,
                             0.08838834764831845, mskb, op0=ALU.mult,
                             op1=ALU.add)
                        w.fence("vector", "vector")
                        w.op("vector", "reduce_max", cols1[:, 0:1], scm,
                             axis=AX.X)
                        w.fence("vector", "vector")
                        w.op("vector", "tensor_scalar", cols1[:, 1:2],
                             cols1[:, 0:1], -1.0, None, op0=ALU.mult)
                        w.fence("vector", ("scalar", "tensor"))
                        w.op("scalar", "activation", Pt, scm, AF.Exp,
                             bias=cols1[:, 1:2], accum_out=cols1[:, 2:3])
                        w.fence("scalar", "vector")
                        w.op("vector", "reciprocal", cols1[:, 3:4],
                             cols1[:, 2:3])
                        w.fence("vector", "vector")
                        w.op("vector", "tensor_scalar", Pt, Pt,
                             cols1[:, 3:4], None, op0=ALU.mult)
                        w.fence("vector", "tensor")
                        for sx in range(7):
                            w.op("tensor", "transpose",
                                 ps_b[:, sx * 128:(sx + 1) * 128],
                                 Pt[:, sx * 128:(sx + 1) * 128], idb)
                        w.fence("tensor", "vector")
                        w.op("vector", "tensor_copy", PTs, ps_b[:, :896])
                        w.fence("vector", "tensor")
                        for sx in range(7):
                            gk = Qg + sx if sx < 5 else 21 + (sx - 5)
                            w.op("tensor", "matmul", ps_att,
                                 VT1[g][:, gk * 128:(gk + 1) * 128],
                                 PTs[:, sx * 128:(sx + 1) * 128],
                                 start=(sx == 0), stop=(sx == 6))
                        w.fence("tensor", "scalar")
                        w.op("scalar", "copy",
                             ao2[h][:, qb * 128:(qb + 1) * 128], ps_att)
                        w.fence("scalar", "tensor")
                PQ = (SL[0], SL[4])

                def wo_load(j):
                    w.dma("gpsimd", wbuf1[j % 2].rearrange(
                        "p (c j) -> p c j", c=16),
                        wo[:, j * 128:(j + 1) * 128].rearrange(
                            "(c p) j -> p c j", p=128), "WT")

                def wo_mm(j):
                    for kc in range(16):
                        w.op("tensor", "matmul", PQ[j % 2][:, :T],
                             wbuf1[j % 2][:, kc * 128:(kc + 1) * 128],
                             ao2[kc][:, :T],
                             start=(kc == 0), stop=(kc == 15))

                def wo_evac(j):
                    w.op("scalar", "copy", q2[j][:, :T], PQ[j % 2][:, :T])

                w.fence(("vector", "scalar"), "tensor")
                w.fence("tensor", "gpsimd")
                w.pipe(16, wo_load, wo_mm, wo_evac, "scalar")
                w.fence(("scalar", "tensor"), ("tensor", "vector", "scalar"))
                ln([t[:, :T] for t in q2], [t[:, :T] for t in q2], T)
                tgt = x2h if kind == "halo" else [
                    resid[c][:, qoff:qoff + T] for c in range(16)]
                for c in range(16):
                    w.op("vector", "tensor_tensor", tgt[c], tgt[c],
                         q2[c][:, :T], op=ALU.add)
                w.fence("vector", ("tensor", "scalar"))

        def p2(w):
            drain_pc_all()
            ps_mm = SL[0]
            ps_a = ps_f[0:8, 512:1024]
            ps_at = ps_f[:, 1024:1056]
            ps_af = ps_f[0:8, 1536:2048]
            ps_bc = ps_f[:, 512:1024]
            for m in range(4):
                w.dma("gpsimd", vpw[m].rearrange("p (c j) -> p c j", c=16),
                      VPd[m].rearrange("(c p) j -> p c j", p=128), "WT")
                w.dma("gpsimd", upw[m], UPd[m], "WT")
            w.dma("gpsimd", w2w.rearrange("p (c j) -> p c j", c=16),
                  W2.rearrange("(c p) j -> p c j", p=128), "WT")
            w.dma("sync", b1s[:, 0:16], b1r, "LD")
            w.dma("sync", b2s[0:8, 0:1], b2c, "LD")
            w.dma("sync", selw[0:8, :], SELd, "LD")
            w.wait("tensor", "WT")
            w.wait("tensor", "LD")
            w.wait("scalar", "LD")
            tiles = [(None, 128)] + [(tt, 512) for tt in range(4)]
            for (tt, T) in tiles:
                if tt is None:
                    srcv = x2h
                else:
                    srcv = [resid[c][:, tt * 512:tt * 512 + T]
                            for c in range(16)]
                hsrc = [hb2[c][:, :T] for c in range(16)]
                ln(srcv, hsrc, T)
                PQ = (SL[0], SL[4])

                def w1_load(j):
                    w.dma("gpsimd", wbuf2f[j % 2].rearrange(
                        "p (c j) -> p c j", c=16),
                        W1[:, j * 128:(j + 1) * 128].rearrange(
                            "(c p) j -> p c j", p=128), "WT")

                def w1_mm(j):
                    for kc in range(16):
                        w.op("tensor", "matmul", PQ[j % 2][:, :T],
                             wbuf2f[j % 2][:, kc * 128:(kc + 1) * 128],
                             hsrc[kc], start=(kc == 0), stop=(kc == 15))

                def w1_evac(j):
                    w.op("scalar", "activation", hid[j][:, :T],
                         PQ[j % 2][:, :T], AF.Relu, bias=b1s[:, j:j + 1])

                w.fence(("vector", "scalar"), "tensor")
                w.fence("tensor", "gpsimd")
                w.pipe(16, w1_load, w1_mm, w1_evac, "scalar")
                w.fence(("scalar", "tensor"), ("tensor", "scalar", "vector"))
                for kc in range(16):
                    w.op("tensor", "matmul", ps_a[:, :T],
                         w2w[:, kc * 8:(kc + 1) * 8], hid[kc][:, :T],
                         start=(kc == 0), stop=(kc == 15))
                w.fence("tensor", "scalar")
                w.op("scalar", "activation", sa2[0:8, :T], ps_a[:, :T],
                     AF.Identity, bias=b2s[0:8, 0:1])
                w.fence("scalar", "tensor")
                nb = T // 128
                for b in range(nb):
                    w.op("tensor", "transpose", ps_at[:, b * 8:(b + 1) * 8],
                         sa2[0:8, b * 128:(b + 1) * 128], idf[0:8, 0:8])
                w.fence("tensor", "vector")
                atv = at2[:, 0:nb * 8].rearrange("p (b e) -> p b e", e=8)
                w.op("vector", "tensor_copy", at2[:, 0:nb * 8],
                     ps_at[:, 0:nb * 8])
                w.fence("vector", "vector")
                w.op("vector", "reduce_max", am2[:, 0:nb], atv, axis=AX.X)
                w.fence("vector", "vector")
                amb = am2[:, 0:nb, None].broadcast_to([128, nb, 8])
                w.op("vector", "tensor_tensor", atv, atv, amb,
                     op=ALU.subtract)
                w.fence("vector", "scalar")
                w.op("scalar", "activation", at2[:, 0:nb * 8],
                     at2[:, 0:nb * 8], AF.Exp)
                w.fence("scalar", "vector")
                w.op("vector", "tensor_reduce", asum2[:, 0:nb], atv,
                     axis=AX.X, op=ALU.add)
                w.fence("vector", "vector")
                w.op("vector", "reciprocal", asum2[:, 0:nb], asum2[:, 0:nb])
                w.fence("vector", "vector")
                asb = asum2[:, 0:nb, None].broadcast_to([128, nb, 8])
                w.op("vector", "tensor_tensor", atv, atv, asb, op=ALU.mult)
                w.fence("vector", "tensor")
                for b in range(nb):
                    w.op("tensor", "transpose",
                         ps_af[:, b * 128:(b + 1) * 128],
                         at2[:, b * 8:(b + 1) * 8], idf)
                w.fence("tensor", "vector")
                w.op("vector", "tensor_copy", alf2[0:8, :T],
                     ps_af[:, 0:T])
                w.fence("vector", "tensor")
                PQ = (SL[0], SL[4])

                def vx_mm(m):
                    for kc in range(16):
                        w.op("tensor", "matmul", PQ[m % 2][:, :T],
                             vpw[m][:, kc * 128:(kc + 1) * 128], hsrc[kc],
                             start=(kc == 0), stop=(kc == 15))

                def vx_evac(m):
                    w.op("scalar", "copy", vx[m][:, :T], PQ[m % 2][:, :T])

                w.fence(("vector", "scalar"), "tensor")
                w.pipe(4, None, vx_mm, vx_evac, "scalar")
                w.fence(("scalar", "tensor"), ("tensor", "scalar", "vector"))
                for m in range(4):
                    w.op("tensor", "matmul", ps_bc[:, :T],
                         selw[0:8, m * 128:(m + 1) * 128], alf2[0:8, :T],
                         start=True, stop=True)
                    w.fence("tensor", "vector")
                    w.op("vector", "tensor_tensor", wsc[m][:, :T],
                         vx[m][:, :T], ps_bc[:, :T], op=ALU.mult)
                    w.fence("vector", "tensor")
                PQ = (SL[0], SL[4])

                def uo_mm(j):
                    for m in range(4):
                        w.op("tensor", "matmul", PQ[j % 2][:, :T],
                             upw[m][:, j * 128:(j + 1) * 128], wsc[m][:, :T],
                             start=(m == 0), stop=(m == 3))

                def uo_evac(j):
                    w.op("scalar", "copy", o22[j][:, :T], PQ[j % 2][:, :T])

                w.fence(("vector", "scalar"), "tensor")
                w.pipe(16, None, uo_mm, uo_evac, "scalar")
                w.fence(("scalar", "tensor"), ("tensor", "scalar", "vector"))
                ln([t[:, :T] for t in o22], [t[:, :T] for t in o22], T)
                tgt = x2h if tt is None else [
                    resid[c][:, tt * 512:tt * 512 + T] for c in range(16)]
                for c in range(16):
                    w.op("vector", "tensor_tensor", tgt[c], tgt[c],
                         o22[c][:, :T], op=ALU.add)
                w.fence("vector", ("tensor", "scalar"))

        def p3(w):
            for e in ("sync", "scalar", "tensor", "gpsimd"):
                w.wait(e, "PC", pc_total)
            ps_mm = SL[0]
            ps_h3 = ps_f[:, 1024:1032]
            w.dma("sync", cwf, cwr, "LD")
            w.dma("sync", cbf, cbr, "LD")
            w.dma("sync", dtbc[0:64, 0:1], dtb, "LD")
            w.dma("sync", hmk, hmaskd, "LD")
            w.wait("scalar", "LD")
            w.wait("vector", "LD")
            ln([t[:, 124:128] for t in x2h], [t[:, 0:4] for t in hh3], 4)
            for tt in range(4):
                c0 = tt * TT
                srcv = [resid[c][:, c0:c0 + TT] for c in range(16)]
                hsrc = [a_src[c] for c in range(16)]
                ln(srcv, hsrc, TT)
                gd_after = [0] * 35
                pp_after = [0] * 35
                sc_after = [0] * 35
                for j in range(35):
                    wd = 128 if j < 34 else 64
                    jc0 = j * 128
                    par = j % 2
                    if j >= 2:
                        w.wait("gpsimd", "PP", pp_after[j - 2])
                    w.dma("gpsimd",
                          a_wbuf[par][:, :16 * wd].rearrange(
                              "p (c j) -> p c j", c=16),
                          wpT_b[:, jc0:jc0 + wd].rearrange(
                              "(c p) j -> p c j", p=128), "WT")
                    w.wait("tensor", "WT")
                    if j >= 2:
                        w.wait("tensor", "SC", sc_after[j - 2])
                    sl = SL[par] if j < 34 else SL[par][0:64, :]
                    for kc in range(16):
                        w.op("tensor", "matmul", sl[:, :TT] if j < 34
                             else SL[par][0:64, :TT],
                             a_wbuf[par][:, kc * wd:(kc + 1) * wd],
                             hsrc[kc], start=(kc == 0), stop=(kc == 15))
                    if j < 34 and tt == 0:
                        for kc in range(16):
                            w.op("tensor", "matmul", ps_h3[:, 0:4],
                                 a_wbuf[par][:, kc * 128:(kc + 1) * 128],
                                 hh3[kc][:, 0:4],
                                 start=(kc == 0), stop=(kc == 15))
                    pp_after[j] = w.inc_last("tensor", "PP")
                    w.wait("scalar", "PP", pp_after[j])
                    if j >= 2:
                        w.wait("scalar", "GD", gd_after[j - 2])
                        w.wait("vector", "GD", gd_after[j - 2])
                    if j < 34:
                        w.op("scalar", "copy", projx2[par][:, 3:515],
                             SL[par][:, :TT])
                        if tt == 0:
                            w.op("scalar", "copy", projx2[par][:, 0:3],
                                 ps_h3[:, 1:4])
                        sc_after[j] = w.inc_last("scalar", "SC")
                        w.fence("scalar", "vector")
                        if tt == 0:
                            w.op("vector", "tensor_scalar",
                                 projx2[par][:, 0:3], projx2[par][:, 0:3],
                                 hmk[:, 0:1], None, op0=ALU.mult)
                            w.fence("vector", "vector")
                        else:
                            w.op("vector", "tensor_copy",
                                 projx2[par][:, 0:3],
                                 tails[:, j * 3:j * 3 + 3])
                            w.fence("vector", "vector")
                        w.op("vector", "tensor_scalar", acc32[par],
                             projx2[par][:, 0:512],
                             cwf[:, j * 4:j * 4 + 1], cbf[:, j:j + 1],
                             op0=ALU.mult, op1=ALU.add)
                        for i in range(1, 4):
                            w.fence("vector", "vector")
                            w.op("vector", "scalar_tensor_tensor", acc32[par],
                                 projx2[par][:, i:i + 512],
                                 cwf[:, j * 4 + i:j * 4 + i + 1], acc32[par],
                                 op0=ALU.mult, op1=ALU.add)
                        w.fence("vector", "vector")
                        w.op("vector", "tensor_copy",
                             tails[:, j * 3:j * 3 + 3],
                             projx2[par][:, 512:515])
                        w.fence("vector", "scalar")
                        w.op("scalar", "activation", xbc32[par], acc32[par],
                             AF.Silu)
                        w.fence("scalar", "gpsimd")
                        if j < 32:
                            w.dma("gpsimd",
                                  xs_scr[jc0:jc0 + 128, c0:c0 + TT],
                                  xbc32[par], "GD")
                        else:
                            w.fence("scalar", "vector")
                            if j == 32:
                                w.op("vector", "tensor_copy",
                                     Bres[:, c0:c0 + TT], xbc32[par])
                            else:
                                w.op("vector", "tensor_copy",
                                     Cres[:, c0:c0 + TT], xbc32[par])
                            w.fence("vector", "gpsimd")
                            w.dma("gpsimd",
                                  sx_in.ap()[0:1, 0:1],
                                  xbc32[par][0:1, 0:1], "GD")
                        gd_after[j] = w.val("GD")
                    else:
                        w.fence("tensor", "vector")
                        d0 = dtx3[0:64, 0:512]
                        w.op("vector", "tensor_scalar", d0,
                             SL[par][0:64, :TT], dtbc[0:64, 0:1], None,
                             op0=ALU.add)
                        sc_after[j] = w.inc_last("vector", "SC")
                        w.fence("vector", "scalar")
                        w.op("scalar", "activation", acc3[0:64, :],
                             d0, AF.Abs)
                        w.fence("scalar", "scalar")
                        w.op("scalar", "activation", acc3[0:64, :],
                             acc3[0:64, :], AF.Exp, scale=-1.0)
                        w.fence("scalar", "vector")
                        w.op("vector", "tensor_scalar", acc3[0:64, :],
                             acc3[0:64, :], 1.0, None, op0=ALU.add)
                        w.fence("vector", "scalar")
                        w.op("scalar", "activation", acc3[0:64, :],
                             acc3[0:64, :], AF.Ln)
                        w.fence("scalar", "vector")
                        w.op("vector", "tensor_scalar", d0, d0, 0.0, None,
                             op0=ALU.max)
                        w.fence("vector", "vector")
                        w.op("vector", "tensor_tensor", d0, d0,
                             acc3[0:64, :], op=ALU.add)
                        w.fence("vector", "tensor")
                        for b in range(4):
                            w.op("tensor", "transpose",
                                 ps_f[:, 1024 + b * 64:1024 + (b + 1) * 64],
                                 d0[:, b * 128:(b + 1) * 128],
                                 idf[0:64, 0:64])
                        w.fence("tensor", "scalar")
                        for b in range(4):
                            w.op("scalar", "copy",
                                 dt2tok[:, (tt * 4 + b) * 64:
                                        (tt * 4 + b + 1) * 64],
                                 ps_f[:, 1024 + b * 64:1024 + (b + 1) * 64])
                        w.fence("scalar", ("tensor", "vector"))
                w.fence(("vector", "scalar"), ("tensor", "gpsimd"))
            w.wait("sync", "GD")
            w.wait("gpsimd", "GD")

        def p4(w):
            ps_cbt = SL[0]
            ps_seg = ps_f[:, 512:1536]
            ps_yd = ps_f[:, 1536:2560]
            ps_yo = ps_f[:, 2560:3584]
            w.dma("sync", acol4[0:64, 0:1], alogd, "LD")
            w.dma("sync", dbc44[:, 0:64], dbc_in, "LD")
            w.wait("scalar", "LD")
            w.op("scalar", "activation", acol4[0:64, 0:1], acol4[0:64, 0:1],
                 AF.Exp)
            w.fence("scalar", "vector")
            w.op("vector", "tensor_scalar", acol4[0:64, 0:1],
                 acol4[0:64, 0:1], -1.0, None, op0=ALU.mult)
            w.op("vector", "memset", S_in[:, :], 0.0)
            w.op("vector", "memset", pc_cur[:, :], 1.0)
            w.fence("vector", ("tensor", "scalar"))
            for ch in range(p4chunks):
                t0 = ch * 256
                w.fence(("tensor", "vector"), "sync")
                for sx in range(2):
                    w.dirty["sync"] = True
                    w.bump("LD", 16)
                    if w.en == "sync":
                        ins = nc.sync.dma_start(
                            out=xT4[sx],
                            in_=xs_scr[:, t0 + sx * 128:t0 + (sx + 1) * 128],
                            transpose=True)
                        ins.then_inc(w.sems["LD"], 16)
                        w.last["sync"] = ins
                w.wait("tensor", "LD")
                w.wait("vector", "LD")
                Bm = Bres[:, t0:t0 + 256]
                Cm = Cres[:, t0:t0 + 256]
                if p4parts < 1:
                    continue
                for sx in range(2):
                    w.op("tensor", "transpose",
                         ps_cbt[0:64, sx * 128:(sx + 1) * 128],
                         dt2tok[:, (2 * ch + sx) * 64:(2 * ch + sx + 1) * 64],
                         idf)
                w.fence("tensor", "vector")
                w.op("vector", "tensor_scalar", dAf4[0:64, :],
                     ps_cbt[0:64, 0:256], acol4[0:64, 0:1], None,
                     op0=ALU.mult)
                w.fence("vector", "vector")
                w.op("vector", "memset", cumf4[0:64, :], 0.0)
                w.fence("vector", "vector")
                w.op("vector", "tensor_tensor_scan", cumf4[0:64, :],
                     dAf4[0:64, :], cumf4[0:64, :], 0.0, op0=ALU.add,
                     op1=ALU.add)
                w.fence("vector", "tensor")
                if p4parts < 2:
                    continue
                for sx in range(2):
                    w.op("tensor", "transpose",
                         ps_cbt[:, 256 + sx * 64:256 + (sx + 1) * 64],
                         cumf4[0:64, sx * 128:(sx + 1) * 128],
                         idf[0:64, 0:64])
                w.op("tensor", "transpose", ps_cbt[0:1, 384:448],
                     cumf4[0:64, 255:256], idf[0:64, 0:64])
                w.fence("tensor", "scalar")
                for sx in range(2):
                    w.op("scalar", "copy", cumT44[sx][:, 0:64],
                         ps_cbt[:, 256 + sx * 64:256 + (sx + 1) * 64])
                w.op("scalar", "copy", rrow4[0:1, 0:64], ps_cbt[0:1, 384:448])
                w.fence("scalar", "tensor")
                if p4parts < 2.5:
                    continue
                w.op("tensor", "matmul", ps_cbt[:, 448:512],
                     onesr_f[0:1, :128], rrow4[0:1, 0:64],
                     start=True, stop=True)
                w.fence("tensor", ("vector", "scalar"))
                w.op("scalar", "activation", rexp4[:, 0:64],
                     ps_cbt[:, 448:512], AF.Exp)
                w.fence("scalar", "vector")
                for sx in range(2):
                    w.op("vector", "tensor_tensor",
                         tseg4[:, sx * 64:sx * 64 + 64],
                         ps_cbt[:, 448:512], cumT44[sx][:, 0:64],
                         op=ALU.subtract)
                w.fence("vector", "scalar")
                for sx in range(2):
                    w.op("scalar", "activation", decf44[sx][:, 0:64],
                         tseg4[:, sx * 64:sx * 64 + 64], AF.Exp)
                    w.op("scalar", "activation", ect44[sx][:, 0:64],
                         cumT44[sx][:, 0:64], AF.Exp)
                w.fence("scalar", "vector")
                if p4parts < 2.8:
                    continue
                for sx in range(2):
                    w.op("vector", "tensor_copy",
                         ect_all[:, (ch * 2 + sx) * 64:(ch * 2 + sx + 1) * 64],
                         ect44[sx][:, 0:64])
                w.op("vector", "tensor_copy", pc_all[:, ch * 64:(ch + 1) * 64],
                     pc_cur[:, 0:64])
                w.fence("vector", "vector")
                w.op("vector", "tensor_tensor", pc_cur[:, 0:64],
                     pc_cur[:, 0:64], rexp4[:, 0:64], op=ALU.mult)
                for sx in range(2):
                    w.op("vector", "tensor_tensor", sfac44[sx][:, 0:64],
                         dt2tok[:, (2 * ch + sx) * 64:(2 * ch + sx + 1) * 64],
                         decf44[sx][:, 0:64], op=ALU.mult)
                w.fence("vector", "vector")
                if p4parts < 3:
                    continue
                for sx in range(2):
                    w.op("tensor", "transpose", ps_b[:, 0:128],
                         Bm[:, sx * 128:(sx + 1) * 128], idb)
                    w.fence("tensor", "vector")
                    w.op("vector", "tensor_copy", BTt[sx], ps_b[:, 0:128])
                    w.fence("vector", "tensor")
                for sx in range(2):
                    w.op("tensor", "matmul", ps_cbt[:, 0:256],
                         Bm[:, sx * 128:(sx + 1) * 128], Cm,
                         start=True, stop=True)
                    w.fence("tensor", "vector")
                    if sx == 0:
                        w.op("vector", "tensor_tensor", CBTm4[0][:, 0:128],
                             ps_cbt[:, 0:128], trimask, op=ALU.mult)
                        w.op("vector", "tensor_copy", CBTm4[0][:, 128:256],
                             ps_cbt[:, 128:256])
                    else:
                        w.op("vector", "memset", CBTm4[1][:, 0:128], 0.0)
                        w.op("vector", "tensor_tensor", CBTm4[1][:, 128:256],
                             ps_cbt[:, 128:256], trimask, op=ALU.mult)
                    w.fence("vector", "tensor")
                if p4parts < 4:
                    continue
                for g in range(4):
                    w.fence("tensor", ("sync", "vector"))
                    w.dma("sync", ehs4[0:64, :],
                          ehseld[:, g * 2048:(g + 1) * 2048], "LD")
                    w.wait("tensor", "LD")
                    hsl = slice(g * 16, (g + 1) * 16)
                    for sx in range(2):
                        dv = dt2tok[:, (2 * ch + sx) * 64 + g * 16:
                                    (2 * ch + sx) * 64 + (g + 1) * 16]
                        sv = sfac44[sx][:, g * 16:(g + 1) * 16]
                        xtv = xT4[sx][:, g * 1024:(g + 1) * 1024].rearrange(
                            "p (h q) -> p h q", h=16)
                        w.op("vector", "tensor_tensor",
                             xdt4[sx].rearrange("p (h q) -> p h q", h=16),
                             xtv, dv[:, :, None].broadcast_to([128, 16, 64]),
                             op=ALU.mult)
                        w.op("vector", "tensor_tensor",
                             xdec4[sx].rearrange("p (h q) -> p h q", h=16),
                             xtv, sv[:, :, None].broadcast_to([128, 16, 64]),
                             op=ALU.mult)
                    w.fence("vector", "tensor")
                    for sx in range(2):
                        for gq in range(4):
                            for hh2 in range(4):
                                h = (g * 16 + gq * 4 + hh2)
                                w.op("tensor", "matmul",
                                     ps_seg[:, hh2 * 256:(hh2 + 1) * 256],
                                     ehs4[0:64, (gq * 4 + hh2) * 128:
                                          (gq * 4 + hh2 + 1) * 128],
                                     cumf4[0:64, :], start=True, stop=True)
                            w.fence("tensor", "vector")
                            cv = cumT44[sx][:, g * 16 + gq * 4:
                                            g * 16 + (gq + 1) * 4, None] \
                                .broadcast_to([128, 4, 256])
                            w.op("vector", "tensor_tensor",
                                 tseg4[:, 0:1024].rearrange(
                                     "p (h t) -> p h t", h=4),
                                 ps_seg[:, 0:1024].rearrange(
                                     "p (h t) -> p h t", h=4),
                                 cv, op=ALU.subtract)
                            w.fence("vector", "vector")
                            w.op("vector", "tensor_scalar", tseg4[:, 0:1024],
                                 tseg4[:, 0:1024], 0.0, None, op0=ALU.min)
                            w.fence("vector", ("scalar", "tensor"))
                            w.op("scalar", "activation", tseg2b[:, 0:1024],
                                 tseg4[:, 0:1024], AF.Exp)
                            w.fence("scalar", "vector")
                            cbv = CBTm4[sx][:, None, :].broadcast_to(
                                [128, 4, 256])
                            w.op("vector", "tensor_tensor",
                                 Pg4[sx][:, gq * 1024:(gq + 1) * 1024]
                                 .rearrange("p (h t) -> p h t", h=4),
                                 tseg2b[:, 0:1024].rearrange(
                                     "p (h t) -> p h t", h=4),
                                 cbv, op=ALU.mult)
                            w.fence("vector", ("tensor", "scalar"))
                    if p4parts < 5:
                        continue
                    if dbg and ch == 0 and g == 0:
                        w.fence(("vector", "scalar", "tensor"), "gpsimd")
                        w.dma("gpsimd", pdbg[3], Pg4[0], "GD")
                        w.dma("gpsimd", pdbg[4], Pg4[1], "GD")
                        w.dma("gpsimd", pdbg[5][:, 0:1024], xdt4[0], "GD")
                        w.dma("gpsimd", pdbg[5][:, 1024:2048], xdt4[1], "GD")
                        w.dma("gpsimd", pdbg[5][:, 2048:2304],
                              CBTm4[0][:, :], "GD")
                        w.dma("gpsimd", pdbg[5][:, 2304:2560],
                              CBTm4[1][:, :], "GD")
                        w.wait("tensor", "GD")
                        w.wait("vector", "GD")
                        w.wait("scalar", "GD")
                    w.op("vector", "tensor_copy", S_bf4,
                         S_in[:, g * 1024:(g + 1) * 1024])
                    w.fence("vector", "tensor")
                    for tq in range(2):
                        w.op("tensor", "matmul", ps_yo[:, 0:512],
                             Cm[:, tq * 128:(tq + 1) * 128], S_bf4[:, 0:512],
                             start=True, stop=True)
                        w.op("tensor", "matmul", ps_yo[:, 512:1024],
                             Cm[:, tq * 128:(tq + 1) * 128],
                             S_bf4[:, 512:1024], start=True, stop=True)
                        for hl in range(16):
                            for sx in range(2):
                                w.op("tensor", "matmul",
                                     ps_yd[:, hl * 64:(hl + 1) * 64],
                                     Pg4[sx][:, hl * 256 + tq * 128:
                                             hl * 256 + tq * 128 + 128],
                                     xdt4[sx][:, hl * 64:(hl + 1) * 64],
                                     start=(sx == 0), stop=(sx == 1))
                        w.fence("tensor", "vector")
                        w.wait("vector", "GD")
                        ev = ect44[tq][:, g * 16:(g + 1) * 16, None] \
                            .broadcast_to([128, 16, 64])
                        w.op("vector", "tensor_tensor",
                             yst4.rearrange("p (h q) -> p h q", h=16),
                             ps_yo[:, :].rearrange("p (h q) -> p h q", h=16),
                             ev, op=ALU.mult)
                        w.fence("vector", "vector")
                        w.op("vector", "tensor_tensor", yst4, yst4,
                             ps_yd[:, :], op=ALU.add)
                        w.fence("vector", "vector")
                        dvv = dbc44[:, g * 16:(g + 1) * 16, None] \
                            .broadcast_to([128, 16, 64])
                        w.op("vector", "tensor_tensor",
                             tseg4[:, 0:1024].rearrange(
                                 "p (h q) -> p h q", h=16),
                             xT4[tq][:, g * 1024:(g + 1) * 1024].rearrange(
                                 "p (h q) -> p h q", h=16),
                             dvv, op=ALU.mult)
                        w.fence("vector", "vector")
                        w.op("vector", "tensor_tensor", yst4, yst4,
                             tseg4[:, 0:1024], op=ALU.add)
                        w.fence("vector", ("gpsimd", "tensor"))
                        w.dma("gpsimd",
                              y_scr[t0 + tq * 128:t0 + (tq + 1) * 128,
                                    g * 1024:(g + 1) * 1024], yst4, "GD")
                    if p4parts < 6:
                        continue
                    for sx in range(2):
                        w.op("tensor", "matmul", ps_yd[:, 0:512], BTt[sx],
                             xdec4[sx][:, 0:512], start=(sx == 0),
                             stop=(sx == 1))
                    for sx in range(2):
                        w.op("tensor", "matmul", ps_yd[:, 512:1024], BTt[sx],
                             xdec4[sx][:, 512:1024], start=(sx == 0),
                             stop=(sx == 1))
                    w.fence("tensor", "vector")
                    rv = rexp4[:, g * 16:(g + 1) * 16, None].broadcast_to(
                        [128, 16, 64])
                    sl4 = S_in[:, g * 1024:(g + 1) * 1024]
                    w.op("vector", "tensor_tensor",
                         sl4.rearrange("p (h q) -> p h q", h=16),
                         sl4.rearrange("p (h q) -> p h q", h=16), rv,
                         op=ALU.mult)
                    w.fence("vector", "vector")
                    w.op("vector", "tensor_tensor", sl4, sl4, ps_yd[:, :],
                         op=ALU.add)
                    w.fence("vector", ("tensor", "gpsimd"))
            w.wait("sync", "GD")
            w.wait("gpsimd", "GD")

        def p5(w):
            w.dma("sync", mv1s[0:128, 0:3], mv1d, "LD")
            w.dma("sync", mv2s[0:128, 0:3], mv2d, "LD")
            w.fence("vector", "sync")
            w.dma("sync", sx_in.ap()[:, 0:4096], S_in[:, :], "ST")
            w.dma("sync", sx_in.ap()[:, 4096:4160], pc_cur[:, 0:64], "ST")
            w.wait("gpsimd", "ST")
            cc = w.bump("CC", 1)
            w.dirty["gpsimd"] = True
            if w.en == "gpsimd":
                ins = nc.gpsimd.collective_compute(
                    "AllGather", mybir.AluOpType.bypass,
                    replica_groups=[[0, 1, 2, 3], [4, 5, 6, 7]],
                    ins=[sx_in.ap().opt()], outs=[sx_out.ap().opt()])
                ins.then_inc(w.sems["CC"], 1)
                w.last["gpsimd"] = ins
            w.wait("sync", "CC", cc)
            w.wait("vector", "CC", cc)
            w.op("vector", "memset", S_in[:, :], 0.0)
            w.fence("vector", "vector")
            for p in range(3):
                w.fence("vector", "sync")
                w.dma("sync", xtmp5[:, :],
                      sx_out.ap()[p * 128:(p + 1) * 128, 0:4096], "LD")
                w.dma("sync", lam5[:, 0:64],
                      sx_out.ap()[p * 128:(p + 1) * 128, 4096:4160], "LD")
                w.wait("vector", "LD")
                w.op("vector", "tensor_scalar", lam5[:, 0:64], lam5[:, 0:64],
                     mv1s[:, p:p + 1], mv2s[:, p:p + 1], op0=ALU.mult,
                     op1=ALU.add)
                w.op("vector", "tensor_scalar", xtmp5[:, :], xtmp5[:, :],
                     mv1s[:, p:p + 1], None, op0=ALU.mult)
                w.fence("vector", "vector")
                w.op("vector", "tensor_tensor",
                     S_in[:, :].rearrange("p (h q) -> p h q", h=64),
                     S_in[:, :].rearrange("p (h q) -> p h q", h=64),
                     lam5[:, 0:64, None].broadcast_to([128, 64, 64]),
                     op=ALU.mult)
                w.fence("vector", "vector")
                w.op("vector", "tensor_tensor", S_in[:, :], S_in[:, :],
                     xtmp5[:, :], op=ALU.add)
                w.fence("vector", "vector")

        def p6(w):
            ps_c = ps_f[:, 512:1536]
            for ch in range(8):
                t0 = ch * 256
                w.fence(("vector", "tensor"), "vector")
                w.op("vector", "tensor_tensor",
                     sc6.rearrange("p (h q) -> p h q", h=64),
                     S_in[:, :].rearrange("p (h q) -> p h q", h=64),
                     pc_all[:, ch * 64:(ch + 1) * 64][:, :, None]
                     .broadcast_to([128, 64, 64]), op=ALU.mult)
                w.fence("vector", "tensor")
                for tq in range(2):
                    for g in range(4):
                        w.op("tensor", "matmul", ps_c[:, 0:512],
                             Cres[:, t0 + tq * 128:t0 + (tq + 1) * 128],
                             sc6[:, g * 1024:g * 1024 + 512],
                             start=True, stop=True)
                        w.op("tensor", "matmul", ps_c[:, 512:1024],
                             Cres[:, t0 + tq * 128:t0 + (tq + 1) * 128],
                             sc6[:, g * 1024 + 512:(g + 1) * 1024],
                             start=True, stop=True)
                        w.fence("tensor", ("vector", "sync"))
                        w.wait("sync", "GD")
                        w.dma("sync", yrmw,
                              y_scr[t0 + tq * 128:t0 + (tq + 1) * 128,
                                    g * 1024:(g + 1) * 1024], "LD")
                        w.wait("vector", "LD")
                        ev = ect_all[:, (ch * 2 + tq) * 64 + g * 16:
                                     (ch * 2 + tq) * 64 + (g + 1) * 16] \
                            [:, :, None].broadcast_to([128, 16, 64])
                        w.op("vector", "tensor_tensor",
                             corr6[:, :].rearrange("p (h q) -> p h q", h=16),
                             ps_c[:, 0:1024].rearrange(
                                 "p (h q) -> p h q", h=16),
                             ev, op=ALU.mult)
                        w.fence("vector", "vector")
                        w.op("vector", "tensor_tensor", yrmw, yrmw,
                             corr6[:, :], op=ALU.add)
                        w.fence("vector", ("gpsimd", "tensor"))
                        w.dma("gpsimd",
                              y_scr[t0 + tq * 128:t0 + (tq + 1) * 128,
                                    g * 1024:(g + 1) * 1024], yrmw, "GD")
            w.wait("sync", "GD")
            w.wait("gpsimd", "GD")

        def p7(w):
            for e in ("sync", "scalar", "tensor", "gpsimd"):
                w.wait(e, "PC", pc_total)
            sg7x = [sg7, sg7b]
            yb7x = [yb7, yb7b]
            w.dma("sync", nw7[:, 0:32], nwr, "LD")
            w.wait("vector", "LD")
            for tt in range(4):
                c0 = tt * TT
                srcv = [resid[c][:, c0:c0 + TT] for c in range(16)]
                hsrc = [hb7[c] for c in range(16)]
                ln(srcv, hsrc, TT)
                for j in range(32):
                    w.fence("tensor", "gpsimd")
                    w.dma("gpsimd",
                          c_wbuf[j % 2].rearrange("p (c j) -> p c j", c=16),
                          gpT_b[:, j * 128:(j + 1) * 128].rearrange(
                              "(c p) j -> p c j", p=128), "WT")
                    w.fence("vector", "sync")
                    w.dma("sync", yb7x[j % 2],
                          y_scr[c0:c0 + TT, j * 128:(j + 1) * 128],
                          "LD", transpose=True)
                    w.wait("tensor", "WT")
                    for kc in range(16):
                        w.op("tensor", "matmul", SL[j % 2],
                             c_wbuf[j % 2][:, kc * 128:(kc + 1) * 128],
                             hsrc[kc], start=(kc == 0), stop=(kc == 15))
                    w.fence("tensor", "scalar")
                    w.op("scalar", "activation", sg7, SL[j % 2], AF.Silu)
                    w.fence("scalar", ("vector", "tensor"))
                    w.wait("vector", "LD")
                    w.op("vector", "tensor_tensor", hg7[j], yb7x[j % 2], sg7,
                         op=ALU.mult)
                    w.fence("vector", ("tensor", "gpsimd", "sync"))
                for j in range(32):
                    sq = sqbuf[:, (j % 2) * 512:(j % 2) * 512 + TT]
                    if j >= 2:
                        w.fence("tensor", "scalar")
                    w.op("scalar", "activation", sq, hg7[j], AF.Square)
                    w.fence("scalar", "tensor")
                    w.op("tensor", "matmul", env["ps_st"][0:1, :TT],
                         ones_b[:, 0:1], sq, start=(j == 0), stop=(j == 31))
                w.fence("tensor", "vector")
                m = rows[0:1, 0:TT]
                w.op("vector", "tensor_scalar", m, env["ps_st"][0:1, :TT],
                     1.0 / 4096, None, op0=ALU.mult)
                w.fence("vector", "vector")
                w.op("vector", "tensor_scalar", m, m, 1e-5, None, op0=ALU.add)
                w.fence("vector", "scalar")
                w.op("scalar", "activation", m, m, AF.Sqrt)
                w.fence("scalar", "vector")
                w.op("vector", "reciprocal", m, m)
                w.fence("vector", "tensor")
                w.op("tensor", "matmul", SL[4][:, :TT], onesr_f[0:1, :128], m,
                     start=True, stop=True)
                w.fence("tensor", "scalar")
                w.op("scalar", "copy", bcR7, SL[4][:, :TT])
                w.fence("scalar", "vector")
                for j in range(32):
                    w.op("vector", "tensor_tensor", hg7[j], hg7[j], bcR7,
                         op=ALU.mult)
                    w.fence("vector", "vector")
                    w.op("vector", "tensor_scalar", hg7[j], hg7[j],
                         nw7[:, j:j + 1], None, op0=ALU.mult)
                    w.fence("vector", "vector")
                w.fence("vector", "tensor")
                def o7_load(j2):
                    w.dma("gpsimd",
                          c_wbuf2[j2 % 2].rearrange("p (c j) -> p c j", c=32),
                          opT_b[:, j2 * 128:(j2 + 1) * 128].rearrange(
                              "(c p) j -> p c j", p=128), "WT")

                def o7_mm(j2):
                    for kc in range(32):
                        w.op("tensor", "matmul", SL[j2 % 2],
                             c_wbuf2[j2 % 2][:, kc * 128:(kc + 1) * 128],
                             hg7[kc], start=(kc == 0), stop=(kc == 31))

                def o7_evac(j2):
                    w.op("scalar", "copy", o27[j2], SL[j2 % 2])

                w.pipe(16, o7_load, o7_mm, o7_evac, "scalar",
                       dma_eng=("gpsimd",))
                w.fence(("scalar", "tensor"), ("tensor", "vector", "scalar"))
                ln(o27, o27, TT)
                for c in range(16):
                    rv7 = resid[c][:, c0:c0 + TT]
                    w.op("vector", "tensor_tensor", rv7, rv7, o27[c],
                         op=ALU.add)
                w.fence("vector", ("tensor", "scalar"))

        def p8(w):
            for e in ("sync", "scalar", "tensor", "gpsimd"):
                w.wait(e, "PC", pc_total)
            ps_up = SL[0]
            ps_gt = SL[1]
            ps_h8 = SL[2]
            w.dma("sync", suugf[:, 0:16], suugd, "LD")
            w.dma("sync", sudpf[:, 0:64], sudpd, "LD")
            w.dma("sync", wsupf[:, 0:64], wsupd, "LD")
            w.dma("sync", wsgtf[:, 0:64], wsgtd, "LD")
            w.dma("sync", wsdpf[:, 0:16], wsdpd, "LD")
            w.wait("vector", "LD")
            w.wait("scalar", "LD")
            for tt in range(4):
                c0 = tt * TT
                srcv = [resid[c][:, c0:c0 + TT] for c in range(16)]
                ln(srcv, src8, TT)
                za, zb = src8, o28
                for c in range(16):
                    w.op("vector", "tensor_scalar", za[c], za[c],
                         suugf[:, c:c + 1], S2048, op0=ALU.mult, op1=ALU.mult)
                w.fence("vector", "tensor")
                def zh_mm(c):
                    w.op("tensor", "matmul", SL[2 + (c % 2)], h128b, za[c],
                         start=True, stop=True)

                def zh_evac(c):
                    w.op("scalar", "copy", zb[c], SL[2 + (c % 2)])

                w.pipe(16, None, zh_mm, zh_evac, "scalar")
                w.fence(("scalar", "tensor"), ("tensor", "scalar", "vector"))
                for b in range(4):
                    step = 1 << b
                    for i in range(16):
                        if i & step:
                            continue
                        j = i | step
                        w.op("vector", "tensor_tensor", za[i], zb[i], zb[j],
                             op=ALU.add)
                        w.op("vector", "tensor_tensor", za[j], zb[i], zb[j],
                             op=ALU.subtract)
                    w.fence("vector", "vector")
                    za, zb = zb, za
                z = zb
                w.fence("vector", ("tensor", "scalar"))
                def ug_load(j):
                    par = j % 2
                    w.dma("sync",
                          f_wbufA[par].rearrange("p (c j) -> p c j", c=16),
                          WupT_b[:, j * 128:(j + 1) * 128].rearrange(
                              "(c p) j -> p c j", p=128), "LD")
                    w.dma("scalar",
                          f_wbufG[par].rearrange("p (c j) -> p c j", c=16),
                          WgtT_b[:, j * 128:(j + 1) * 128].rearrange(
                              "(c p) j -> p c j", p=128), "LS")

                def ug_mm(j):
                    par = j % 2
                    sl_u = SL[par]
                    sl_g = SL[2 + par]
                    for kc in range(16):
                        w.op("tensor", "matmul", sl_u,
                             f_wbufA[par][:, kc * 128:(kc + 1) * 128], z[kc],
                             start=(kc == 0), stop=(kc == 15))
                    for kc in range(16):
                        w.op("tensor", "matmul", sl_g,
                             f_wbufG[par][:, kc * 128:(kc + 1) * 128], z[kc],
                             start=(kc == 0), stop=(kc == 15))

                def ug_evac(j):
                    par = j % 2
                    sgv = sgb8x[par]
                    w.op("scalar", "activation", sgv, SL[2 + par], AF.Silu,
                         scale=wsgtf[:, j:j + 1])
                    w.fence("scalar", "vector")
                    uv = ub8[:, j * TT:(j + 1) * TT]
                    w.op("vector", "scalar_tensor_tensor", uv, SL[par],
                         wsupf[:, j:j + 1], sgv, op0=ALU.mult, op1=ALU.mult)
                    w.fence("vector", "vector")
                    w.op("vector", "tensor_scalar", uv, uv,
                         sudpf[:, j:j + 1], S8192, op0=ALU.mult, op1=ALU.mult)

                sgb8x = [sgb8, sgb8b]
                if p8f & 1:
                    w.pipe(64, ug_load, ug_mm, ug_evac, "vector", npsum=2,
                        dma_eng=("sync", "scalar"))
                else:
                    for j in range(64):
                        w.fence("tensor", "gpsimd")
                        ug_load(j)
                        w.wait("tensor", "WT")
                        ug_mm(j)
                        w.fence("tensor", ("scalar", "vector"))
                        ug_evac(j)
                        w.fence("vector", ("tensor", "scalar"))
                w.fence(("vector", "tensor", "scalar"),
                        ("tensor", "scalar", "vector"))
                def h2_mm(c):
                    uv = ub8[:, c * TT:(c + 1) * TT]
                    w.op("tensor", "matmul", SL[c % 2], h128b, uv,
                         start=True, stop=True)

                def h2_evac(c):
                    uv = ub8[:, c * TT:(c + 1) * TT]
                    w.op("scalar", "copy", uv, SL[c % 2])

                if dbg and tt == 0:
                    w.fence(("vector", "scalar", "tensor"), "gpsimd")
                    w.dma("gpsimd", u8dbg.rearrange("(a p) t -> p a t", p=128),
                          ub8.rearrange("p (a t) -> p a t", a=64), "GD")
                    w.wait("tensor", "GD")
                    w.wait("scalar", "GD")
                    w.wait("vector", "GD")
                if p8f & 2:
                    w.pipe(64, None, h2_mm, h2_evac, "scalar")
                else:
                    for c in range(64):
                        h2_mm(c)
                        w.fence("tensor", "scalar")
                        h2_evac(c)
                        w.fence("scalar", ("tensor", "vector"))
                w.fence(("scalar", "tensor"), ("tensor", "vector", "scalar"))
                import math
                for b in range(6):
                    step = 1 << b
                    nhi = 64 // (2 * step)
                    va = ub8[:, :].rearrange(
                        "p (hi two lo t) -> p hi two lo t", hi=nhi, two=2,
                        lo=step)
                    a0 = va[:, :, 0, :, :]
                    a1 = va[:, :, 1, :, :]
                    sv = uscr8[:, :].rearrange(
                        "p (hi lo t) -> p hi lo t", hi=nhi, lo=step)
                    w.op("vector", "tensor_tensor", sv, a0, a1,
                         op=ALU.subtract)
                    w.fence("vector", "vector")
                    w.op("vector", "tensor_tensor", a0, a0, a1, op=ALU.add)
                    w.fence("vector", "vector")
                    w.op("vector", "tensor_copy", a1, sv)
                    w.fence("vector", "vector")
                w.fence("vector", "tensor")
                def dn_load(it):
                    j2, hf = it // 2, it % 2
                    w.dma("sync",
                          f_wbuf2[it % 2].rearrange("p (c j) -> p c j", c=32),
                          WdnT_b[hf * 4096:(hf + 1) * 4096,
                                 j2 * 128:(j2 + 1) * 128].rearrange(
                              "(c p) j -> p c j", p=128), "LD")

                def dn_mm(it):
                    j2, hf = it // 2, it % 2
                    sl = SL[(j2 % 2)]
                    for kc in range(32):
                        w.op("tensor", "matmul", sl,
                             f_wbuf2[it % 2][:, kc * 128:(kc + 1) * 128],
                             ub8[:, (hf * 32 + kc) * TT:
                                 (hf * 32 + kc + 1) * TT],
                             start=(kc == 0 and hf == 0),
                             stop=(kc == 31 and hf == 1))

                def dn_evac(it):
                    j2, hf = it // 2, it % 2
                    if hf == 1:
                        w.op("vector", "tensor_scalar", o28[j2],
                             SL[(j2 % 2)], wsdpf[:, j2:j2 + 1], None,
                             op0=ALU.mult)
                    else:
                        w.op("vector", "tensor_copy", rows[0:1, 0:1],
                             rows[0:1, 1:2])

                if p8f & 4:
                    w.pipe(32, dn_load, dn_mm, dn_evac, "vector", npsum=3,
                        dma_eng=("sync",))
                else:
                    for it in range(32):
                        w.fence("tensor", "gpsimd")
                        dn_load(it)
                        w.wait("tensor", "WT")
                        dn_mm(it)
                        w.fence("tensor", "vector")
                        dn_evac(it)
                        w.fence("vector", ("tensor", "scalar"))
                w.fence(("vector", "tensor"), ("tensor", "scalar", "vector"))
                ln(o28, o28, TT)
                for c in range(16):
                    w.wait("vector", "ST")
                    w.op("vector", "tensor_tensor", outst8,
                         resid[c][:, c0:c0 + TT], o28[c], op=ALU.add)
                    w.fence("vector", "sync")
                    w.dma("sync", outT[c * 128:(c + 1) * 128, c0:c0 + TT],
                          outst8, "ST")
        p1(w)
        phase_barrier()
        if upto >= 2:
            p2(w)
            phase_barrier()
        if upto >= 3:
            p3(w)
            phase_barrier()
        if upto >= 4:
            p4(w)
            phase_barrier()
        if upto >= 5:
            p5(w)
            phase_barrier()
        if upto >= 6:
            p6(w)
            phase_barrier()
        if upto >= 7:
            p7(w)
            phase_barrier()
        if upto >= 8:
            p8(w)
            phase_barrier()
        if dbg:
            for e in ("sync", "gpsimd"):
                for sem in ("LD", "ST", "WT", "GD"):
                    w.wait(e, sem)
            for r0 in range(16):
                w.dma("gpsimd", ydbg[r0 * 128:(r0 + 1) * 128, :],
                      y_scr[r0 * 128:(r0 + 1) * 128, :], "GD")
            for r0 in range(32):
                w.dma("gpsimd", xsdbg[r0 * 128:(r0 + 1) * 128, :],
                      xs_scr[r0 * 128:(r0 + 1) * 128, :], "GD")
            if upto in (4, 5, 6):
                w.dma("gpsimd", pdbg[6], xT4[0], "GD")
                w.dma("gpsimd", pdbg[7][:, 0:64], cumT44[0][:, 0:64], "GD")
                w.dma("gpsimd", pdbg[7][:, 64:1088],
                      Pg4[0][:, 0:1024], "GD")
                w.dma("gpsimd", pdbg[7][:, 1088:2112], S_in[:, 0:1024], "GD")
        if upto < 8:
            w.fence(("vector", "tensor", "scalar"), "gpsimd")
            for c in range(16):
                w.dma("gpsimd", outT[c * 128:(c + 1) * 128, :], resid[c],
                      "ST")
        w.wait("sync", "ST")
        w.wait("sync", "LD")
        w.wait("sync", "WT")
        w.wait("sync", "GD")
        w.wait("gpsimd", "ST")

    run_walks(nc, st, emit)
    st.close()
    return nc


def fused_inputs(inputs):
    x = np.asarray(inputs["L0_attn_x"])
    perm = rope_perm()
    permQ = np.concatenate([h * 128 + perm for h in range(16)])
    permK = np.concatenate([h * 128 + perm for h in range(4)])
    wqP = np.ascontiguousarray(np.asarray(inputs["L0_attn_wq"])[:, permQ])
    wkP = np.ascontiguousarray(np.asarray(inputs["L0_attn_wk"])[:, permK])
    wvv = np.ascontiguousarray(np.asarray(inputs["L0_attn_wv"]))
    wov = np.ascontiguousarray(np.asarray(inputs["L0_attn_wo"]))
    cs = cs_table()
    ident = np.eye(128)
    U_T = np.ascontiguousarray(np.asarray(inputs["L0_ffn_U"]).transpose(0, 2, 1))
    V_T = np.ascontiguousarray(np.asarray(inputs["L0_ffn_V"]).transpose(0, 2, 1))
    VP = np.stack([np.concatenate([V_T[2 * m], V_T[2 * m + 1]], axis=1)
                   for m in range(4)]).astype(np.float32)
    UP = np.stack([np.concatenate([U_T[2 * m], U_T[2 * m + 1]], axis=0)
                   for m in range(4)]).astype(np.float32)
    b1r_ = np.ascontiguousarray(
        np.asarray(inputs["L0_ffn_b1"]).reshape(16, 128).T)
    b2c_ = np.asarray(inputs["L0_ffn_W2"])[0:0]
    b2c_ = np.asarray(inputs["L0_ffn_b2"]).reshape(8, 1).astype(np.float32)
    SEL = np.zeros((8, 512), dtype=np.float32)
    for m in range(4):
        SEL[2 * m, m * 128:m * 128 + 64] = 1.0
        SEL[2 * m + 1, m * 128 + 64:(m + 1) * 128] = 1.0
    ipw = np.asarray(inputs["L1_attn_in_proj_w"])
    wpT_ = np.ascontiguousarray(ipw.T[:, 4096:8512])
    gpT_ = np.ascontiguousarray(ipw.T[:, 0:4096])
    cw = np.asarray(inputs["L1_attn_conv_w"])
    cwr_ = np.ascontiguousarray(
        cw.reshape(34, 128, 4).transpose(1, 0, 2).reshape(128, 136))
    cbr_ = np.ascontiguousarray(
        np.asarray(inputs["L1_attn_conv_b"]).reshape(34, 128).T)
    dtb_ = np.asarray(inputs["L1_attn_dt_bias"]).reshape(64, 1).astype(np.float32)
    alog_ = np.asarray(inputs["L1_attn_A_log"]).reshape(64, 1).astype(np.float32)
    dbc_ = np.tile(np.asarray(inputs["L1_attn_D"]).astype(np.float32), (128, 1))
    ehsel_ = np.zeros((64, 8192), dtype=np.float32)
    for h in range(64):
        ehsel_[h % 16 if False else h, h * 128:(h + 1) * 128] = 1.0
    ehsel_ = np.zeros((64, 8192), dtype=np.float32)
    for g in range(4):
        for l in range(16):
            ehsel_[g * 16 + l, g * 2048 + l * 128:g * 2048 + (l + 1) * 128] = 1.0
    tri = np.tril(np.ones((128, 128), dtype=np.float32)).T
    opT_ = np.ascontiguousarray(np.asarray(inputs["L1_attn_out_proj_w"]).T)
    nwr_ = np.ascontiguousarray(
        np.asarray(inputs["L1_attn_norm_w"]).reshape(32, 128).T)
    WupT_ = np.ascontiguousarray(np.asarray(inputs["L1_ffn_W_up"]).T)
    WgtT_ = np.ascontiguousarray(np.asarray(inputs["L1_ffn_W_gate"]).T)
    WdnT_ = np.ascontiguousarray(np.asarray(inputs["L1_ffn_W_down"]).T)
    pack = lambda v, n: np.ascontiguousarray(
        np.asarray(v).reshape(n, 128).T.astype(np.float32))
    wsu = np.asarray(inputs["L1_ffn_Wscale_ug"])
    Hm = np.array([[1.0]])
    while Hm.shape[0] < 128:
        Hm = np.block([[Hm, Hm], [Hm, -Hm]])
    ins = []
    for c in range(NCORES):
        b, q = c // 4, c % 4
        tok0 = q * NTOK
        lo = max(0, tok0 - 640)
        xb = np.zeros((D, BAND2), dtype=np.float32)
        xb[:, 640 - (tok0 - lo):] = x[b, lo:tok0 + NTOK].T
        csb = np.zeros((128, BAND2), dtype=np.float32)
        csb[:, 640 - (tok0 - lo):] = cs[:, lo:tok0 + NTOK]
        csb[0:64, :640 - (tok0 - lo)] = 1.0
        loq = max(0, tok0 - 128)
        csq = np.zeros((128, 2176), dtype=np.float32)
        csq[:, 128 - (tok0 - loq):] = cs[:, loq:tok0 + NTOK]
        csq[0:64, :128 - (tok0 - loq)] = 1.0
        sw = lambda a: np.ascontiguousarray(
            np.concatenate([a[64:128], a[0:64]], axis=0))
        M = np.empty((17, 128, 896), dtype=np.float32)
        ii = np.arange(128)
        for Qg in range(17):
            iabs = tok0 - 128 + Qg * 128 + ii[:, None]
            for s7 in range(7):
                if s7 < 5:
                    jabs = tok0 - 640 + (Qg + s7) * 128 + ii[None, :]
                else:
                    jabs = (s7 - 5) * 128 + ii[None, :]
                allowed = ((jabs <= iabs) & (jabs > iabs - 512)) | \
                          ((jabs < 256) & (jabs <= iabs - 512))
                allowed &= (jabs >= 0)
                if s7 >= 5:
                    fs0 = (s7 - 5) * 128
                    bandlo = tok0 - 640 + Qg * 128
                    bandhi = bandlo + 640
                    if bandlo <= fs0 and bandhi >= fs0 + 128:
                        allowed = np.zeros_like(allowed)
                M[Qg, :, s7 * 128:(s7 + 1) * 128] = np.where(
                    allowed, 0.0, -1e30)
        mv1_ = np.zeros((128, 3), dtype=np.float32)
        mv2_ = np.ones((128, 3), dtype=np.float32)
        for p in range(3):
            if p < q:
                mv1_[:, p] = 1.0
                mv2_[:, p] = 0.0
        ins.append({
            "x_band": np.ascontiguousarray(xb),
            "x_front": np.ascontiguousarray(x[b, 0:FRONT].T),
            "wq": wqP, "wk": wkP, "wv": wvv, "wo": wov,
            "cs_band": np.ascontiguousarray(csb), "cs2_band": sw(csb),
            "cs_front": np.ascontiguousarray(cs[:, 0:FRONT]),
            "cs2_front": sw(np.ascontiguousarray(cs[:, 0:FRONT])),
            "cs_q": np.ascontiguousarray(csq), "cs2_q": sw(csq),
            "masks": M.astype(BF),
            "identb": ident.astype(BF), "identf": ident.astype(np.float32),
            "W1": np.ascontiguousarray(np.asarray(inputs["L0_ffn_W1"])),
            "b1r": b1r_, "W2": np.ascontiguousarray(
                np.asarray(inputs["L0_ffn_W2"])), "b2c": b2c_,
            "VP": VP, "UP": UP, "SEL": SEL,
            "wpT": wpT_, "cwr": cwr_, "cbr": cbr_, "dtb": dtb_,
            "hmask": np.full((128, 1), 0.0 if q == 0 else 1.0, np.float32),
            "alog": alog_, "dbc_in": dbc_, "ehsel": ehsel_,
            "trim": tri.astype(BF), "mv1": mv1_, "mv2": mv2_,
            "gpT": gpT_, "opT": opT_, "nwr": nwr_,
            "WupT": WupT_, "WgtT": WgtT_, "WdnT": WdnT_,
            "suug": pack(inputs["L1_ffn_SU_ug"], 16),
            "sudp": pack(inputs["L1_ffn_SU_dp"], 64),
            "wsup": pack(wsu[:8192], 64), "wsgt": pack(wsu[8192:], 64),
            "wsdp": pack(inputs["L1_ffn_Wscale_dp"], 16),
            "h128": Hm.astype(BF),
        })
    return ins


def kernel(**inputs):
    nc = build_fused()
    ins = fused_inputs(inputs)
    res = run_bass_kernel_spmd(nc, ins, core_ids=list(range(NCORES)))
    out = np.empty((2, 8192, 2048), dtype=np.float32)
    for c in range(NCORES):
        b, q = c // 4, c % 4
        out[b, q * 2048:(q + 1) * 2048] = res.results[c]["outT"].T
    return out


# version 06
# speedup vs baseline: 1.2172x; 1.0155x over previous
import contextlib
import numpy as np
import ml_dtypes
import concourse.bass as bass
import concourse.mybir as mybir
from concourse.bass_utils import run_bass_kernel_spmd

F32 = mybir.dt.float32
BF16 = mybir.dt.bfloat16
DRW = mybir.MatmulPerfMode.DoubleRow
AF = mybir.ActivationFunctionType
ALU = mybir.AluOpType
AX = mybir.AxisListType

NCORES = 8
D = 2048
NTOK = 2048
BAND = 2560
FRONT = 256
KVLEN = BAND + FRONT

BF = ml_dtypes.bfloat16

ENGS = ("sync", "tensor", "vector", "scalar", "gpsimd")


class Walk:

    def __init__(self, nc, sems, en):
        self.nc = nc
        self.sems = sems
        self.en = en
        self.c = {}
        self.last = {}
        self.dirty = {}
        self.lastinc = {}
        self.ftarget = {}

    def bump(self, name, n=1):
        self.c[name] = self.c.get(name, 0) + n
        return self.c[name]

    def val(self, name):
        return self.c.get(name, 0)

    def op(self, eng, method, *a, **k):
        self.dirty[eng] = True
        self.lastinc[eng] = None
        if self.en == eng:
            ins = getattr(getattr(self.nc, eng), method)(*a, **k)
            self.last[eng] = ins
            return ins
        return None

    def inc_last(self, eng, sem):
        assert self.lastinc.get(eng) is None
        v = self.bump(sem)
        self.lastinc[eng] = (sem, v)
        if self.en == eng:
            self.last[eng].then_inc(self.sems[sem], 1)
        return v

    def dma(self, eng, out, in_, sem, transpose=False):
        self.dirty[eng] = True
        v = self.bump(sem, 16)
        self.lastinc[eng] = (sem, v)
        if self.en == eng:
            kw = {"transpose": True} if transpose else {}
            ins = getattr(self.nc, eng).dma_start(out=out, in_=in_, **kw)
            ins.then_inc(self.sems[sem], 16)
            self.last[eng] = ins
        return v

    def wait(self, eng, sem, v=None):
        if v is None:
            v = self.val(sem)
        if self.en == eng:
            getattr(self.nc, eng).wait_ge(self.sems[sem], v)

    def fence(self, prods, conss):
        if isinstance(prods, str):
            prods = (prods,)
        if isinstance(conss, str):
            conss = (conss,)
        for p in prods:
            if self.dirty.get(p):
                li = self.lastinc.get(p)
                if li is not None:
                    self.ftarget[p] = li
                else:
                    v = self.bump("F_" + p)
                    if self.en == p:
                        self.last[p].then_inc(self.sems["F_" + p], 1)
                    self.lastinc[p] = ("F_" + p, v)
                    self.ftarget[p] = ("F_" + p, v)
                self.dirty[p] = False
        for ce in conss:
            for p in prods:
                t = self.ftarget.get(p)
                if t is not None and self.en == ce:
                    getattr(self.nc, ce).wait_ge(self.sems[t[0]], t[1])

    def pipe(self, n, load, mm, evac, evac_eng, nbuf=2, npsum=2,
             dma_eng="gpsimd"):
        if isinstance(dma_eng, str):
            dma_eng = (dma_eng,)
        pp0 = self.val("PP")
        pp_after = [0] * n
        sc_after = [0] * n
        for j in range(n):
            if load is not None:
                if nbuf == 0:
                    self.fence("tensor", dma_eng)
                elif j >= nbuf:
                    for de in dma_eng:
                        self.wait(de, "PP", pp_after[j - nbuf])
                else:
                    for de in dma_eng:
                        self.wait(de, "PP", pp0)
                load(j)
                self.wait("tensor", "WT")
                self.wait("tensor", "LD")
                self.wait("tensor", "LS")
            if j >= npsum:
                self.wait("tensor", "SC", sc_after[j - npsum])
            mm(j)
            pp_after[j] = self.inc_last("tensor", "PP")
            self.wait("scalar", "PP", pp_after[j])
            self.wait("vector", "PP", pp_after[j])
            if j >= 2:
                self.wait("scalar", "SC", sc_after[j - 2])
                self.wait("vector", "SC", sc_after[j - 2])
            evac(j)
            sc_after[j] = self.inc_last(evac_eng, "SC")
        self.wait("tensor", "SC", sc_after[n - 1])


class Alloc:

    def __init__(self, nc, st, nb_cols_bf16, nf_cols_f32):
        self.ab = st.enter_context(
            nc.sbuf_tensor("arena_b", [128, nb_cols_bf16], BF16))
        self.af = st.enter_context(
            nc.sbuf_tensor("arena_f", [128, nf_cols_f32], F32))
        self.ob = 0
        self.of = 0
        self.nb = nb_cols_bf16
        self.nf = nf_cols_f32

    def b(self, w):
        assert self.ob + w <= self.nb, f"bf16 arena overflow {self.ob}+{w}"
        v = self.ab[:, self.ob:self.ob + w]
        self.ob += w
        return v

    def f(self, w):
        assert self.of + w <= self.nf, f"f32 arena overflow {self.of}+{w}"
        v = self.af[:, self.of:self.of + w]
        self.of += w
        return v

    def save(self):
        return (self.ob, self.of)

    def restore(self, s):
        self.ob, self.of = s


def make_sems(nc, st, extra=()):
    names = ["LD", "WT", "ST", "GD", "CC", "PP", "SC", "PC", "LS"] + ["F_" + e for e in ENGS] + list(extra)
    return {n: st.enter_context(nc.semaphore(n)) for n in names}


def run_walks(nc, st, emit):
    sems = make_sems(nc, st)
    with nc.Block() as blk:
        @blk.sync
        def _(e):
            emit(Walk(nc, sems, "sync"))

        @blk.tensor
        def _(e):
            emit(Walk(nc, sems, "tensor"))

        @blk.vector
        def _(e):
            emit(Walk(nc, sems, "vector"))

        @blk.scalar
        def _(e):
            emit(Walk(nc, sems, "scalar"))

        @blk.gpsimd
        def _(e):
            emit(Walk(nc, sems, "gpsimd"))
    return nc



def ln_tile_g(w, env, src_chunks, dst_chunks, T):
    nch = len(src_chunks)
    nd = nch * 128
    ones_b = env["ones_b"]; onesr = env["onesr"]; sqbuf = env["sqbuf"]
    rows = env["rows"]; bcA = env["bcA"]; bcB = env["bcB"]; tmpb = env["tmpb"]
    ps_st = env["ps_st"]; ps_bA = env["ps_bA"]; ps_bB = env["ps_bB"]
    for c in range(nch):
        w.op("tensor", "matmul", ps_st[0:1, :T], ones_b[:, 0:1],
             src_chunks[c], start=(c == 0), stop=(c == nch - 1))
    for c in range(nch):
        sq = sqbuf[:, (c % 2) * 512:(c % 2) * 512 + T]
        if c >= 2:
            w.fence("tensor", "scalar")
        w.op("scalar", "activation", sq, src_chunks[c], AF.Square)
        w.fence("scalar", "tensor")
        w.op("tensor", "matmul", ps_st[32:33, :T], ones_b[:, 0:1], sq,
             start=(c == 0), stop=(c == nch - 1))
    w.fence("tensor", "vector")
    m = rows[0:1, 0:T]
    va = rows[0:1, 512:512 + T]
    aa = rows[0:1, 1024:1024 + T]
    bb = rows[0:1, 1536:1536 + T]
    w.op("vector", "tensor_scalar", m, ps_st[0:1, :T], 1.0 / nd, None,
         op0=ALU.mult)
    w.op("vector", "tensor_scalar", va, ps_st[32:33, :T], 1.0 / nd,
         None, op0=ALU.mult)
    w.fence("vector", "vector")
    w.op("vector", "tensor_tensor", aa, m, m, op=ALU.mult)
    w.fence("vector", "vector")
    w.op("vector", "tensor_tensor", va, va, aa, op=ALU.subtract)
    w.fence("vector", "vector")
    w.op("vector", "tensor_scalar", va, va, 1e-5, None, op0=ALU.add)
    w.fence("vector", "scalar")
    w.op("scalar", "activation", aa, va, AF.Sqrt)
    w.fence("scalar", "vector")
    w.op("vector", "reciprocal", aa, aa)
    w.fence("vector", "vector")
    w.op("vector", "tensor_tensor", bb, m, aa, op=ALU.mult)
    w.fence("vector", "tensor")
    w.op("tensor", "matmul", ps_bA[:, :T], onesr[0:1, :128], aa,
         start=True, stop=True)
    w.op("tensor", "matmul", ps_bB[:, :T], onesr[0:1, :128], bb,
         start=True, stop=True)
    w.fence("tensor", "scalar")
    w.op("scalar", "copy", bcA[:, :T], ps_bA[:, :T])
    w.op("scalar", "copy", bcB[:, :T], ps_bB[:, :T])
    w.fence("scalar", "vector")
    for c in range(nch):
        w.op("vector", "tensor_tensor", tmpb[:, :T], src_chunks[c],
             bcA[:, :T], op=ALU.mult)
        w.fence("vector", "vector")
        w.op("vector", "tensor_tensor", dst_chunks[c], tmpb[:, :T],
             bcB[:, :T], op=ALU.subtract)
        w.fence("vector", "vector")
    w.fence("vector", ("tensor", "scalar"))


def build_l0a(dbg=False):
    nc = bass.Bass("TRN2", target_bir_lowering=False)
    TQ = 512

    xT_band = nc.dram_tensor("xT_band", [D, BAND], F32, kind="ExternalInput").ap()
    xT_front = nc.dram_tensor("xT_front", [D, FRONT], F32, kind="ExternalInput").ap()
    wq = nc.dram_tensor("wq", [D, D], F32, kind="ExternalInput").ap()
    wk = nc.dram_tensor("wk", [D, 512], F32, kind="ExternalInput").ap()
    wv = nc.dram_tensor("wv", [D, 512], F32, kind="ExternalInput").ap()
    wo = nc.dram_tensor("wo", [D, D], F32, kind="ExternalInput").ap()
    cs_band = nc.dram_tensor("cs_band", [128, BAND], F32, kind="ExternalInput").ap()
    cs_front = nc.dram_tensor("cs_front", [128, FRONT], F32, kind="ExternalInput").ap()
    cs_q = nc.dram_tensor("cs_q", [128, NTOK], F32, kind="ExternalInput").ap()
    cs2_band = nc.dram_tensor("cs2_band", [128, BAND], F32, kind="ExternalInput").ap()
    cs2_front = nc.dram_tensor("cs2_front", [128, FRONT], F32, kind="ExternalInput").ap()
    cs2_q = nc.dram_tensor("cs2_q", [128, NTOK], F32, kind="ExternalInput").ap()
    masks = nc.dram_tensor("masks", [16, 128, 896], F32, kind="ExternalInput").ap()
    identb = nc.dram_tensor("identb", [128, 128], BF16, kind="ExternalInput").ap()
    x1T = nc.dram_tensor("x1T", [D, NTOK], F32, kind="ExternalOutput").ap()
    if dbg:
        h_dbg = nc.dram_tensor("h_dbg", [D, NTOK], F32, kind="ExternalOutput").ap()
        k_dbg = nc.dram_tensor("k_dbg", [512, KVLEN], F32, kind="ExternalOutput").ap()
        vt_dbg = nc.dram_tensor("vt_dbg", [512, KVLEN], F32, kind="ExternalOutput").ap()
        q_dbg = nc.dram_tensor("q_dbg", [D, 512], F32, kind="ExternalOutput").ap()
        ao_dbg = nc.dram_tensor("ao_dbg", [D, 512], F32, kind="ExternalOutput").ap()
        ap_dbg = nc.dram_tensor("ap_dbg", [D, 512], F32, kind="ExternalOutput").ap()

    st = contextlib.ExitStack()
    al = Alloc(nc, st, nb_cols_bf16=91000, nf_cols_f32=5700)
    hq = [al.b(NTOK) for _ in range(16)]
    ht = [al.b(512) for _ in range(16)]
    K = [al.b(KVLEN) for _ in range(4)]
    VT = [al.b(KVLEN) for _ in range(4)]
    q2 = [al.b(TQ) for _ in range(16)]
    ao2 = [al.b(TQ) for _ in range(16)]
    wbuf = [al.b(D) for _ in range(2)]
    sqbuf = al.b(1024)
    bcA = al.b(512)
    bcB = al.b(512)
    tmpb = al.b(512)
    cs_t = al.b(512)
    cs_t2 = al.b(512)
    scr = al.b(1024)
    Pt = al.b(896)
    PTs = al.b(896)
    idb = al.b(128)
    ones_b = al.b(2)
    xt = [al.f(512) for _ in range(2)]
    rows = al.f(2048)
    msk = al.f(896)
    scm = al.f(896)
    cols = al.f(8)
    outst = al.f(512)
    onesr = al.f(128)

    ps_qkv = st.enter_context(nc.psum_tensor("ps_qkv", [128, 512], F32))
    ps_sc = st.enter_context(nc.psum_tensor("ps_sc", [128, 896], F32))
    ps_att = st.enter_context(nc.psum_tensor("ps_att", [128, 128], F32))
    ps_PT = st.enter_context(nc.psum_tensor("ps_PT", [128, 896], BF16))
    ps_st = st.enter_context(nc.psum_tensor("ps_st", [64, 512], F32))
    ps_bA = st.enter_context(nc.psum_tensor("ps_bA", [128, 512], F32))
    ps_bB = st.enter_context(nc.psum_tensor("ps_bB", [128, 512], F32))
    env = dict(ones_b=ones_b, onesr=onesr, sqbuf=sqbuf, rows=rows, bcA=bcA,
               bcB=bcB, tmpb=tmpb, ps_st=ps_st, ps_bA=ps_bA, ps_bB=ps_bB)

    def emit(w):
        w.op("vector", "memset", ones_b[:, 0:1], 1.0)
        w.op("vector", "memset", onesr[0:1, :128], 1.0)
        w.fence("vector", ("tensor", "scalar"))
        w.dma("sync", idb, identb, "LD")

        def ln_tile(src_chunks, dst_chunks, T):
            ln_tile_g(w, env, src_chunks, dst_chunks, T)

        def rope_from_psum(ps, dst, T):
            a = scr[0:64, 0:T]
            b = scr[0:64, 512:512 + T]
            a2 = scr[64:128, 0:T]
            b2 = scr[64:128, 512:512 + T]
            w.op("vector", "tensor_tensor", a, ps[0:64, :T], cs_t[0:64, :T],
                 op=ALU.mult)
            w.op("vector", "tensor_tensor", b, ps[64:128, :T],
                 cs_t[64:128, :T], op=ALU.mult)
            w.op("vector", "tensor_tensor", a2, ps[0:64, :T],
                 cs_t2[0:64, :T], op=ALU.mult)
            w.op("vector", "tensor_tensor", b2, ps[64:128, :T],
                 cs_t2[64:128, :T], op=ALU.mult)
            w.fence("vector", "vector")
            w.op("vector", "tensor_tensor", dst[0:64, :T], a, b,
                 op=ALU.subtract)
            w.op("vector", "tensor_tensor", dst[64:128, :T], a2, b2,
                 op=ALU.add)

        def stream_wcol(dram_w, j, par):
            w.fence("tensor", "gpsimd")
            w.dma("gpsimd", wbuf[par].rearrange("p (c j) -> p c j", c=16),
                  dram_w[:, j * 128:(j + 1) * 128].rearrange(
                      "(c p) j -> p c j", p=128), "WT")
            w.wait("tensor", "WT")

        for kvt in range(6):
            T = 512 if kvt < 5 else FRONT
            if kvt < 5:
                src_ap = xT_band
                col0 = kvt * 512
                csrc = (cs_band, cs2_band)
            else:
                src_ap = xT_front
                col0 = 0
                csrc = (cs_front, cs2_front)
            if 1 <= kvt <= 4:
                dst = [hq[c][:, (kvt - 1) * 512:(kvt - 1) * 512 + T]
                       for c in range(16)]
            else:
                dst = [ht[c][:, :T] for c in range(16)]
            w.fence(("tensor", "vector", "scalar"), "gpsimd")
            for c in range(16):
                w.dma("gpsimd", dst[c],
                      src_ap[c * 128:(c + 1) * 128, col0:col0 + T], "WT")
            w.wait("tensor", "WT")
            w.wait("scalar", "WT")
            w.wait("vector", "WT")
            ln_tile(dst, dst, T)
            w.fence("vector", "gpsimd")
            w.dma("gpsimd", cs_t[:, :T], csrc[0][:, col0:col0 + T], "WT")
            w.dma("gpsimd", cs_t2[:, :T], csrc[1][:, col0:col0 + T], "WT")
            w.wait("vector", "WT")
            koff = kvt * 512 if kvt < 5 else BAND
            for j in range(4):
                stream_wcol(wk, j, j % 2)
                for kc in range(16):
                    w.op("tensor", "matmul", ps_qkv[:, :T],
                         wbuf[j % 2][:, kc * 128:(kc + 1) * 128], dst[kc],
                         start=(kc == 0), stop=(kc == 15))
                w.fence("tensor", "vector")
                rope_from_psum(ps_qkv, K[j][:, koff:koff + T], T)
                w.fence("vector", "tensor")
            for j in range(4):
                stream_wcol(wv, j, j % 2)
                for kc in range(16):
                    w.op("tensor", "matmul", ps_qkv[:, :T],
                         wbuf[j % 2][:, kc * 128:(kc + 1) * 128], dst[kc],
                         start=(kc == 0), stop=(kc == 15))
                w.fence("tensor", "scalar")
                w.op("scalar", "copy", tmpb[:, :T], ps_qkv[:, :T])
                w.fence("scalar", "tensor")
                w.wait("tensor", "LD", 16)
                for t in range(T // 128):
                    w.op("tensor", "transpose",
                         ps_PT[:, t * 128:(t + 1) * 128],
                         tmpb[:, t * 128:(t + 1) * 128], idb)
                w.fence("tensor", "vector")
                w.op("vector", "tensor_copy",
                     VT[j][:, koff:koff + T], ps_PT[:, :T])
                w.fence("vector", "tensor")

        if dbg:
            w.fence(("vector", "tensor", "scalar"), "gpsimd")
            for c in range(16):
                w.dma("gpsimd", h_dbg[c * 128:(c + 1) * 128, :], hq[c], "GD")
            for j in range(4):
                w.dma("gpsimd", k_dbg[j * 128:(j + 1) * 128, :], K[j], "GD")
                w.dma("gpsimd", vt_dbg[j * 128:(j + 1) * 128, :], VT[j], "GD")

        for qt in range(4):
            w.fence("vector", "gpsimd")
            w.dma("gpsimd", cs_t[:, :512], cs_q[:, qt * 512:(qt + 1) * 512],
                  "WT")
            w.dma("gpsimd", cs_t2[:, :512],
                  cs2_q[:, qt * 512:(qt + 1) * 512], "WT")
            for j in range(16):
                stream_wcol(wq, j, j % 2)
                for kc in range(16):
                    w.op("tensor", "matmul", ps_qkv[:, :512],
                         wbuf[j % 2][:, kc * 128:(kc + 1) * 128],
                         hq[kc][:, qt * 512:(qt + 1) * 512],
                         start=(kc == 0), stop=(kc == 15))
                w.fence("tensor", "vector")
                w.wait("vector", "WT")
                rope_from_psum(ps_qkv, q2[j], 512)
                w.fence("vector", "tensor")
            if dbg and qt == 0:
                w.fence("vector", "gpsimd")
                for j in range(16):
                    w.dma("gpsimd", q_dbg[j * 128:(j + 1) * 128, :], q2[j],
                          "GD")
            for h in range(16):
                g = h // 4
                for qb in range(4):
                    Qg = qt * 4 + qb
                    w.fence("vector", "sync")
                    w.dma("sync", msk, masks[Qg], "LD")
                    b0 = Qg * 128
                    w.op("tensor", "matmul", ps_sc[:, 0:512],
                         q2[h][:, qb * 128:(qb + 1) * 128],
                         K[g][:, b0:b0 + 512], start=True, stop=True)
                    w.op("tensor", "matmul", ps_sc[:, 512:640],
                         q2[h][:, qb * 128:(qb + 1) * 128],
                         K[g][:, b0 + 512:b0 + 640], start=True, stop=True)
                    w.op("tensor", "matmul", ps_sc[:, 640:896],
                         q2[h][:, qb * 128:(qb + 1) * 128],
                         K[g][:, BAND:BAND + 256], start=True, stop=True)
                    w.fence("tensor", "vector")
                    w.wait("vector", "LD")
                    w.op("vector", "scalar_tensor_tensor", scm, ps_sc[:, :],
                         0.08838834764831845, msk, op0=ALU.mult, op1=ALU.add)
                    w.fence("vector", "vector")
                    w.op("vector", "reduce_max", cols[:, 0:1], scm, axis=AX.X)
                    w.fence("vector", "vector")
                    w.op("vector", "tensor_scalar", cols[:, 1:2],
                         cols[:, 0:1], -1.0, None, op0=ALU.mult)
                    w.fence("vector", ("scalar", "tensor"))
                    w.op("scalar", "activation", Pt, scm, AF.Exp,
                         bias=cols[:, 1:2], accum_out=cols[:, 2:3])
                    w.fence("scalar", "vector")
                    w.op("vector", "reciprocal", cols[:, 3:4], cols[:, 2:3])
                    w.fence("vector", "vector")
                    w.op("vector", "tensor_scalar", Pt, Pt, cols[:, 3:4],
                         None, op0=ALU.mult)
                    w.fence("vector", "tensor")
                    for s in range(7):
                        w.op("tensor", "transpose",
                             ps_PT[:, s * 128:(s + 1) * 128],
                             Pt[:, s * 128:(s + 1) * 128], idb)
                    w.fence("tensor", "vector")
                    w.op("vector", "tensor_copy", PTs, ps_PT[:, :896])
                    w.fence("vector", "tensor")
                    for s in range(7):
                        gk = Qg + s if s < 5 else 20 + (s - 5)
                        w.op("tensor", "matmul", ps_att[:, :],
                             VT[g][:, gk * 128:(gk + 1) * 128],
                             PTs[:, s * 128:(s + 1) * 128],
                             start=(s == 0), stop=(s == 6))
                    w.fence("tensor", "scalar")
                    w.op("scalar", "copy", ao2[h][:, qb * 128:(qb + 1) * 128],
                         ps_att[:, :])
                    w.fence("scalar", "tensor")
            if dbg and qt == 0:
                w.fence("scalar", "gpsimd")
                for j in range(16):
                    w.dma("gpsimd", ao_dbg[j * 128:(j + 1) * 128, :], ao2[j],
                          "GD")
            for j in range(16):
                stream_wcol(wo, j, j % 2)
                for kc in range(16):
                    w.op("tensor", "matmul", ps_qkv[:, :512],
                         wbuf[j % 2][:, kc * 128:(kc + 1) * 128], ao2[kc],
                         start=(kc == 0), stop=(kc == 15))
                w.fence("tensor", "scalar")
                w.op("scalar", "copy", q2[j], ps_qkv[:, :512])
                w.fence("scalar", ("tensor", "vector"))
            if dbg and qt == 0:
                w.fence("scalar", "gpsimd")
                for j in range(16):
                    w.dma("gpsimd", ap_dbg[j * 128:(j + 1) * 128, :], q2[j],
                          "GD")
            ln_tile(q2, q2, 512)
            for c in range(16):
                par = c % 2
                if c >= 2:
                    w.fence("vector", "sync")
                w.dma("sync", xt[par][:, :512],
                      xT_band[c * 128:(c + 1) * 128,
                              512 + qt * 512:512 + (qt + 1) * 512], "LD")
                w.wait("vector", "LD")
                w.wait("vector", "ST")
                w.op("vector", "tensor_tensor", outst, xt[par][:, :512],
                     q2[c], op=ALU.add)
                w.fence("vector", "sync")
                w.dma("sync", x1T[c * 128:(c + 1) * 128,
                                  qt * 512:(qt + 1) * 512], outst, "ST")
        w.wait("sync", "ST")
        w.wait("sync", "LD")
        w.wait("sync", "WT")
        w.wait("sync", "GD")

    run_walks(nc, st, emit)
    st.close()
    return nc


def rope_perm():
    return np.r_[np.arange(0, 128, 2), np.arange(1, 128, 2)]


def cs_table(S=8192):
    inv = 1.0 / 10000.0 ** (np.arange(0, 128, 2).astype(np.float32) / 128)
    t = np.arange(S, dtype=np.float32)
    ang = np.outer(inv, t)
    return np.concatenate([np.cos(ang), np.sin(ang)], axis=0).astype(np.float32)


def build_masks(tok0):
    M = np.empty((16, 128, 896), dtype=np.float32)
    ii = np.arange(128)
    for Qg in range(16):
        iabs = tok0 + Qg * 128 + ii[:, None]
        for s in range(7):
            if s < 5:
                jabs = tok0 - 512 + (Qg + s) * 128 + ii[None, :]
            else:
                jabs = (s - 5) * 128 + ii[None, :]
            allowed = ((jabs <= iabs) & (jabs > iabs - 512)) | \
                      ((jabs < 256) & (jabs <= iabs - 512))
            allowed &= (jabs >= 0)
            if s >= 5:
                fs0 = (s - 5) * 128
                bandlo = tok0 + (Qg - 4) * 128
                bandhi = tok0 + (Qg + 1) * 128
                if bandlo <= fs0 and bandhi >= fs0 + 128:
                    allowed = np.zeros_like(allowed)
            M[Qg, :, s * 128:(s + 1) * 128] = np.where(allowed, 0.0, -1e30)
    return M


def l0a_inputs(x_full, wq, wk, wv, wo):
    perm = rope_perm()
    permQ = np.concatenate([h * 128 + perm for h in range(16)])
    permK = np.concatenate([h * 128 + perm for h in range(4)])
    wqP = np.ascontiguousarray(np.asarray(wq)[:, permQ])
    wkP = np.ascontiguousarray(np.asarray(wk)[:, permK])
    wv = np.ascontiguousarray(np.asarray(wv))
    wo = np.ascontiguousarray(np.asarray(wo))
    cs = cs_table()
    ident = np.eye(128).astype(BF)
    ins = []
    for c in range(NCORES):
        b, q = c // 4, c % 4
        tok0 = q * NTOK
        xb = np.zeros((D, BAND), dtype=np.float32)
        lo = max(0, tok0 - 512)
        xb[:, 512 - (tok0 - lo):] = x_full[b, lo:tok0 + NTOK].T
        xf = np.ascontiguousarray(x_full[b, 0:FRONT].T)
        csb = np.zeros((128, BAND), dtype=np.float32)
        csb[:, 512 - (tok0 - lo):] = cs[:, lo:tok0 + NTOK]
        csb[0:64, :512 - (tok0 - lo)] = 1.0
        sw = lambda a: np.ascontiguousarray(np.concatenate(
            [a[64:128], a[0:64]], axis=0))
        csb_c = np.ascontiguousarray(csb)
        csf = np.ascontiguousarray(cs[:, 0:FRONT])
        csq = np.ascontiguousarray(cs[:, tok0:tok0 + NTOK])
        ins.append({
            "xT_band": np.ascontiguousarray(xb),
            "xT_front": xf,
            "wq": wqP, "wk": wkP, "wv": wv, "wo": wo,
            "cs_band": csb_c, "cs2_band": sw(csb_c),
            "cs_front": csf, "cs2_front": sw(csf),
            "cs_q": csq, "cs2_q": sw(csq),
            "masks": build_masks(tok0),
            "identb": ident,
        })
    return ins


def build_l0f(dbg=False):
    nc = bass.Bass("TRN2", target_bir_lowering=False)
    TT = 512

    xT = nc.dram_tensor("xT", [D, NTOK], F32, kind="ExternalInput").ap()
    W1 = nc.dram_tensor("W1", [D, D], F32, kind="ExternalInput").ap()
    b1r = nc.dram_tensor("b1r", [128, 16], F32, kind="ExternalInput").ap()
    W2 = nc.dram_tensor("W2", [D, 8], F32, kind="ExternalInput").ap()
    b2c = nc.dram_tensor("b2c", [8, 1], F32, kind="ExternalInput").ap()
    VP = nc.dram_tensor("VP", [4, D, 128], F32, kind="ExternalInput").ap()
    UP = nc.dram_tensor("UP", [4, 128, D], F32, kind="ExternalInput").ap()
    SEL = nc.dram_tensor("SEL", [8, 512], F32, kind="ExternalInput").ap()
    identf = nc.dram_tensor("identf", [128, 128], F32, kind="ExternalInput").ap()
    x2T = nc.dram_tensor("x2T", [D, NTOK], F32, kind="ExternalOutput").ap()
    if dbg:
        hid_dbg = nc.dram_tensor("hid_dbg", [D, TT], F32, kind="ExternalOutput").ap()
        al_dbg = nc.dram_tensor("al_dbg", [8, TT], F32, kind="ExternalOutput").ap()
        vx_dbg = nc.dram_tensor("vx_dbg", [512, TT], F32, kind="ExternalOutput").ap()
        o_dbg = nc.dram_tensor("o_dbg", [D, TT], F32, kind="ExternalOutput").ap()

    st = contextlib.ExitStack()
    al = Alloc(nc, st, nb_cols_bf16=62000, nf_cols_f32=6000)
    src = [al.b(TT) for _ in range(16)]
    hid = [al.b(TT) for _ in range(16)]
    o2 = [al.b(TT) for _ in range(16)]
    vx = [al.b(TT) for _ in range(4)]
    wsc = [al.b(TT) for _ in range(4)]
    wbuf = [al.b(D) for _ in range(2)]
    vpw = [al.b(D) for _ in range(4)]
    upw = [al.b(D) for _ in range(4)]
    w2w = al.b(8 * 16)
    sqbuf = al.b(1024)
    bcA = al.b(512)
    bcB = al.b(512)
    tmpb = al.b(512)
    idb = al.b(128)
    ones_b = al.b(2)
    xt = [al.f(512) for _ in range(2)]
    rows = al.f(2048)
    onesr = al.f(128)
    idf = al.f(128)
    sa = al.f(512)
    at = al.f(32)
    am = al.f(4)
    asum = al.f(4)
    outst = al.f(512)
    alf = al.f(512)
    b1s = al.f(16)
    b2s = al.f(1)

    ps_mm = st.enter_context(nc.psum_tensor("ps_mm", [128, 512], F32))
    ps_a = st.enter_context(nc.psum_tensor("ps_a", [8, 512], F32))
    ps_at = st.enter_context(nc.psum_tensor("ps_at", [128, 32], F32))
    ps_af = st.enter_context(nc.psum_tensor("ps_af", [8, 512], F32))
    ps_bc = st.enter_context(nc.psum_tensor("ps_bc", [128, 512], F32))
    ps_st = st.enter_context(nc.psum_tensor("ps_st", [64, 512], F32))
    ps_bA = st.enter_context(nc.psum_tensor("ps_bA", [128, 512], F32))
    ps_bB = st.enter_context(nc.psum_tensor("ps_bB", [128, 512], F32))
    env = dict(ones_b=ones_b, onesr=onesr, sqbuf=sqbuf, rows=rows, bcA=bcA,
               bcB=bcB, tmpb=tmpb, ps_st=ps_st, ps_bA=ps_bA, ps_bB=ps_bB)

    def emit(w):
        w.op("vector", "memset", ones_b[:, 0:1], 1.0)
        w.op("vector", "memset", onesr[0:1, :128], 1.0)
        w.fence("vector", ("tensor", "scalar"))
        for m in range(4):
            w.dma("gpsimd", vpw[m].rearrange("p (c j) -> p c j", c=16),
                  VP[m].rearrange("(c p) j -> p c j", p=128), "WT")
            w.dma("gpsimd", upw[m], UP[m], "WT")
        w.dma("gpsimd", w2w.rearrange("p (c j) -> p c j", c=16),
              W2.rearrange("(c p) j -> p c j", p=128), "WT")
        w.dma("sync", b1s[:, 0:16], b1r, "LD")
        w.dma("sync", b2s[0:8, 0:1], b2c, "LD")
        w.dma("sync", idf, identf, "LD")
        w.dma("sync", selw[0:8, :], SEL, "LD")
        w.wait("tensor", "WT")
        w.wait("tensor", "LD")
        w.wait("scalar", "LD")

        def ln_tile(srcs, dsts, T):
            ln_tile_g(w, env, srcs, dsts, T)

        for tt in range(4):
            c0 = tt * TT
            w.fence(("tensor", "vector", "scalar"), "gpsimd")
            for c in range(16):
                w.dma("gpsimd", src[c], xT[c * 128:(c + 1) * 128,
                                           c0:c0 + TT], "WT")
            w.wait("tensor", "WT")
            w.wait("scalar", "WT")
            w.wait("vector", "WT")
            ln_tile(src, src, TT)
            for j in range(16):
                w.fence("tensor", "gpsimd")
                w.dma("gpsimd", wbuf[j % 2].rearrange("p (c j) -> p c j", c=16),
                      W1[:, j * 128:(j + 1) * 128].rearrange(
                          "(c p) j -> p c j", p=128), "WT")
                w.wait("tensor", "WT")
                for kc in range(16):
                    w.op("tensor", "matmul", ps_mm[:, :],
                         wbuf[j % 2][:, kc * 128:(kc + 1) * 128], src[kc],
                         start=(kc == 0), stop=(kc == 15))
                w.fence("tensor", "scalar")
                w.op("scalar", "activation", hid[j], ps_mm[:, :], AF.Relu,
                     bias=b1s[:, j:j + 1])
                w.fence("scalar", "tensor")
            for kc in range(16):
                w.op("tensor", "matmul", ps_a[:, :],
                     w2w[:, kc * 8:(kc + 1) * 8], hid[kc],
                     start=(kc == 0), stop=(kc == 15))
            w.fence("tensor", "scalar")
            w.op("scalar", "activation", sa[0:8, :], ps_a[:, :], AF.Identity,
                 bias=b2s[0:8, 0:1])
            w.fence("scalar", "tensor")
            for b in range(4):
                w.op("tensor", "transpose", ps_at[:, b * 8:(b + 1) * 8],
                     sa[0:8, b * 128:(b + 1) * 128], idf[0:8, 0:8])
            w.fence("tensor", "vector")
            atv = at[:, 0:32].rearrange("p (b e) -> p b e", e=8)
            w.op("vector", "tensor_copy", at[:, 0:32], ps_at[:, :])
            w.fence("vector", "vector")
            w.op("vector", "reduce_max", am[:, 0:4],
                 atv, axis=AX.X)
            w.fence("vector", "vector")
            amb = am[:, 0:4, None].broadcast_to([128, 4, 8])
            w.op("vector", "tensor_tensor", atv, atv, amb, op=ALU.subtract)
            w.fence("vector", "scalar")
            w.op("scalar", "activation", at[:, 0:32], at[:, 0:32], AF.Exp)
            w.fence("scalar", "vector")
            w.op("vector", "tensor_reduce", asum[:, 0:4], atv, axis=AX.X,
                 op=ALU.add)
            w.fence("vector", "vector")
            w.op("vector", "reciprocal", asum[:, 0:4], asum[:, 0:4])
            w.fence("vector", "vector")
            asb = asum[:, 0:4, None].broadcast_to([128, 4, 8])
            w.op("vector", "tensor_tensor", atv, atv, asb, op=ALU.mult)
            w.fence("vector", "tensor")
            for b in range(4):
                w.op("tensor", "transpose", ps_af[:, b * 128:(b + 1) * 128],
                     at[:, b * 8:(b + 1) * 8], idf)
            w.fence("tensor", "vector")
            w.op("vector", "tensor_copy", alf[0:8, :], ps_af[:, :])
            w.fence("vector", "tensor")
            if dbg and tt == 0:
                w.fence("vector", "gpsimd")
                w.dma("gpsimd", al_dbg, alf[0:8, :], "GD")
                for j in range(16):
                    w.dma("gpsimd", hid_dbg[j * 128:(j + 1) * 128, :],
                          hid[j], "GD")
            for m in range(4):
                for kc in range(16):
                    w.op("tensor", "matmul", ps_mm[:, :],
                         vpw[m][:, kc * 128:(kc + 1) * 128], src[kc],
                         start=(kc == 0), stop=(kc == 15))
                w.fence("tensor", "scalar")
                w.op("scalar", "copy", vx[m], ps_mm[:, :])
                w.fence("scalar", "tensor")
            for m in range(4):
                w.op("tensor", "matmul", ps_bc[:, :],
                     selw[0:8, m * 128:(m + 1) * 128],
                     alf[0:8, :], start=True, stop=True)
                w.fence("tensor", "vector")
                w.op("vector", "tensor_tensor", wsc[m], vx[m], ps_bc[:, :],
                     op=ALU.mult)
                w.fence("vector", "tensor")
            if dbg and tt == 0:
                w.fence(("vector", "scalar"), "gpsimd")
                for m in range(4):
                    w.dma("gpsimd", vx_dbg[m * 128:(m + 1) * 128, :], vx[m],
                          "GD")
            for j in range(16):
                for m in range(4):
                    w.op("tensor", "matmul", ps_mm[:, :],
                         upw[m][:, j * 128:(j + 1) * 128], wsc[m],
                         start=(m == 0), stop=(m == 3))
                w.fence("tensor", "scalar")
                w.op("scalar", "copy", o2[j], ps_mm[:, :])
                w.fence("scalar", ("tensor", "vector"))
            if dbg and tt == 0:
                w.fence("scalar", "gpsimd")
                for j in range(16):
                    w.dma("gpsimd", o_dbg[j * 128:(j + 1) * 128, :], o2[j],
                          "GD")
            ln_tile(o2, o2, TT)
            for c in range(16):
                par = c % 2
                if c >= 2:
                    w.fence("vector", "sync")
                w.dma("sync", xt[par][:, :], xT[c * 128:(c + 1) * 128,
                                               c0:c0 + TT], "LD")
                w.wait("vector", "LD")
                w.wait("vector", "ST")
                w.op("vector", "tensor_tensor", outst, xt[par][:, :], o2[c],
                     op=ALU.add)
                w.fence("vector", "sync")
                w.dma("sync", x2T[c * 128:(c + 1) * 128, c0:c0 + TT],
                      outst, "ST")
        w.wait("sync", "ST")
        w.wait("sync", "LD")
        w.wait("sync", "WT")
        w.wait("sync", "GD")

    selw = al.f(512)
    run_walks(nc, st, emit)
    st.close()
    return nc


def l0f_inputs(x2_like_core_slices, U, V, W1, b1, W2, b2):
    U_T = np.ascontiguousarray(np.asarray(U).transpose(0, 2, 1))
    V_T = np.ascontiguousarray(np.asarray(V).transpose(0, 2, 1))
    VP = np.stack([np.concatenate([V_T[2 * m], V_T[2 * m + 1]], axis=1)
                   for m in range(4)])
    UP = np.stack([np.concatenate([U_T[2 * m], U_T[2 * m + 1]], axis=0)
                   for m in range(4)])
    b1r = np.ascontiguousarray(np.asarray(b1).reshape(16, 128).T)
    b2c = np.asarray(b2).reshape(8, 1).astype(np.float32)
    SEL = np.zeros((8, 512), dtype=np.float32)
    for m in range(4):
        SEL[2 * m, m * 128:m * 128 + 64] = 1.0
        SEL[2 * m + 1, m * 128 + 64:(m + 1) * 128] = 1.0
    identf = np.eye(128, dtype=np.float32)
    ins = []
    for c in range(NCORES):
        ins.append({
            "xT": x2_like_core_slices[c],
            "W1": np.ascontiguousarray(np.asarray(W1)),
            "b1r": b1r, "W2": np.ascontiguousarray(np.asarray(W2)),
            "b2c": b2c,
            "VP": VP.astype(np.float32), "UP": UP.astype(np.float32),
            "SEL": SEL, "identf": identf,
        })
    return ins


def build_l1a_a(dbg=False):
    nc = bass.Bass("TRN2", target_bir_lowering=False)
    TT = 512

    xT = nc.dram_tensor("xT", [D, NTOK], F32, kind="ExternalInput").ap()
    xh = nc.dram_tensor("xh", [D, 4], F32, kind="ExternalInput").ap()
    wpT = nc.dram_tensor("wpT", [D, 4416], F32, kind="ExternalInput").ap()
    cwr = nc.dram_tensor("cwr", [128, 136], F32, kind="ExternalInput").ap()
    cbr = nc.dram_tensor("cbr", [128, 34], F32, kind="ExternalInput").ap()
    dtb = nc.dram_tensor("dtb", [64, 1], F32, kind="ExternalInput").ap()
    xBCT = nc.dram_tensor("xBCT", [4352, NTOK], BF16, kind="ExternalOutput").ap()
    dt2T = nc.dram_tensor("dt2T", [64, NTOK], F32, kind="ExternalOutput").ap()

    st = contextlib.ExitStack()
    al = Alloc(nc, st, nb_cols_bf16=30000, nf_cols_f32=8000)
    src = [al.b(TT) for _ in range(16)]
    hh = [al.b(4) for _ in range(16)]
    wbuf = [al.b(D) for _ in range(2)]
    projx = al.b(515 + 1)
    xbc = al.b(TT)
    sqbuf = al.b(1024)
    bcA = al.b(512)
    bcB = al.b(512)
    tmpb = al.b(512)
    ones_b = al.b(2)
    rows = al.f(2048)
    onesr = al.f(128)
    cw = al.f(136)
    cb = al.f(34)
    dtbc = al.f(1)
    tails = al.f(110)
    acc = al.f(512)
    dta = al.f(512)
    dtx = al.f(512 * 3)
    ps_mm = st.enter_context(nc.psum_tensor("ps_mm", [128, 512], F32))
    ps_h = st.enter_context(nc.psum_tensor("ps_h", [128, 8], F32))
    ps_st = st.enter_context(nc.psum_tensor("ps_st", [64, 512], F32))
    ps_bA = st.enter_context(nc.psum_tensor("ps_bA", [128, 512], F32))
    ps_bB = st.enter_context(nc.psum_tensor("ps_bB", [128, 512], F32))
    env = dict(ones_b=ones_b, onesr=onesr, sqbuf=sqbuf, rows=rows, bcA=bcA,
               bcB=bcB, tmpb=tmpb, ps_st=ps_st, ps_bA=ps_bA, ps_bB=ps_bB)

    def emit(w):
        w.op("vector", "memset", ones_b[:, 0:1], 1.0)
        w.op("vector", "memset", onesr[0:1, :128], 1.0)
        w.fence("vector", ("tensor", "scalar"))
        w.dma("sync", cw, cwr, "LD")
        w.dma("sync", cb, cbr, "LD")
        w.dma("sync", dtbc[0:64, 0:1], dtb, "LD")
        w.fence(("tensor", "vector", "scalar"), "gpsimd")
        for c in range(16):
            w.dma("gpsimd", hh[c], xh[c * 128:(c + 1) * 128, :], "WT")
        w.wait("tensor", "WT")
        w.wait("scalar", "WT")
        w.wait("vector", "WT")
        ln_tile_g(w, env, [t[:, 0:4] for t in hh], [t[:, 0:4] for t in hh], 4)
        w.wait("scalar", "LD")
        w.wait("vector", "LD")

        for tt in range(4):
            c0 = tt * TT
            w.fence(("tensor", "vector", "scalar"), "gpsimd")
            for c in range(16):
                w.dma("gpsimd", src[c], xT[c * 128:(c + 1) * 128,
                                           c0:c0 + TT], "WT")
            w.wait("tensor", "WT")
            w.wait("scalar", "WT")
            w.wait("vector", "WT")
            ln_tile_g(w, env, src, src, TT)
            for j in range(35):
                wd = 128 if j < 34 else 64
                jc0 = j * 128
                w.fence("tensor", "gpsimd")
                w.dma("gpsimd",
                      wbuf[j % 2][:, :16 * wd].rearrange(
                          "p (c j) -> p c j", c=16),
                      wpT[:, jc0:jc0 + wd].rearrange(
                          "(c p) j -> p c j", p=128), "WT")
                w.wait("tensor", "WT")
                for kc in range(16):
                    w.op("tensor", "matmul", ps_mm[:wd if wd < 128 else 128, :TT],
                         wbuf[j % 2][:, kc * wd:(kc + 1) * wd], src[kc],
                         start=(kc == 0), stop=(kc == 15))
                if j < 34:
                    if tt == 0:
                        for kc in range(16):
                            w.op("tensor", "matmul", ps_h[:, 0:4],
                                 wbuf[j % 2][:, kc * 128:(kc + 1) * 128],
                                 hh[kc][:, 0:4],
                                 start=(kc == 0), stop=(kc == 15))
                    w.fence("tensor", "scalar")
                    w.op("scalar", "copy", projx[:, 3:515], ps_mm[:, :TT])
                    if tt == 0:
                        w.op("scalar", "copy", projx[:, 0:3], ps_h[:, 1:4])
                    w.fence("scalar", ("tensor", "vector"))
                    if tt > 0:
                        w.op("vector", "tensor_copy", projx[:, 0:3],
                             tails[:, j * 3:j * 3 + 3])
                        w.fence("vector", "vector")
                    w.op("vector", "tensor_scalar", acc, projx[:, 0:512],
                         cw[:, j * 4:j * 4 + 1], cb[:, j:j + 1],
                         op0=ALU.mult, op1=ALU.add)
                    for i in range(1, 4):
                        w.fence("vector", "vector")
                        w.op("vector", "scalar_tensor_tensor", acc,
                             projx[:, i:i + 512], cw[:, j * 4 + i:j * 4 + i + 1],
                             acc, op0=ALU.mult, op1=ALU.add)
                    w.fence("vector", "vector")
                    w.op("vector", "tensor_copy", tails[:, j * 3:j * 3 + 3],
                         projx[:, 512:515])
                    w.fence("vector", "scalar")
                    w.op("scalar", "activation", xbc, acc, AF.Silu)
                    w.fence("scalar", ("gpsimd", "vector"))
                    w.dma("gpsimd", xBCT[jc0:jc0 + 128, c0:c0 + TT], xbc,
                          "GD")
                else:
                    w.fence("tensor", "vector")
                    d0 = dtx[0:64, 0:512]
                    d1 = dtx[0:64, 512:1024]
                    d2 = dtx[0:64, 1024:1536]
                    w.op("vector", "tensor_scalar", d0, ps_mm[0:64, :TT],
                         dtbc[0:64, 0:1], None, op0=ALU.add)
                    w.fence("vector", "scalar")
                    w.op("scalar", "activation", d1, d0, AF.Abs)
                    w.fence("scalar", "scalar")
                    w.op("scalar", "activation", d1, d1, AF.Exp, scale=-1.0)
                    w.fence("scalar", "vector")
                    w.op("vector", "tensor_scalar", d1, d1, 1.0, None,
                         op0=ALU.add)
                    w.fence("vector", "scalar")
                    w.op("scalar", "activation", d2, d1, AF.Ln)
                    w.fence("scalar", "vector")
                    w.op("vector", "tensor_scalar", d0, d0, 0.0, None,
                         op0=ALU.max)
                    w.fence("vector", "vector")
                    w.op("vector", "tensor_tensor", dta[0:64, :], d0, d2,
                         op=ALU.add)
                    w.fence("vector", "sync")
                    w.dma("sync", dt2T[:, c0:c0 + TT], dta[0:64, :], "ST")
                w.fence(("vector", "scalar"), "tensor")
        w.wait("sync", "ST")
        w.wait("sync", "GD")
        w.wait("sync", "WT")
        w.wait("sync", "LD")

    run_walks(nc, st, emit)
    st.close()
    return nc


def build_l1a_b(dbg=False):
    nc = bass.Bass("TRN2", target_bir_lowering=False)
    CH = 256
    NCH = 32

    xsT = nc.dram_tensor("xsT", [1024, 8192], BF16, kind="ExternalInput").ap()
    BT_d = nc.dram_tensor("BT_d", [128, 8192], BF16, kind="ExternalInput").ap()
    CT_d = nc.dram_tensor("CT_d", [128, 8192], BF16, kind="ExternalInput").ap()
    dt2d = nc.dram_tensor("dt2d", [16, 8192], F32, kind="ExternalInput").ap()
    alog = nc.dram_tensor("alog", [16, 1], F32, kind="ExternalInput").ap()
    dbc_in = nc.dram_tensor("dbc_in", [128, 16], F32, kind="ExternalInput").ap()
    ehsel = nc.dram_tensor("ehsel", [16, 2048], F32, kind="ExternalInput").ap()
    trim = nc.dram_tensor("trim", [128, 128], BF16, kind="ExternalInput").ap()
    identf = nc.dram_tensor("identf", [128, 128], F32, kind="ExternalInput").ap()
    identb = nc.dram_tensor("identb", [128, 128], BF16, kind="ExternalInput").ap()
    yTok = nc.dram_tensor("yTok", [8192, 1024], F32, kind="ExternalOutput").ap()

    st = contextlib.ExitStack()
    al = Alloc(nc, st, nb_cols_bf16=20000, nf_cols_f32=12000)
    xs = [al.b(CH) for _ in range(8)]
    Bm = al.b(CH)
    Cm = al.b(CH)
    xT = [al.b(1024) for _ in range(2)]
    BTt = [al.b(128) for _ in range(2)]
    CBTm = [al.b(CH) for _ in range(2)]
    Pg = [al.b(4096) for _ in range(2)]
    xdt = [al.b(1024) for _ in range(2)]
    xdec = [al.b(1024) for _ in range(2)]
    tseg2b = al.b(1024)
    S_bf = al.b(1024)
    trimask = al.b(128)
    idb = al.b(128)
    ones_b = al.b(2)
    dt2c = al.f(256)
    dAf = al.f(256)
    cumf = al.f(256)
    cumT = [al.f(16) for _ in range(2)]
    dtT = [al.f(16) for _ in range(2)]
    decf = [al.f(16) for _ in range(2)]
    ect = [al.f(16) for _ in range(2)]
    sfac = [al.f(16) for _ in range(2)]
    rexp = al.f(16)
    acol = al.f(1)
    dbc = al.f(16)
    ehs = al.f(2048)
    rrow = al.f(16)
    S_in = al.f(1024)
    yst = [al.f(1024) for _ in range(2)]
    tseg = al.f(1024)
    onesr = al.f(128)
    idf = al.f(128)

    ps_tr = st.enter_context(nc.psum_tensor("ps_tr", [128, 1024], BF16))
    ps_cbt = st.enter_context(nc.psum_tensor("ps_cbt", [128, 512], F32))
    ps_seg = st.enter_context(nc.psum_tensor("ps_seg", [128, 1024], F32))
    ps_yd = st.enter_context(nc.psum_tensor("ps_yd", [128, 1024], F32))
    ps_yo = st.enter_context(nc.psum_tensor("ps_yo", [128, 1024], F32))

    def emit(w):
        w.op("vector", "memset", ones_b[:, 0:1], 1.0)
        w.op("vector", "memset", onesr[0:1, :128], 1.0)
        w.op("vector", "memset", S_in[:, :], 0.0)
        w.fence("vector", ("tensor", "scalar"))
        w.dma("sync", idf, identf, "LD")
        w.dma("sync", acol[0:16, 0:1], alog, "LD")
        w.dma("sync", dbc[:, 0:16], dbc_in, "LD")
        w.dma("sync", ehs[0:16, :], ehsel, "LD")
        w.dma("gpsimd", trimask, trim, "WT")
        w.dma("gpsimd", idb, identb, "WT")
        w.wait("scalar", "LD")
        w.op("scalar", "activation", acol[0:16, 0:1], acol[0:16, 0:1], AF.Exp)
        w.fence("scalar", "vector")
        w.op("vector", "tensor_scalar", acol[0:16, 0:1], acol[0:16, 0:1],
             -1.0, None, op0=ALU.mult)
        w.fence("vector", ("tensor", "scalar"))

        for ch in range(NCH):
            t0 = ch * CH
            w.fence(("tensor", "vector", "scalar"), "gpsimd")
            for c in range(8):
                w.dma("gpsimd", xs[c], xsT[c * 128:(c + 1) * 128,
                                           t0:t0 + CH], "WT")
            w.dma("gpsimd", Bm, BT_d[:, t0:t0 + CH], "WT")
            w.dma("gpsimd", Cm, CT_d[:, t0:t0 + CH], "WT")
            w.fence("vector", "sync")
            w.dma("sync", dt2c[0:16, :], dt2d[:, t0:t0 + CH], "LD")
            w.wait("vector", "LD")
            w.wait("tensor", "WT")
            w.wait("vector", "WT")
            w.op("vector", "tensor_scalar", dAf[0:16, :], dt2c[0:16, :],
                 acol[0:16, 0:1], None, op0=ALU.mult)
            w.fence("vector", "vector")
            w.op("vector", "memset", cumf[0:16, :], 0.0)
            w.fence("vector", "vector")
            w.op("vector", "tensor_tensor_scan", cumf[0:16, :], dAf[0:16, :],
                 cumf[0:16, :], 0.0, op0=ALU.add, op1=ALU.add)
            w.fence("vector", "tensor")
            for sx in range(2):
                w.op("tensor", "transpose", ps_cbt[:, sx * 16:16 + sx * 16],
                     cumf[0:16, sx * 128:(sx + 1) * 128], idf[0:16, 0:16])
                w.op("tensor", "transpose",
                     ps_cbt[:, 32 + sx * 16:48 + sx * 16],
                     dt2c[0:16, sx * 128:(sx + 1) * 128], idf[0:16, 0:16])
            w.op("tensor", "transpose", ps_cbt[0:1, 64:80],
                 cumf[0:16, 255:256], idf[0:16, 0:16])
            w.fence("tensor", "scalar")
            for sx in range(2):
                w.op("scalar", "copy", cumT[sx][:, 0:16],
                     ps_cbt[:, sx * 16:16 + sx * 16])
                w.op("scalar", "copy", dtT[sx][:, 0:16],
                     ps_cbt[:, 32 + sx * 16:48 + sx * 16])
            w.op("scalar", "copy", rrow[0:1, 0:16], ps_cbt[0:1, 64:80])
            w.fence("scalar", "tensor")
            w.op("tensor", "matmul", ps_cbt[:, 80:96], onesr[0:1, :128],
                 rrow[0:1, 0:16], start=True, stop=True)
            w.fence("tensor", ("vector", "scalar"))
            w.op("scalar", "activation", rexp[:, 0:16], ps_cbt[:, 80:96],
                 AF.Exp)
            for sx in range(2):
                w.op("vector", "tensor_tensor", tseg[:, sx * 16:sx * 16 + 16],
                     ps_cbt[:, 80:96], cumT[sx][:, 0:16], op=ALU.subtract)
            w.fence("vector", "scalar")
            for sx in range(2):
                w.op("scalar", "activation", decf[sx][:, 0:16],
                     tseg[:, sx * 16:sx * 16 + 16], AF.Exp)
                w.op("scalar", "activation", ect[sx][:, 0:16],
                     cumT[sx][:, 0:16], AF.Exp)
            w.fence("scalar", "vector")
            for sx in range(2):
                w.op("vector", "tensor_tensor", sfac[sx][:, 0:16],
                     dtT[sx][:, 0:16], decf[sx][:, 0:16], op=ALU.mult)
            w.fence("vector", "vector")
            for sx in range(2):
                for c in range(8):
                    w.op("tensor", "transpose",
                         ps_tr[:, c * 128:(c + 1) * 128],
                         xs[c][:, sx * 128:(sx + 1) * 128], idb)
                w.fence("tensor", "vector")
                w.op("vector", "tensor_copy", xT[sx], ps_tr[:, :])
                w.fence("vector", "tensor")
                w.op("tensor", "transpose", ps_tr[:, 0:128],
                     Bm[:, sx * 128:(sx + 1) * 128], idb)
                w.fence("tensor", "vector")
                w.op("vector", "tensor_copy", BTt[sx], ps_tr[:, 0:128])
                w.fence("vector", "tensor")
            for sx in range(2):
                dv = dtT[sx][:, 0:16, None].broadcast_to([128, 16, 64])
                sv = sfac[sx][:, 0:16, None].broadcast_to([128, 16, 64])
                xtv = xT[sx].rearrange("p (h q) -> p h q", h=16)
                w.op("vector", "tensor_tensor",
                     xdt[sx].rearrange("p (h q) -> p h q", h=16), xtv, dv,
                     op=ALU.mult)
                w.op("vector", "tensor_tensor",
                     xdec[sx].rearrange("p (h q) -> p h q", h=16), xtv, sv,
                     op=ALU.mult)
            w.fence("vector", "tensor")
            for sx in range(2):
                w.op("tensor", "matmul", ps_cbt[:, 128:384],
                     Bm[:, sx * 128:(sx + 1) * 128], Cm[:, :],
                     start=True, stop=True)
                w.fence("tensor", "vector")
                if sx == 0:
                    w.op("vector", "tensor_tensor", CBTm[0][:, 0:128],
                         ps_cbt[:, 128:256], trimask, op=ALU.mult)
                    w.op("vector", "tensor_copy", CBTm[0][:, 128:256],
                         ps_cbt[:, 256:384])
                else:
                    w.op("vector", "memset", CBTm[1][:, 0:128], 0.0)
                    w.op("vector", "tensor_tensor", CBTm[1][:, 128:256],
                         ps_cbt[:, 256:384], trimask, op=ALU.mult)
                w.fence("vector", "tensor")
            for sx in range(2):
                for gq in range(4):
                    for hh2 in range(4):
                        h = gq * 4 + hh2
                        w.op("tensor", "matmul",
                             ps_seg[:, hh2 * 256:(hh2 + 1) * 256],
                             ehs[0:16, h * 128:(h + 1) * 128],
                             cumf[0:16, :], start=True, stop=True)
                    w.fence("tensor", "vector")
                    cv = cumT[sx][:, gq * 4:(gq + 1) * 4, None].broadcast_to(
                        [128, 4, 256])
                    w.op("vector", "tensor_tensor",
                         tseg[:, 0:1024].rearrange("p (h t) -> p h t", h=4),
                         ps_seg[:, 0:1024].rearrange("p (h t) -> p h t", h=4),
                         cv, op=ALU.subtract)
                    w.fence("vector", "vector")
                    w.op("vector", "tensor_scalar", tseg[:, 0:1024],
                         tseg[:, 0:1024], 0.0, None, op0=ALU.min)
                    w.fence("vector", ("scalar", "tensor"))
                    w.op("scalar", "activation", tseg2b[:, 0:1024],
                         tseg[:, 0:1024], AF.Exp)
                    w.fence("scalar", "vector")
                    cbv = CBTm[sx][:, None, :].broadcast_to([128, 4, 256])
                    w.op("vector", "tensor_tensor",
                         Pg[sx][:, gq * 1024:(gq + 1) * 1024].rearrange(
                             "p (h t) -> p h t", h=4),
                         tseg2b[:, 0:1024].rearrange("p (h t) -> p h t", h=4),
                         cbv, op=ALU.mult)
                    w.fence("vector", ("tensor", "scalar"))
            w.op("vector", "tensor_copy", S_bf, S_in[:, :])
            w.fence("vector", "tensor")
            for tq in range(2):
                w.op("tensor", "matmul", ps_yo[:, 0:512],
                     Cm[:, tq * 128:(tq + 1) * 128], S_bf[:, 0:512],
                     start=True, stop=True)
                w.op("tensor", "matmul", ps_yo[:, 512:1024],
                     Cm[:, tq * 128:(tq + 1) * 128], S_bf[:, 512:1024],
                     start=True, stop=True)
                for h in range(16):
                    for sx in range(2):
                        w.op("tensor", "matmul",
                             ps_yd[:, h * 64:(h + 1) * 64],
                             Pg[sx][:, h * 256 + tq * 128:
                                    h * 256 + tq * 128 + 128],
                             xdt[sx][:, h * 64:(h + 1) * 64],
                             start=(sx == 0), stop=(sx == 1))
                w.fence("tensor", "vector")
                ev = ect[tq][:, 0:16, None].broadcast_to([128, 16, 64])
                w.op("vector", "tensor_tensor",
                     yst[tq].rearrange("p (h q) -> p h q", h=16),
                     ps_yo[:, :].rearrange("p (h q) -> p h q", h=16), ev,
                     op=ALU.mult)
                w.fence("vector", "vector")
                w.op("vector", "tensor_tensor", yst[tq], yst[tq],
                     ps_yd[:, :], op=ALU.add)
                w.fence("vector", "vector")
                dvv = dbc[:, 0:16, None].broadcast_to([128, 16, 64])
                w.op("vector", "tensor_tensor",
                     tseg[:, 0:1024].rearrange("p (h q) -> p h q", h=16),
                     xT[tq].rearrange("p (h q) -> p h q", h=16), dvv,
                     op=ALU.mult)
                w.fence("vector", "vector")
                w.op("vector", "tensor_tensor", yst[tq], yst[tq],
                     tseg[:, 0:1024], op=ALU.add)
                w.fence("vector", ("sync", "tensor"))
                w.dma("sync", yTok[t0 + tq * 128:t0 + (tq + 1) * 128, :],
                      yst[tq], "ST")
            for sx in range(2):
                w.op("tensor", "matmul", ps_yd[:, 0:512], BTt[sx],
                     xdec[sx][:, 0:512], start=(sx == 0), stop=(sx == 1))
            for sx in range(2):
                w.op("tensor", "matmul", ps_yd[:, 512:1024], BTt[sx],
                     xdec[sx][:, 512:1024], start=(sx == 0), stop=(sx == 1))
            w.fence("tensor", "vector")
            rv = rexp[:, 0:16, None].broadcast_to([128, 16, 64])
            w.op("vector", "tensor_tensor",
                 S_in[:, :].rearrange("p (h q) -> p h q", h=16),
                 S_in[:, :].rearrange("p (h q) -> p h q", h=16), rv,
                 op=ALU.mult)
            w.fence("vector", "vector")
            w.op("vector", "tensor_tensor", S_in[:, :], S_in[:, :],
                 ps_yd[:, :], op=ALU.add)
            w.fence("vector", "tensor")
        w.wait("sync", "ST")
        w.wait("sync", "LD")
        w.wait("sync", "WT")

    run_walks(nc, st, emit)
    st.close()
    return nc


def l1a_a_inputs(x2_slices, in_proj_w, conv_w, conv_b, dt_bias):
    wpT = np.ascontiguousarray(np.asarray(in_proj_w).T[:, 4096:8512])
    cw = np.asarray(conv_w)
    cwr = np.ascontiguousarray(
        cw.reshape(34, 128, 4).transpose(1, 0, 2).reshape(128, 136))
    cbr = np.ascontiguousarray(np.asarray(conv_b).reshape(34, 128).T)
    dtb = np.asarray(dt_bias).reshape(64, 1).astype(np.float32)
    ins = []
    for c in range(NCORES):
        b, q = c // 4, c % 4
        tok0 = q * NTOK
        xh = np.zeros((D, 4), dtype=np.float32)
        if tok0 > 0:
            xh[:, 1:4] = x2_slices[c - 1][:, NTOK - 3:NTOK]
        ins.append({"xT": x2_slices[c], "xh": xh, "wpT": wpT,
                    "cwr": cwr, "cbr": cbr, "dtb": dtb})
    return ins


def l1a_b_inputs(xBC_full, dt2_full, A_log, Dv):
    ident = np.eye(128)
    ehsel = np.zeros((16, 2048), dtype=np.float32)
    for h in range(16):
        ehsel[h, h * 128:(h + 1) * 128] = 1.0
    tri = np.tril(np.ones((128, 128), dtype=np.float32)).T
    ins = []
    for c in range(NCORES):
        b, q = c // 4, c % 4
        h0 = 16 * q
        ins.append({
            "xsT": np.ascontiguousarray(xBC_full[b][h0 * 64:h0 * 64 + 1024]),
            "BT_d": np.ascontiguousarray(xBC_full[b][4096:4224]),
            "CT_d": np.ascontiguousarray(xBC_full[b][4224:4352]),
            "dt2d": np.ascontiguousarray(dt2_full[b][h0:h0 + 16]),
            "alog": np.asarray(A_log)[h0:h0 + 16].reshape(16, 1).astype(np.float32),
            "dbc_in": np.tile(np.asarray(Dv)[h0:h0 + 16].astype(np.float32),
                              (128, 1)),
            "ehsel": ehsel,
            "trim": tri.astype(BF),
            "identf": ident.astype(np.float32),
            "identb": ident.astype(BF),
        })
    return ins


def build_l1a_c(dbg=False):
    nc = bass.Bass("TRN2", target_bir_lowering=False)
    TT = 512

    xT = nc.dram_tensor("xT", [D, NTOK], F32, kind="ExternalInput").ap()
    yT = nc.dram_tensor("yT", [4096, NTOK], F32, kind="ExternalInput").ap()
    gpT = nc.dram_tensor("gpT", [D, 4096], F32, kind="ExternalInput").ap()
    opT = nc.dram_tensor("opT", [4096, D], F32, kind="ExternalInput").ap()
    nwr = nc.dram_tensor("nwr", [128, 32], F32, kind="ExternalInput").ap()
    x3T = nc.dram_tensor("x3T", [D, NTOK], F32, kind="ExternalOutput").ap()

    st = contextlib.ExitStack()
    al = Alloc(nc, st, nb_cols_bf16=62000, nf_cols_f32=6000)
    src = [al.b(TT) for _ in range(16)]
    hg = [al.b(TT) for _ in range(32)]
    o2 = [al.b(TT) for _ in range(16)]
    wbuf = [al.b(D) for _ in range(2)]
    wbuf2 = [al.b(4096) for _ in range(2)]
    sg = al.b(TT)
    yb = al.b(TT)
    bcR = al.b(TT)
    sqbuf = al.b(1024)
    bcA = al.b(512)
    bcB = al.b(512)
    tmpb = al.b(512)
    ones_b = al.b(2)
    xt = [al.f(512) for _ in range(2)]
    rows = al.f(2048)
    onesr = al.f(128)
    nw = al.f(32)
    outst = al.f(512)

    ps_mm = st.enter_context(nc.psum_tensor("ps_mm", [128, 512], F32))
    ps_st = st.enter_context(nc.psum_tensor("ps_st", [64, 512], F32))
    ps_bA = st.enter_context(nc.psum_tensor("ps_bA", [128, 512], F32))
    ps_bB = st.enter_context(nc.psum_tensor("ps_bB", [128, 512], F32))
    env = dict(ones_b=ones_b, onesr=onesr, sqbuf=sqbuf, rows=rows, bcA=bcA,
               bcB=bcB, tmpb=tmpb, ps_st=ps_st, ps_bA=ps_bA, ps_bB=ps_bB)

    def emit(w):
        w.op("vector", "memset", ones_b[:, 0:1], 1.0)
        w.op("vector", "memset", onesr[0:1, :128], 1.0)
        w.fence("vector", ("tensor", "scalar"))
        w.dma("sync", nw[:, 0:32], nwr, "LD")
        w.wait("vector", "LD")

        for tt in range(4):
            c0 = tt * TT
            w.fence(("tensor", "vector", "scalar"), "gpsimd")
            for c in range(16):
                w.dma("gpsimd", src[c], xT[c * 128:(c + 1) * 128,
                                           c0:c0 + TT], "WT")
            w.wait("tensor", "WT")
            w.wait("scalar", "WT")
            w.wait("vector", "WT")
            ln_tile_g(w, env, src, src, TT)
            for j in range(32):
                w.fence("tensor", "gpsimd")
                w.dma("gpsimd", wbuf[j % 2].rearrange("p (c j) -> p c j", c=16),
                      gpT[:, j * 128:(j + 1) * 128].rearrange(
                          "(c p) j -> p c j", p=128), "WT")
                w.fence("vector", "gpsimd")
                w.dma("gpsimd", yb, yT[j * 128:(j + 1) * 128, c0:c0 + TT],
                      "WT")
                w.wait("tensor", "WT")
                for kc in range(16):
                    w.op("tensor", "matmul", ps_mm[:, :],
                         wbuf[j % 2][:, kc * 128:(kc + 1) * 128], src[kc],
                         start=(kc == 0), stop=(kc == 15))
                w.fence("tensor", "scalar")
                w.op("scalar", "activation", sg, ps_mm[:, :], AF.Silu)
                w.fence("scalar", ("vector", "tensor"))
                w.wait("vector", "WT")
                w.op("vector", "tensor_tensor", hg[j], yb, sg, op=ALU.mult)
                w.fence("vector", ("tensor", "gpsimd"))
            for j in range(32):
                sq = sqbuf[:, (j % 2) * 512:(j % 2) * 512 + TT]
                if j >= 2:
                    w.fence("tensor", "scalar")
                w.op("scalar", "activation", sq, hg[j], AF.Square)
                w.fence("scalar", "tensor")
                w.op("tensor", "matmul", ps_st[0:1, :TT], ones_b[:, 0:1], sq,
                     start=(j == 0), stop=(j == 31))
            w.fence("tensor", "vector")
            m = rows[0:1, 0:TT]
            w.op("vector", "tensor_scalar", m, ps_st[0:1, :TT], 1.0 / 4096,
                 None, op0=ALU.mult)
            w.fence("vector", "vector")
            w.op("vector", "tensor_scalar", m, m, 1e-5, None, op0=ALU.add)
            w.fence("vector", "scalar")
            w.op("scalar", "activation", m, m, AF.Sqrt)
            w.fence("scalar", "vector")
            w.op("vector", "reciprocal", m, m)
            w.fence("vector", "tensor")
            w.op("tensor", "matmul", ps_bA[:, :TT], onesr[0:1, :128], m,
                 start=True, stop=True)
            w.fence("tensor", "scalar")
            w.op("scalar", "copy", bcR, ps_bA[:, :TT])
            w.fence("scalar", "vector")
            for j in range(32):
                w.op("vector", "tensor_tensor", hg[j], hg[j], bcR,
                     op=ALU.mult)
                w.fence("vector", "vector")
                w.op("vector", "tensor_scalar", hg[j], hg[j],
                     nw[:, j:j + 1], None, op0=ALU.mult)
                w.fence("vector", "vector")
            w.fence("vector", "tensor")
            for j2 in range(16):
                w.fence("tensor", "gpsimd")
                w.dma("gpsimd",
                      wbuf2[j2 % 2].rearrange("p (c j) -> p c j", c=32),
                      opT[:, j2 * 128:(j2 + 1) * 128].rearrange(
                          "(c p) j -> p c j", p=128), "WT")
                w.wait("tensor", "WT")
                for kc in range(32):
                    w.op("tensor", "matmul", ps_mm[:, :],
                         wbuf2[j2 % 2][:, kc * 128:(kc + 1) * 128], hg[kc],
                         start=(kc == 0), stop=(kc == 31))
                w.fence("tensor", "scalar")
                w.op("scalar", "copy", o2[j2], ps_mm[:, :])
                w.fence("scalar", ("tensor", "vector"))
            ln_tile_g(w, env, o2, o2, TT)
            for c in range(16):
                par = c % 2
                if c >= 2:
                    w.fence("vector", "sync")
                w.dma("sync", xt[par][:, :], xT[c * 128:(c + 1) * 128,
                                               c0:c0 + TT], "LD")
                w.wait("vector", "LD")
                w.wait("vector", "ST")
                w.op("vector", "tensor_tensor", outst, xt[par][:, :], o2[c],
                     op=ALU.add)
                w.fence("vector", "sync")
                w.dma("sync", x3T[c * 128:(c + 1) * 128, c0:c0 + TT],
                      outst, "ST")
        w.wait("sync", "ST")
        w.wait("sync", "LD")
        w.wait("sync", "WT")

    run_walks(nc, st, emit)
    st.close()
    return nc


def l1a_c_inputs(x2_slices, y_slices, in_proj_w, norm_w, out_proj_w):
    gpT = np.ascontiguousarray(np.asarray(in_proj_w).T[:, 0:4096])
    opT = np.ascontiguousarray(np.asarray(out_proj_w).T)
    nwr = np.ascontiguousarray(np.asarray(norm_w).reshape(32, 128).T)
    ins = []
    for c in range(NCORES):
        ins.append({"xT": x2_slices[c], "yT": y_slices[c],
                    "gpT": gpT, "opT": opT, "nwr": nwr})
    return ins


def build_l1f(dbg=False):
    nc = bass.Bass("TRN2", target_bir_lowering=False)
    TT = 512
    S2048 = 1.0 / float(np.sqrt(2048.0))
    S8192 = 1.0 / float(np.sqrt(8192.0))

    xT = nc.dram_tensor("xT", [D, NTOK], F32, kind="ExternalInput").ap()
    WupT = nc.dram_tensor("WupT", [D, 8192], F32, kind="ExternalInput").ap()
    WgtT = nc.dram_tensor("WgtT", [D, 8192], F32, kind="ExternalInput").ap()
    WdnT = nc.dram_tensor("WdnT", [8192, D], F32, kind="ExternalInput").ap()
    suug = nc.dram_tensor("suug", [128, 16], F32, kind="ExternalInput").ap()
    sudp = nc.dram_tensor("sudp", [128, 64], F32, kind="ExternalInput").ap()
    wsup = nc.dram_tensor("wsup", [128, 64], F32, kind="ExternalInput").ap()
    wsgt = nc.dram_tensor("wsgt", [128, 64], F32, kind="ExternalInput").ap()
    wsdp = nc.dram_tensor("wsdp", [128, 16], F32, kind="ExternalInput").ap()
    h128 = nc.dram_tensor("h128", [128, 128], BF16, kind="ExternalInput").ap()
    x4T = nc.dram_tensor("x4T", [D, NTOK], F32, kind="ExternalOutput").ap()
    if dbg:
        z_dbg = nc.dram_tensor("z_dbg", [D, TT], F32, kind="ExternalOutput").ap()
        u_dbg = nc.dram_tensor("u_dbg", [8192, TT], F32, kind="ExternalOutput").ap()
        wf_dbg = nc.dram_tensor("wf_dbg", [8192, TT], F32, kind="ExternalOutput").ap()
        o_dbg = nc.dram_tensor("o_dbg", [D, TT], F32, kind="ExternalOutput").ap()

    st = contextlib.ExitStack()
    al = Alloc(nc, st, nb_cols_bf16=86000, nf_cols_f32=6000)
    srcO = al.b(32 * TT)
    src = [srcO[:, c * TT:(c + 1) * TT] for c in range(16)]
    o2 = [srcO[:, (16 + c) * TT:(17 + c) * TT] for c in range(16)]
    uscr = srcO
    zA = [al.b(TT) for _ in range(16)]
    zB = [al.b(TT) for _ in range(16)]
    ub = al.b(64 * TT)
    wbuf = [al.b(D) for _ in range(2)]
    wbufg = [al.b(D) for _ in range(2)]
    wbuf2 = [al.b(4096) for _ in range(2)]
    sgb = al.b(TT)
    h128b = al.b(128)
    sqbuf = al.b(1024)
    bcA = al.b(512)
    bcB = al.b(512)
    tmpb = al.b(512)
    ones_b = al.b(2)
    xt = [al.f(512) for _ in range(2)]
    rows = al.f(2048)
    onesr = al.f(128)
    suugf = al.f(16)
    sudpf = al.f(64)
    wsupf = al.f(64)
    wsgtf = al.f(64)
    wsdpf = al.f(16)
    outst = al.f(512)

    ps_up = st.enter_context(nc.psum_tensor("ps_up", [128, 512], F32))
    ps_gt = st.enter_context(nc.psum_tensor("ps_gt", [128, 512], F32))
    ps_h = st.enter_context(nc.psum_tensor("ps_h", [128, 512], F32))
    ps_st = st.enter_context(nc.psum_tensor("ps_st", [64, 512], F32))
    ps_bA = st.enter_context(nc.psum_tensor("ps_bA", [128, 512], F32))
    ps_bB = st.enter_context(nc.psum_tensor("ps_bB", [128, 512], F32))
    env = dict(ones_b=ones_b, onesr=onesr, sqbuf=sqbuf, rows=rows, bcA=bcA,
               bcB=bcB, tmpb=tmpb, ps_st=ps_st, ps_bA=ps_bA, ps_bB=ps_bB)

    def emit(w):
        w.op("vector", "memset", ones_b[:, 0:1], 1.0)
        w.op("vector", "memset", onesr[0:1, :128], 1.0)
        w.fence("vector", ("tensor", "scalar"))
        w.dma("sync", suugf[:, 0:16], suug, "LD")
        w.dma("sync", sudpf[:, 0:64], sudp, "LD")
        w.dma("sync", wsupf[:, 0:64], wsup, "LD")
        w.dma("sync", wsgtf[:, 0:64], wsgt, "LD")
        w.dma("sync", wsdpf[:, 0:16], wsdp, "LD")
        w.dma("gpsimd", h128b, h128, "WT")
        w.wait("vector", "LD")
        w.wait("scalar", "LD")
        w.wait("tensor", "WT")

        def butterfly(chunks_view, nch, scrv, width):
            import math
            nb = int(math.log2(nch))
            for b in range(nb):
                step = 1 << b
                nhi = nch // (2 * step)
                va = chunks_view.rearrange(
                    "p (hi two lo t) -> p hi two lo t", hi=nhi, two=2,
                    lo=step)
                a0 = va[:, :, 0, :, :]
                a1 = va[:, :, 1, :, :]
                sv = scrv.rearrange("p (hi lo t) -> p hi lo t", hi=nhi,
                                    lo=step)
                w.op("vector", "tensor_tensor", sv, a0, a1, op=ALU.subtract)
                w.fence("vector", "vector")
                w.op("vector", "tensor_tensor", a0, a0, a1, op=ALU.add)
                w.fence("vector", "vector")
                w.op("vector", "tensor_copy", a1, sv)
                w.fence("vector", "vector")

        for tt in range(4):
            c0 = tt * TT
            w.fence(("tensor", "vector", "scalar"), "gpsimd")
            for c in range(16):
                w.dma("gpsimd", src[c], xT[c * 128:(c + 1) * 128,
                                           c0:c0 + TT], "WT")
            w.wait("tensor", "WT")
            w.wait("scalar", "WT")
            w.wait("vector", "WT")
            ln_tile_g(w, env, src, src, TT)
            za, zb = zA, zB
            for c in range(16):
                w.op("vector", "tensor_scalar", za[c], src[c],
                     suugf[:, c:c + 1], S2048, op0=ALU.mult, op1=ALU.mult)
            w.fence("vector", "tensor")
            for c in range(16):
                w.op("tensor", "matmul", ps_h[:, :], h128b, za[c],
                     start=True, stop=True)
                w.fence("tensor", "scalar")
                w.op("scalar", "copy", zb[c], ps_h[:, :])
                w.fence("scalar", "tensor")
            w.fence("scalar", "vector")
            for b in range(4):
                step = 1 << b
                for i in range(16):
                    if i & step:
                        continue
                    j = i | step
                    w.op("vector", "tensor_tensor", za[i], zb[i], zb[j],
                         op=ALU.add)
                    w.op("vector", "tensor_tensor", za[j], zb[i], zb[j],
                         op=ALU.subtract)
                w.fence("vector", "vector")
                za, zb = zb, za
            z = zb
            w.fence("vector", ("tensor", "scalar"))
            if dbg and tt == 0:
                w.fence("vector", "gpsimd")
                for c in range(16):
                    w.dma("gpsimd", z_dbg[c * 128:(c + 1) * 128, :], z[c],
                          "GD")
            for j in range(64):
                par = j % 2
                w.fence("tensor", "gpsimd")
                w.dma("gpsimd", wbuf[par].rearrange("p (c j) -> p c j", c=16),
                      WupT[:, j * 128:(j + 1) * 128].rearrange(
                          "(c p) j -> p c j", p=128), "WT")
                w.dma("gpsimd", wbufg[par].rearrange("p (c j) -> p c j", c=16),
                      WgtT[:, j * 128:(j + 1) * 128].rearrange(
                          "(c p) j -> p c j", p=128), "WT")
                w.wait("tensor", "WT")
                for kc in range(16):
                    w.op("tensor", "matmul", ps_up[:, :],
                         wbuf[par][:, kc * 128:(kc + 1) * 128], z[kc],
                         start=(kc == 0), stop=(kc == 15))
                for kc in range(16):
                    w.op("tensor", "matmul", ps_gt[:, :],
                         wbufg[par][:, kc * 128:(kc + 1) * 128], z[kc],
                         start=(kc == 0), stop=(kc == 15))
                w.fence("tensor", "scalar")
                w.op("scalar", "activation", sgb, ps_gt[:, :], AF.Silu,
                     scale=wsgtf[:, j:j + 1])
                w.fence("scalar", "vector")
                uv = ub[:, j * TT:(j + 1) * TT]
                w.op("vector", "scalar_tensor_tensor", uv, ps_up[:, :],
                     wsupf[:, j:j + 1], sgb, op0=ALU.mult, op1=ALU.mult)
                w.fence("vector", "vector")
                w.op("vector", "tensor_scalar", uv, uv, sudpf[:, j:j + 1],
                     S8192, op0=ALU.mult, op1=ALU.mult)
                w.fence("vector", ("tensor", "scalar"))
            if dbg and tt == 0:
                w.fence("vector", "gpsimd")
                w.dma("gpsimd", u_dbg.rearrange("(a p) t -> p a t", p=128),
                      ub.rearrange("p (a t) -> p a t", a=64), "GD")
                w.wait("tensor", "GD")
                w.wait("scalar", "GD")
            for c in range(64):
                uv = ub[:, c * TT:(c + 1) * TT]
                w.op("tensor", "matmul", ps_h[:, :], h128b, uv,
                     start=True, stop=True)
                w.fence("tensor", "scalar")
                w.op("scalar", "copy", uv, ps_h[:, :])
                w.fence("scalar", ("tensor", "vector"))
            butterfly(ub[:, :], 64, uscr[:, :], TT)
            if dbg and tt == 0:
                w.fence("vector", "gpsimd")
                w.dma("gpsimd", wf_dbg.rearrange("(a p) t -> p a t", p=128),
                      ub.rearrange("p (a t) -> p a t", a=64), "GD")
                w.wait("tensor", "GD")
                w.wait("vector", "GD")
            w.fence("vector", "tensor")
            for j2 in range(16):
                par = j2 % 2
                w.fence("tensor", "gpsimd")
                w.dma("gpsimd",
                      wbuf2[par].rearrange("p (c j) -> p c j", c=32),
                      WdnT[0:4096, j2 * 128:(j2 + 1) * 128].rearrange(
                          "(c p) j -> p c j", p=128), "WT")
                w.wait("tensor", "WT")
                for kc in range(32):
                    w.op("tensor", "matmul", ps_up[:, :],
                         wbuf2[par][:, kc * 128:(kc + 1) * 128],
                         ub[:, kc * TT:(kc + 1) * TT],
                         start=(kc == 0), stop=False)
                w.fence("tensor", "gpsimd")
                w.dma("gpsimd",
                      wbuf2[par].rearrange("p (c j) -> p c j", c=32),
                      WdnT[4096:8192, j2 * 128:(j2 + 1) * 128].rearrange(
                          "(c p) j -> p c j", p=128), "WT")
                w.wait("tensor", "WT")
                for kc in range(32):
                    w.op("tensor", "matmul", ps_up[:, :],
                         wbuf2[par][:, kc * 128:(kc + 1) * 128],
                         ub[:, (32 + kc) * TT:(33 + kc) * TT],
                         start=False, stop=(kc == 31))
                w.fence("tensor", "vector")
                w.op("vector", "tensor_scalar", o2[j2], ps_up[:, :],
                     wsdpf[:, j2:j2 + 1], None, op0=ALU.mult)
                w.fence("vector", ("tensor", "scalar"))
            if dbg and tt == 0:
                w.fence("vector", "gpsimd")
                for j2 in range(16):
                    w.dma("gpsimd", o_dbg[j2 * 128:(j2 + 1) * 128, :],
                          o2[j2], "GD")
            ln_tile_g(w, env, o2, o2, TT)
            for c in range(16):
                par = c % 2
                if c >= 2:
                    w.fence("vector", "sync")
                w.dma("sync", xt[par][:, :], xT[c * 128:(c + 1) * 128,
                                               c0:c0 + TT], "LD")
                w.wait("vector", "LD")
                w.wait("vector", "ST")
                w.op("vector", "tensor_tensor", outst, xt[par][:, :], o2[c],
                     op=ALU.add)
                w.fence("vector", "sync")
                w.dma("sync", x4T[c * 128:(c + 1) * 128, c0:c0 + TT],
                      outst, "ST")
        w.wait("sync", "ST")
        w.wait("sync", "LD")
        w.wait("sync", "WT")
        w.wait("sync", "GD")

    run_walks(nc, st, emit)
    st.close()
    return nc


def l1f_inputs(x3_slices, SU_ug, SU_dp, Wscale_ug, Wscale_dp, W_up, W_gate,
               W_down):
    WupT = np.ascontiguousarray(np.asarray(W_up).T)
    WgtT = np.ascontiguousarray(np.asarray(W_gate).T)
    WdnT = np.ascontiguousarray(np.asarray(W_down).T)
    pack = lambda v, n: np.ascontiguousarray(
        np.asarray(v).reshape(n, 128).T.astype(np.float32))
    wsu = np.asarray(Wscale_ug)
    Hm = np.array([[1.0]])
    while Hm.shape[0] < 128:
        Hm = np.block([[Hm, Hm], [Hm, -Hm]])
    ins = []
    for c in range(NCORES):
        ins.append({
            "xT": x3_slices[c], "WupT": WupT, "WgtT": WgtT, "WdnT": WdnT,
            "suug": pack(SU_ug, 16), "sudp": pack(SU_dp, 64),
            "wsup": pack(wsu[:8192], 64), "wsgt": pack(wsu[8192:], 64),
            "wsdp": pack(Wscale_dp, 16), "h128": Hm.astype(BF),
        })
    return ins


BAND2 = 2688
KVLEN2 = BAND2 + FRONT


def build_fused(upto=99, dbg=False, p4chunks=8, p4parts=7, p8f=7, ugn=2):
    nc = bass.Bass("TRN2", target_bir_lowering=False)
    TT = 512
    S2048 = 1.0 / float(np.sqrt(2048.0))
    S8192 = 1.0 / float(np.sqrt(8192.0))

    def din(name, shape, dt=F32):
        return nc.dram_tensor(name, shape, dt, kind="ExternalInput").ap()

    x_band = din("x_band", [D, BAND2])
    x_front = din("x_front", [D, FRONT])
    wq = din("wq", [D, D]); wk = din("wk", [D, 512])
    wv = din("wv", [D, 512]); wo = din("wo", [D, D])
    cs_band = din("cs_band", [128, BAND2])
    cs2_band = din("cs2_band", [128, BAND2])
    cs_front = din("cs_front", [128, FRONT])
    cs2_front = din("cs2_front", [128, FRONT])
    cs_q = din("cs_q", [128, 2176]); cs2_q = din("cs2_q", [128, 2176])
    masks = din("masks", [17, 128, 896], BF16)
    identb = din("identb", [128, 128], BF16)
    identf = din("identf", [128, 128], F32)
    W1 = din("W1", [D, D]); b1r = din("b1r", [128, 16])
    W2 = din("W2", [D, 8]); b2c = din("b2c", [8, 1])
    VPd = din("VP", [4, D, 128]); UPd = din("UP", [4, 128, D])
    SELd = din("SEL", [8, 512])
    wpT = din("wpT", [D, 4416])
    cwr = din("cwr", [128, 136]); cbr = din("cbr", [128, 34])
    dtb = din("dtb", [64, 1])
    hmaskd = din("hmask", [128, 1])
    alogd = din("alog", [64, 1])
    dbc_in = din("dbc_in", [128, 64])
    ehseld = din("ehsel", [64, 8192])
    trim = din("trim", [128, 128], BF16)
    mv1d = din("mv1", [128, 3]); mv2d = din("mv2", [128, 3])
    gpT = din("gpT", [D, 4096]); opT = din("opT", [4096, D])
    nwr = din("nwr", [128, 32])
    WupT = din("WupT", [D, 8192]); WgtT = din("WgtT", [D, 8192])
    WdnT = din("WdnT", [8192, D])
    suugd = din("suug", [128, 16]); sudpd = din("sudp", [128, 64])
    wsupd = din("wsup", [128, 64]); wsgtd = din("wsgt", [128, 64])
    wsdpd = din("wsdp", [128, 16])
    h128d = din("h128", [128, 128], BF16)
    outT = nc.dram_tensor("outT", [D, NTOK], F32, kind="ExternalOutput").ap()
    if dbg:
        ydbg = nc.dram_tensor("ydbg", [NTOK, 4096], F32,
                              kind="ExternalOutput").ap()
        xsdbg = nc.dram_tensor("xsdbg", [4096, NTOK], F32,
                               kind="ExternalOutput").ap()
        pdbg = nc.dram_tensor("pdbg", [8, 128, 4096], F32,
                              kind="ExternalOutput").ap()
        u8dbg = nc.dram_tensor("u8dbg", [8192, 512], F32,
                               kind="ExternalOutput").ap()
    xs_scr = nc.dram_tensor("xs_scr", [4096, NTOK], BF16).ap()
    wpT_b = nc.dram_tensor("wpT_b", [D, 4416], BF16).ap()
    gpT_b = nc.dram_tensor("gpT_b", [D, 4096], BF16).ap()
    opT_b = nc.dram_tensor("opT_b", [4096, D], BF16).ap()
    WupT_b = nc.dram_tensor("WupT_b", [64, 128, 2048], BF16).ap()
    WgtT_b = nc.dram_tensor("WgtT_b", [64, 128, 2048], BF16).ap()
    WdnT_b = nc.dram_tensor("WdnT_b", [16, 2, 128, 4096], BF16).ap()
    y_scr = nc.dram_tensor("y_scr", [NTOK, 4096], BF16).ap()
    sx_in = nc.dram_tensor("sx_in", [128, 4160], F32)
    sx_out = nc.dram_tensor("sx_out", [512, 4160], F32)

    st = contextlib.ExitStack()

    class Alloc2:
        def __init__(self, nc, st, nb):
            self.ab = st.enter_context(
                nc.sbuf_tensor("arena", [128, nb], BF16))
            self.ob = 0
            self.nb = nb

        def b(self, w):
            self.ob = (self.ob + 15) & ~15
            assert self.ob + w <= self.nb, f"arena overflow {self.ob}+{w}"
            v = self.ab[:, self.ob:self.ob + w]
            self.ob += w
            return v

        def f(self, w):
            self.ob = (self.ob + 15) & ~15
            v = self.ab[:, self.ob:self.ob + 2 * w].bitcast(F32)
            self.ob += 2 * w
            return v

        def save(self):
            return self.ob

        def restore(self, s):
            self.ob = s

    al = Alloc2(nc, st, 106400)
    resid = [al.b(NTOK) for _ in range(16)]
    idb = al.b(128)
    trimask = al.b(128)
    h128b = al.b(128)
    ones_b = al.b(2)
    sqbuf = al.b(1024)
    bcA = al.b(512)
    bcB = al.b(512)
    tmpb = al.b(512)
    onesr_f = al.f(128)
    idf = al.f(128)
    rows = al.f(2048)
    mPerm = al.save()
    x2h = [al.b(128) for _ in range(16)]
    mA = al.save()
    Bres = al.b(NTOK)
    Cres = al.b(NTOK)
    pc_all = al.b(8 * 64)
    ect_all = al.b(8 * 2 * 64)
    dt2tok = al.f(16 * 64)
    S_in = al.f(4096)
    pc_cur = al.f(64)
    mB = al.save()

    ps_f = st.enter_context(nc.psum_tensor("ps_f", [128, 3584], F32))
    ps_b = st.enter_context(nc.psum_tensor("ps_b", [128, 1024], BF16))
    SL = [ps_f[:, i * 512:(i + 1) * 512] for i in range(7)]
    env = dict(ones_b=ones_b, onesr=onesr_f, sqbuf=sqbuf, rows=rows, bcA=bcA,
               bcB=bcB, tmpb=tmpb, ps_st=ps_f[0:64, 3072:3584],
               ps_bA=SL[4], ps_bB=SL[5])

    al.restore(mA)
    hb = [al.b(TT) for _ in range(16)]
    K1 = [al.b(KVLEN2) for _ in range(4)]
    VT1 = [al.b(KVLEN2) for _ in range(4)]
    q2 = [al.b(TT) for _ in range(16)]
    ao2 = [al.b(TT) for _ in range(16)]
    wbuf1 = [al.b(D) for _ in range(2)]
    cs_t = al.b(512)
    cs_t2 = al.b(512)
    scr1 = al.b(1024)
    Pt = al.b(896)
    PTs = al.b(896)
    mskb = al.b(896)
    scm = al.f(896)
    cols1 = al.f(8)

    al.restore(mA)
    hb2 = [al.b(TT) for _ in range(16)]
    hid = [al.b(TT) for _ in range(16)]
    o22 = [al.b(TT) for _ in range(16)]
    vx = [al.b(TT) for _ in range(4)]
    wsc = [al.b(TT) for _ in range(4)]
    wbuf2f = [al.b(D) for _ in range(2)]
    vpw = [al.b(D) for _ in range(4)]
    upw = [al.b(D) for _ in range(4)]
    w2w = al.b(8 * 16)
    sa2 = al.f(512)
    at2 = al.f(32)
    am2 = al.f(4)
    asum2 = al.f(4)
    alf2 = al.f(512)
    b1s = al.f(16)
    b2s = al.f(1)
    selw = al.f(512)

    al.restore(mB)
    a_srcP = al.b(16 * TT)
    a_src = [a_srcP[:, c * TT:(c + 1) * TT] for c in range(16)]
    a_wbuf = [al.b(D) for _ in range(2)]
    projx2 = [al.b(516) for _ in range(2)]
    xbc32 = [al.b(TT) for _ in range(2)]
    acc32f = 0
    hh3 = [al.b(4) for _ in range(16)]
    cwf = al.f(136)
    cbf = al.f(34)
    dtbc = al.f(1)
    tails = al.f(102)
    acc3 = al.f(512)
    acc32 = [al.f(512) for _ in range(2)]
    dtx3 = al.f(512)
    hmk = al.f(1)

    al.restore(mB)
    xT4 = [al.b(4096) for _ in range(2)]
    BTt = [al.b(128) for _ in range(2)]
    CBTm4 = [al.b(256) for _ in range(2)]
    Pg4 = [al.b(4096) for _ in range(2)]
    xdt4 = [al.b(1024) for _ in range(2)]
    xdec4 = [al.b(1024) for _ in range(2)]
    tseg2b = al.b(1024)
    S_bf4 = al.b(1024)
    dAf4 = al.f(256)
    cumf4 = al.f(256)
    cumT44 = [al.f(64) for _ in range(2)]
    decf44 = [al.f(64) for _ in range(2)]
    ect44 = [al.f(64) for _ in range(2)]
    sfac44 = [al.f(64) for _ in range(2)]
    rexp4 = al.f(64)
    acol4 = al.f(1)
    dbc44 = al.f(64)
    rrow4 = al.f(64)
    tseg4 = al.f(1024)
    ehs4 = al.f(2048)
    yst4 = al.f(1024)
    mP4 = al.save()

    al.restore(mB)
    xtmp5 = al.f(4096)
    lam5 = al.f(64)
    mv1s = al.f(3)
    mv2s = al.f(3)
    sc6 = al.b(4096)
    yrmw = al.b(1024)
    corr6 = al.f(1024)

    al.restore(mPerm)
    hb7P = al.b(16 * TT)
    hb7 = [hb7P[:, c * TT:(c + 1) * TT] for c in range(16)]
    hg7P = al.b(32 * TT)
    hg7 = [hg7P[:, c * TT:(c + 1) * TT] for c in range(32)]
    c_wbuf = [al.b(D) for _ in range(2)]
    c_wbuf2 = [al.b(4096) for _ in range(2)]
    sg7 = al.b(TT)
    sg7b = al.b(TT)
    yb7 = al.b(TT)
    yb7b = al.b(TT)
    bcR7 = al.b(TT)
    o27 = [al.b(TT) for _ in range(16)]
    nw7 = al.f(32)

    al.restore(mPerm)
    srcO8 = al.b(32 * TT)
    src8 = [srcO8[:, c * TT:(c + 1) * TT] for c in range(16)]
    o28 = [srcO8[:, (16 + c) * TT:(17 + c) * TT] for c in range(16)]
    uscr8 = srcO8
    ub8 = al.b(64 * TT)
    f_wbufA = [al.b(D) for _ in range(2)]
    _g = al.b(D)
    f_wbufG = [_g, _g]
    f_wbuf2 = [al.b(4096) for _ in range(2)]
    sgb8 = al.b(TT)
    sgb8b = al.b(TT)
    suugf = al.f(16)
    sudpf = al.f(64)
    wsupf = al.f(64)
    wsgtf = al.f(64)
    wsdpf = al.f(16)
    outst8 = al.f(512)

    NQT = [(-128, 0, 128, 0, "halo")] + [
        (qt * 512, 128 + qt * 512, 512, 1 + qt * 4, "own") for qt in range(4)]

    def emit(w):
        ALLE = ("sync", "tensor", "vector", "scalar", "gpsimd")

        def phase_barrier():
            w.fence(("tensor", "vector", "scalar"), ALLE)
            for sem in ("LD", "ST", "WT", "GD"):
                for e in ALLE:
                    w.wait(e, sem)

        def ln(srcs, dsts, T):
            ln_tile_g(w, env, srcs, dsts, T)

        w.op("vector", "memset", ones_b[:, 0:1], 1.0)
        w.op("vector", "memset", onesr_f[0:1, :128], 1.0)
        w.fence("vector", ("tensor", "scalar"))
        w.dma("sync", idf, identf, "LD")
        w.dma("gpsimd", idb, identb, "WT")
        w.dma("gpsimd", trimask, trim, "WT")
        w.dma("gpsimd", h128b, h128d, "WT")
        for c in range(16):
            w.dma("gpsimd", resid[c],
                  x_band[c * 128:(c + 1) * 128, 640:2688], "WT")
            w.dma("gpsimd", x2h[c],
                  x_band[c * 128:(c + 1) * 128, 512:640], "WT")
        w.wait("tensor", "WT")
        w.wait("tensor", "LD")
        w.wait("scalar", "WT")
        w.wait("vector", "WT")
        pc_jobs = []
        for (src_t, dst_t, rows_n) in ((wpT, wpT_b, D), (gpT, gpT_b, D),
                                       (opT, opT_b, 4096)):
            step = 512
            for r0 in range(0, rows_n, step):
                pc_jobs.append((dst_t[r0:r0 + step, :],
                                src_t[r0:r0 + step, :]))
        for (src_t, dst_t) in ((WupT, WupT_b), (WgtT, WgtT_b)):
            for c in range(16):
                pc_jobs.append(
                    (dst_t[:, :, c * 128:(c + 1) * 128].rearrange(
                        "jb p j -> p jb j"),
                     src_t[c * 128:(c + 1) * 128, :].rearrange(
                        "p (jb j) -> p jb j", j=128)))
        for cg in range(64):
            hf, cc = cg // 32, cg % 32
            pc_jobs.append(
                (WdnT_b[:, hf, :, cc * 128:(cc + 1) * 128].rearrange(
                    "jt p j -> p jt j"),
                 WdnT[cg * 128:(cg + 1) * 128, :].rearrange(
                    "p (jt j) -> p jt j", j=128)))
        pc_total = 16 * len(pc_jobs)

        def drain_pc(k):
            for _ in range(k):
                if pc_jobs:
                    dst_a, src_a = pc_jobs.pop(0)
                    w.dma("gpsimd", dst_a, src_a, "PC")

        def drain_pc_all():
            drain_pc(len(pc_jobs))

        def rope_from_psum(ps, dst, T):
            a = scr1[0:64, 0:T]
            b = scr1[0:64, 512:512 + T]
            a2 = scr1[64:128, 0:T]
            b2 = scr1[64:128, 512:512 + T]
            w.op("vector", "tensor_tensor", a, ps[0:64, :T], cs_t[0:64, :T],
                 op=ALU.mult)
            w.op("vector", "tensor_tensor", b, ps[64:128, :T],
                 cs_t[64:128, :T], op=ALU.mult)
            w.op("vector", "tensor_tensor", a2, ps[0:64, :T],
                 cs_t2[0:64, :T], op=ALU.mult)
            w.op("vector", "tensor_tensor", b2, ps[64:128, :T],
                 cs_t2[64:128, :T], op=ALU.mult)
            w.fence("vector", "vector")
            w.op("vector", "tensor_tensor", dst[0:64, :T], a, b,
                 op=ALU.subtract)
            w.op("vector", "tensor_tensor", dst[64:128, :T], a2, b2,
                 op=ALU.add)

        def stream_wcol(dram_w, j, par, wb, nch=16):
            w.fence("tensor", "gpsimd")
            w.dma("gpsimd", wb[par][:, :nch * 128].rearrange(
                "p (c j) -> p c j", c=nch),
                dram_w[:, j * 128:(j + 1) * 128].rearrange(
                    "(c p) j -> p c j", p=128), "WT")
            w.wait("tensor", "WT")

        def p1(w):
            ps_qkv = SL[0]
            ps_sc = ps_f[:, 512:1408]
            ps_att = ps_f[:, 1536:1664]
            kvt_list = [(0, 128), (128, 512), (640, 512), (1152, 512),
                        (1664, 512), (2176, 512), (-1, 256)]
            for (koff, T) in kvt_list:
                drain_pc(4)
                if koff >= 0:
                    src_ap, col0 = x_band, koff
                    csrc = (cs_band, cs2_band)
                else:
                    src_ap, col0 = x_front, 0
                    csrc = (cs_front, cs2_front)
                    koff = BAND2
                dst = [hb[c][:, :T] for c in range(16)]
                w.fence(("tensor", "vector", "scalar"), "gpsimd")
                for c in range(16):
                    w.dma("gpsimd", dst[c],
                          src_ap[c * 128:(c + 1) * 128, col0:col0 + T], "WT")
                w.wait("tensor", "WT")
                w.wait("scalar", "WT")
                w.wait("vector", "WT")
                ln(dst, dst, T)
                w.fence("vector", "gpsimd")
                w.dma("gpsimd", cs_t[:, :T], csrc[0][:, col0:col0 + T], "WT")
                w.dma("gpsimd", cs_t2[:, :T], csrc[1][:, col0:col0 + T], "WT")
                w.wait("vector", "WT")
                PQ = (SL[0], SL[4])

                def k_load(j):
                    w.dma("gpsimd", wbuf1[j % 2].rearrange(
                        "p (c j) -> p c j", c=16),
                        wk[:, j * 128:(j + 1) * 128].rearrange(
                            "(c p) j -> p c j", p=128), "WT")

                def k_mm(j):
                    for kc in range(16):
                        w.op("tensor", "matmul", PQ[j % 2][:, :T],
                             wbuf1[j % 2][:, kc * 128:(kc + 1) * 128],
                             dst[kc], start=(kc == 0), stop=(kc == 15))

                def k_evac(j):
                    rope_from_psum(PQ[j % 2], K1[j][:, koff:koff + T], T)

                w.fence(("vector", "scalar"), "tensor")
                w.fence("tensor", "gpsimd")
                w.pipe(4, k_load, k_mm, k_evac, "vector")
                w.fence(("vector", "tensor"), ("tensor", "vector", "gpsimd"))
                for j in range(4):
                    stream_wcol(wv, j, j % 2, wbuf1)
                    for kc in range(16):
                        w.op("tensor", "matmul", ps_qkv[:, :T],
                             wbuf1[j % 2][:, kc * 128:(kc + 1) * 128],
                             dst[kc], start=(kc == 0), stop=(kc == 15))
                    w.fence("tensor", "scalar")
                    w.op("scalar", "copy", tmpb[:, :T], ps_qkv[:, :T])
                    w.fence("scalar", "tensor")
                    for t in range(T // 128):
                        w.op("tensor", "transpose",
                             ps_b[:, t * 128:(t + 1) * 128],
                             tmpb[:, t * 128:(t + 1) * 128], idb)
                    w.fence("tensor", "vector")
                    w.op("vector", "tensor_copy",
                         VT1[j][:, koff:koff + T], ps_b[:, :T])
                    w.fence("vector", "tensor")
            for (qoff, csoff, T, Qg0, kind) in NQT:
                drain_pc(8)
                bcol = qoff + 640
                hq = [hb[c][:, :T] for c in range(16)]
                w.fence(("tensor", "vector", "scalar"), "gpsimd")
                for c in range(16):
                    w.dma("gpsimd", hq[c],
                          x_band[c * 128:(c + 1) * 128, bcol:bcol + T], "WT")
                w.wait("tensor", "WT")
                w.wait("scalar", "WT")
                w.wait("vector", "WT")
                ln(hq, hq, T)
                w.fence("vector", "gpsimd")
                w.dma("gpsimd", cs_t[:, :T], cs_q[:, csoff:csoff + T], "WT")
                w.dma("gpsimd", cs_t2[:, :T], cs2_q[:, csoff:csoff + T], "WT")
                w.wait("vector", "WT")

                PQ = (SL[0], SL[4])

                def q_load(j):
                    w.dma("gpsimd", wbuf1[j % 2].rearrange(
                        "p (c j) -> p c j", c=16),
                        wq[:, j * 128:(j + 1) * 128].rearrange(
                            "(c p) j -> p c j", p=128), "WT")

                def q_mm(j):
                    for kc in range(16):
                        w.op("tensor", "matmul", PQ[j % 2][:, :T],
                             wbuf1[j % 2][:, kc * 128:(kc + 1) * 128],
                             hq[kc], start=(kc == 0), stop=(kc == 15))

                def q_evac(j):
                    rope_from_psum(PQ[j % 2], q2[j][:, :T], T)

                w.fence(("vector", "scalar"), "tensor")
                w.fence("tensor", "gpsimd")
                w.pipe(16, q_load, q_mm, q_evac, "vector")
                w.fence(("vector", "tensor"), ("tensor", "vector", "gpsimd"))
                for h in range(16):
                    g = h // 4
                    for qb in range(T // 128):
                        Qg = Qg0 + qb
                        w.fence("vector", "sync")
                        w.dma("sync", mskb, masks[Qg], "LD")
                        b0 = Qg * 128
                        w.op("tensor", "matmul", ps_sc[:, 0:512],
                             q2[h][:, qb * 128:(qb + 1) * 128],
                             K1[g][:, b0:b0 + 512], start=True, stop=True)
                        w.op("tensor", "matmul", ps_sc[:, 512:640],
                             q2[h][:, qb * 128:(qb + 1) * 128],
                             K1[g][:, b0 + 512:b0 + 640], start=True,
                             stop=True)
                        w.op("tensor", "matmul", ps_sc[:, 640:896],
                             q2[h][:, qb * 128:(qb + 1) * 128],
                             K1[g][:, BAND2:BAND2 + 256], start=True,
                             stop=True)
                        w.fence("tensor", "vector")
                        w.wait("vector", "LD")
                        w.op("vector", "scalar_tensor_tensor", scm, ps_sc,
                             0.08838834764831845, mskb, op0=ALU.mult,
                             op1=ALU.add)
                        w.fence("vector", "vector")
                        w.op("vector", "reduce_max", cols1[:, 0:1], scm,
                             axis=AX.X)
                        w.fence("vector", "vector")
                        w.op("vector", "tensor_scalar", cols1[:, 1:2],
                             cols1[:, 0:1], -1.0, None, op0=ALU.mult)
                        w.fence("vector", ("scalar", "tensor"))
                        w.op("scalar", "activation", Pt, scm, AF.Exp,
                             bias=cols1[:, 1:2], accum_out=cols1[:, 2:3])
                        w.fence("scalar", "vector")
                        w.op("vector", "reciprocal", cols1[:, 3:4],
                             cols1[:, 2:3])
                        w.fence("vector", "vector")
                        w.op("vector", "tensor_scalar", Pt, Pt,
                             cols1[:, 3:4], None, op0=ALU.mult)
                        w.fence("vector", "tensor")
                        for sx in range(7):
                            w.op("tensor", "transpose",
                                 ps_b[:, sx * 128:(sx + 1) * 128],
                                 Pt[:, sx * 128:(sx + 1) * 128], idb)
                        w.fence("tensor", "vector")
                        w.op("vector", "tensor_copy", PTs, ps_b[:, :896])
                        w.fence("vector", "tensor")
                        for sx in range(7):
                            gk = Qg + sx if sx < 5 else 21 + (sx - 5)
                            w.op("tensor", "matmul", ps_att,
                                 VT1[g][:, gk * 128:(gk + 1) * 128],
                                 PTs[:, sx * 128:(sx + 1) * 128],
                                 start=(sx == 0), stop=(sx == 6))
                        w.fence("tensor", "scalar")
                        w.op("scalar", "copy",
                             ao2[h][:, qb * 128:(qb + 1) * 128], ps_att)
                        w.fence("scalar", "tensor")
                PQ = (SL[0], SL[4])

                def wo_load(j):
                    w.dma("gpsimd", wbuf1[j % 2].rearrange(
                        "p (c j) -> p c j", c=16),
                        wo[:, j * 128:(j + 1) * 128].rearrange(
                            "(c p) j -> p c j", p=128), "WT")

                def wo_mm(j):
                    for kc in range(16):
                        w.op("tensor", "matmul", PQ[j % 2][:, :T],
                             wbuf1[j % 2][:, kc * 128:(kc + 1) * 128],
                             ao2[kc][:, :T],
                             start=(kc == 0), stop=(kc == 15))

                def wo_evac(j):
                    w.op("scalar", "copy", q2[j][:, :T], PQ[j % 2][:, :T])

                w.fence(("vector", "scalar"), "tensor")
                w.fence("tensor", "gpsimd")
                w.pipe(16, wo_load, wo_mm, wo_evac, "scalar")
                w.fence(("scalar", "tensor"), ("tensor", "vector", "scalar"))
                ln([t[:, :T] for t in q2], [t[:, :T] for t in q2], T)
                tgt = x2h if kind == "halo" else [
                    resid[c][:, qoff:qoff + T] for c in range(16)]
                for c in range(16):
                    w.op("vector", "tensor_tensor", tgt[c], tgt[c],
                         q2[c][:, :T], op=ALU.add)
                w.fence("vector", ("tensor", "scalar"))

        def p2(w):
            drain_pc_all()
            ps_mm = SL[0]
            ps_a = ps_f[0:8, 512:1024]
            ps_at = ps_f[:, 1024:1056]
            ps_af = ps_f[0:8, 1536:2048]
            ps_bc = ps_f[:, 512:1024]
            for m in range(4):
                w.dma("gpsimd", vpw[m].rearrange("p (c j) -> p c j", c=16),
                      VPd[m].rearrange("(c p) j -> p c j", p=128), "WT")
                w.dma("gpsimd", upw[m], UPd[m], "WT")
            w.dma("gpsimd", w2w.rearrange("p (c j) -> p c j", c=16),
                  W2.rearrange("(c p) j -> p c j", p=128), "WT")
            w.dma("sync", b1s[:, 0:16], b1r, "LD")
            w.dma("sync", b2s[0:8, 0:1], b2c, "LD")
            w.dma("sync", selw[0:8, :], SELd, "LD")
            w.wait("tensor", "WT")
            w.wait("tensor", "LD")
            w.wait("scalar", "LD")
            tiles = [(None, 128)] + [(tt, 512) for tt in range(4)]
            for (tt, T) in tiles:
                if tt is None:
                    srcv = x2h
                else:
                    srcv = [resid[c][:, tt * 512:tt * 512 + T]
                            for c in range(16)]
                hsrc = [hb2[c][:, :T] for c in range(16)]
                ln(srcv, hsrc, T)
                PQ = (SL[0], SL[4])

                def w1_load(j):
                    w.dma("gpsimd", wbuf2f[j % 2].rearrange(
                        "p (c j) -> p c j", c=16),
                        W1[:, j * 128:(j + 1) * 128].rearrange(
                            "(c p) j -> p c j", p=128), "WT")

                def w1_mm(j):
                    for kc in range(16):
                        w.op("tensor", "matmul", PQ[j % 2][:, :T],
                             wbuf2f[j % 2][:, kc * 128:(kc + 1) * 128],
                             hsrc[kc], start=(kc == 0), stop=(kc == 15))

                def w1_evac(j):
                    w.op("scalar", "activation", hid[j][:, :T],
                         PQ[j % 2][:, :T], AF.Relu, bias=b1s[:, j:j + 1])

                w.fence(("vector", "scalar"), "tensor")
                w.fence("tensor", "gpsimd")
                w.pipe(16, w1_load, w1_mm, w1_evac, "scalar")
                w.fence(("scalar", "tensor"), ("tensor", "scalar", "vector"))
                for kc in range(16):
                    w.op("tensor", "matmul", ps_a[:, :T],
                         w2w[:, kc * 8:(kc + 1) * 8], hid[kc][:, :T],
                         start=(kc == 0), stop=(kc == 15))
                w.fence("tensor", "scalar")
                w.op("scalar", "activation", sa2[0:8, :T], ps_a[:, :T],
                     AF.Identity, bias=b2s[0:8, 0:1])
                w.fence("scalar", "tensor")
                nb = T // 128
                for b in range(nb):
                    w.op("tensor", "transpose", ps_at[:, b * 8:(b + 1) * 8],
                         sa2[0:8, b * 128:(b + 1) * 128], idf[0:8, 0:8])
                w.fence("tensor", "vector")
                atv = at2[:, 0:nb * 8].rearrange("p (b e) -> p b e", e=8)
                w.op("vector", "tensor_copy", at2[:, 0:nb * 8],
                     ps_at[:, 0:nb * 8])
                w.fence("vector", "vector")
                w.op("vector", "reduce_max", am2[:, 0:nb], atv, axis=AX.X)
                w.fence("vector", "vector")
                amb = am2[:, 0:nb, None].broadcast_to([128, nb, 8])
                w.op("vector", "tensor_tensor", atv, atv, amb,
                     op=ALU.subtract)
                w.fence("vector", "scalar")
                w.op("scalar", "activation", at2[:, 0:nb * 8],
                     at2[:, 0:nb * 8], AF.Exp)
                w.fence("scalar", "vector")
                w.op("vector", "tensor_reduce", asum2[:, 0:nb], atv,
                     axis=AX.X, op=ALU.add)
                w.fence("vector", "vector")
                w.op("vector", "reciprocal", asum2[:, 0:nb], asum2[:, 0:nb])
                w.fence("vector", "vector")
                asb = asum2[:, 0:nb, None].broadcast_to([128, nb, 8])
                w.op("vector", "tensor_tensor", atv, atv, asb, op=ALU.mult)
                w.fence("vector", "tensor")
                for b in range(nb):
                    w.op("tensor", "transpose",
                         ps_af[:, b * 128:(b + 1) * 128],
                         at2[:, b * 8:(b + 1) * 8], idf)
                w.fence("tensor", "vector")
                w.op("vector", "tensor_copy", alf2[0:8, :T],
                     ps_af[:, 0:T])
                w.fence("vector", "tensor")
                PQ = (SL[0], SL[4])

                def vx_mm(m):
                    for kc in range(16):
                        w.op("tensor", "matmul", PQ[m % 2][:, :T],
                             vpw[m][:, kc * 128:(kc + 1) * 128], hsrc[kc],
                             start=(kc == 0), stop=(kc == 15))

                def vx_evac(m):
                    w.op("scalar", "copy", vx[m][:, :T], PQ[m % 2][:, :T])

                w.fence(("vector", "scalar"), "tensor")
                w.pipe(4, None, vx_mm, vx_evac, "scalar")
                w.fence(("scalar", "tensor"), ("tensor", "scalar", "vector"))
                for m in range(4):
                    w.op("tensor", "matmul", ps_bc[:, :T],
                         selw[0:8, m * 128:(m + 1) * 128], alf2[0:8, :T],
                         start=True, stop=True)
                    w.fence("tensor", "vector")
                    w.op("vector", "tensor_tensor", wsc[m][:, :T],
                         vx[m][:, :T], ps_bc[:, :T], op=ALU.mult)
                    w.fence("vector", "tensor")
                PQ = (SL[0], SL[4])

                def uo_mm(j):
                    for m in range(4):
                        w.op("tensor", "matmul", PQ[j % 2][:, :T],
                             upw[m][:, j * 128:(j + 1) * 128], wsc[m][:, :T],
                             start=(m == 0), stop=(m == 3))

                def uo_evac(j):
                    w.op("scalar", "copy", o22[j][:, :T], PQ[j % 2][:, :T])

                w.fence(("vector", "scalar"), "tensor")
                w.pipe(16, None, uo_mm, uo_evac, "scalar")
                w.fence(("scalar", "tensor"), ("tensor", "scalar", "vector"))
                ln([t[:, :T] for t in o22], [t[:, :T] for t in o22], T)
                tgt = x2h if tt is None else [
                    resid[c][:, tt * 512:tt * 512 + T] for c in range(16)]
                for c in range(16):
                    w.op("vector", "tensor_tensor", tgt[c], tgt[c],
                         o22[c][:, :T], op=ALU.add)
                w.fence("vector", ("tensor", "scalar"))

        def p3(w):
            for e in ("sync", "scalar", "tensor", "gpsimd"):
                w.wait(e, "PC", pc_total)
            ps_mm = SL[0]
            ps_h3 = ps_f[:, 1024:1032]
            w.dma("sync", cwf, cwr, "LD")
            w.dma("sync", cbf, cbr, "LD")
            w.dma("sync", dtbc[0:64, 0:1], dtb, "LD")
            w.dma("sync", hmk, hmaskd, "LD")
            w.wait("scalar", "LD")
            w.wait("vector", "LD")
            ln([t[:, 124:128] for t in x2h], [t[:, 0:4] for t in hh3], 4)
            for tt in range(4):
                c0 = tt * TT
                srcv = [resid[c][:, c0:c0 + TT] for c in range(16)]
                hsrc = [a_src[c] for c in range(16)]
                ln(srcv, hsrc, TT)
                gd_after = [0] * 35
                pp_after = [0] * 35
                sc_after = [0] * 35
                for j in range(35):
                    wd = 128 if j < 34 else 64
                    jc0 = j * 128
                    par = j % 2
                    if j >= 2:
                        w.wait("gpsimd", "PP", pp_after[j - 2])
                    w.dma("gpsimd",
                          a_wbuf[par][:, :16 * wd].rearrange(
                              "p (c j) -> p c j", c=16),
                          wpT_b[:, jc0:jc0 + wd].rearrange(
                              "(c p) j -> p c j", p=128), "WT")
                    w.wait("tensor", "WT")
                    if j >= 2:
                        w.wait("tensor", "SC", sc_after[j - 2])
                    sl = SL[par] if j < 34 else SL[par][0:64, :]
                    for kc in range(16):
                        w.op("tensor", "matmul", sl[:, :TT] if j < 34
                             else SL[par][0:64, :TT],
                             a_wbuf[par][:, kc * wd:(kc + 1) * wd],
                             hsrc[kc], start=(kc == 0), stop=(kc == 15))
                    if j < 34 and tt == 0:
                        for kc in range(16):
                            w.op("tensor", "matmul", ps_h3[:, 0:4],
                                 a_wbuf[par][:, kc * 128:(kc + 1) * 128],
                                 hh3[kc][:, 0:4],
                                 start=(kc == 0), stop=(kc == 15))
                    pp_after[j] = w.inc_last("tensor", "PP")
                    w.wait("scalar", "PP", pp_after[j])
                    if j >= 2:
                        w.wait("scalar", "GD", gd_after[j - 2])
                        w.wait("vector", "GD", gd_after[j - 2])
                    if j < 34:
                        w.op("scalar", "copy", projx2[par][:, 3:515],
                             SL[par][:, :TT])
                        if tt == 0:
                            w.op("scalar", "copy", projx2[par][:, 0:3],
                                 ps_h3[:, 1:4])
                        sc_after[j] = w.inc_last("scalar", "SC")
                        w.fence("scalar", "vector")
                        if tt == 0:
                            w.op("vector", "tensor_scalar",
                                 projx2[par][:, 0:3], projx2[par][:, 0:3],
                                 hmk[:, 0:1], None, op0=ALU.mult)
                            w.fence("vector", "vector")
                        else:
                            w.op("vector", "tensor_copy",
                                 projx2[par][:, 0:3],
                                 tails[:, j * 3:j * 3 + 3])
                            w.fence("vector", "vector")
                        w.op("vector", "tensor_scalar", acc32[par],
                             projx2[par][:, 0:512],
                             cwf[:, j * 4:j * 4 + 1], cbf[:, j:j + 1],
                             op0=ALU.mult, op1=ALU.add)
                        for i in range(1, 4):
                            w.fence("vector", "vector")
                            w.op("vector", "scalar_tensor_tensor", acc32[par],
                                 projx2[par][:, i:i + 512],
                                 cwf[:, j * 4 + i:j * 4 + i + 1], acc32[par],
                                 op0=ALU.mult, op1=ALU.add)
                        w.fence("vector", "vector")
                        w.op("vector", "tensor_copy",
                             tails[:, j * 3:j * 3 + 3],
                             projx2[par][:, 512:515])
                        w.fence("vector", "scalar")
                        w.op("scalar", "activation", xbc32[par], acc32[par],
                             AF.Silu)
                        w.fence("scalar", "gpsimd")
                        if j < 32:
                            w.dma("gpsimd",
                                  xs_scr[jc0:jc0 + 128, c0:c0 + TT],
                                  xbc32[par], "GD")
                        else:
                            w.fence("scalar", "vector")
                            if j == 32:
                                w.op("vector", "tensor_copy",
                                     Bres[:, c0:c0 + TT], xbc32[par])
                            else:
                                w.op("vector", "tensor_copy",
                                     Cres[:, c0:c0 + TT], xbc32[par])
                            w.fence("vector", "gpsimd")
                            w.dma("gpsimd",
                                  sx_in.ap()[0:1, 0:1],
                                  xbc32[par][0:1, 0:1], "GD")
                        gd_after[j] = w.val("GD")
                    else:
                        w.fence("tensor", "vector")
                        d0 = dtx3[0:64, 0:512]
                        w.op("vector", "tensor_scalar", d0,
                             SL[par][0:64, :TT], dtbc[0:64, 0:1], None,
                             op0=ALU.add)
                        sc_after[j] = w.inc_last("vector", "SC")
                        w.fence("vector", "scalar")
                        w.op("scalar", "activation", acc3[0:64, :],
                             d0, AF.Abs)
                        w.fence("scalar", "scalar")
                        w.op("scalar", "activation", acc3[0:64, :],
                             acc3[0:64, :], AF.Exp, scale=-1.0)
                        w.fence("scalar", "vector")
                        w.op("vector", "tensor_scalar", acc3[0:64, :],
                             acc3[0:64, :], 1.0, None, op0=ALU.add)
                        w.fence("vector", "scalar")
                        w.op("scalar", "activation", acc3[0:64, :],
                             acc3[0:64, :], AF.Ln)
                        w.fence("scalar", "vector")
                        w.op("vector", "tensor_scalar", d0, d0, 0.0, None,
                             op0=ALU.max)
                        w.fence("vector", "vector")
                        w.op("vector", "tensor_tensor", d0, d0,
                             acc3[0:64, :], op=ALU.add)
                        w.fence("vector", "tensor")
                        for b in range(4):
                            w.op("tensor", "transpose",
                                 ps_f[:, 1024 + b * 64:1024 + (b + 1) * 64],
                                 d0[:, b * 128:(b + 1) * 128],
                                 idf[0:64, 0:64])
                        w.fence("tensor", "scalar")
                        for b in range(4):
                            w.op("scalar", "copy",
                                 dt2tok[:, (tt * 4 + b) * 64:
                                        (tt * 4 + b + 1) * 64],
                                 ps_f[:, 1024 + b * 64:1024 + (b + 1) * 64])
                        w.fence("scalar", ("tensor", "vector"))
                w.fence(("vector", "scalar"), ("tensor", "gpsimd"))
            w.wait("sync", "GD")
            w.wait("gpsimd", "GD")

        def p4(w):
            ps_cbt = SL[0]
            ps_seg = ps_f[:, 512:1536]
            ps_yd = ps_f[:, 1536:2560]
            ps_yo = ps_f[:, 2560:3584]
            w.dma("sync", acol4[0:64, 0:1], alogd, "LD")
            w.dma("sync", dbc44[:, 0:64], dbc_in, "LD")
            w.wait("scalar", "LD")
            w.op("scalar", "activation", acol4[0:64, 0:1], acol4[0:64, 0:1],
                 AF.Exp)
            w.fence("scalar", "vector")
            w.op("vector", "tensor_scalar", acol4[0:64, 0:1],
                 acol4[0:64, 0:1], -1.0, None, op0=ALU.mult)
            w.op("vector", "memset", S_in[:, :], 0.0)
            w.op("vector", "memset", pc_cur[:, :], 1.0)
            w.fence("vector", ("tensor", "scalar"))
            for ch in range(p4chunks):
                t0 = ch * 256
                w.fence(("tensor", "vector"), "sync")
                for sx in range(2):
                    w.dirty["sync"] = True
                    w.bump("LD", 16)
                    if w.en == "sync":
                        ins = nc.sync.dma_start(
                            out=xT4[sx],
                            in_=xs_scr[:, t0 + sx * 128:t0 + (sx + 1) * 128],
                            transpose=True)
                        ins.then_inc(w.sems["LD"], 16)
                        w.last["sync"] = ins
                w.wait("tensor", "LD")
                w.wait("vector", "LD")
                Bm = Bres[:, t0:t0 + 256]
                Cm = Cres[:, t0:t0 + 256]
                if p4parts < 1:
                    continue
                for sx in range(2):
                    w.op("tensor", "transpose",
                         ps_cbt[0:64, sx * 128:(sx + 1) * 128],
                         dt2tok[:, (2 * ch + sx) * 64:(2 * ch + sx + 1) * 64],
                         idf)
                w.fence("tensor", "vector")
                w.op("vector", "tensor_scalar", dAf4[0:64, :],
                     ps_cbt[0:64, 0:256], acol4[0:64, 0:1], None,
                     op0=ALU.mult)
                w.fence("vector", "vector")
                w.op("vector", "memset", cumf4[0:64, :], 0.0)
                w.fence("vector", "vector")
                w.op("vector", "tensor_tensor_scan", cumf4[0:64, :],
                     dAf4[0:64, :], cumf4[0:64, :], 0.0, op0=ALU.add,
                     op1=ALU.add)
                w.fence("vector", "tensor")
                if p4parts < 2:
                    continue
                for sx in range(2):
                    w.op("tensor", "transpose",
                         ps_cbt[:, 256 + sx * 64:256 + (sx + 1) * 64],
                         cumf4[0:64, sx * 128:(sx + 1) * 128],
                         idf[0:64, 0:64])
                w.op("tensor", "transpose", ps_cbt[0:1, 384:448],
                     cumf4[0:64, 255:256], idf[0:64, 0:64])
                w.fence("tensor", "scalar")
                for sx in range(2):
                    w.op("scalar", "copy", cumT44[sx][:, 0:64],
                         ps_cbt[:, 256 + sx * 64:256 + (sx + 1) * 64])
                w.op("scalar", "copy", rrow4[0:1, 0:64], ps_cbt[0:1, 384:448])
                w.fence("scalar", "tensor")
                if p4parts < 2.5:
                    continue
                w.op("tensor", "matmul", ps_cbt[:, 448:512],
                     onesr_f[0:1, :128], rrow4[0:1, 0:64],
                     start=True, stop=True)
                w.fence("tensor", ("vector", "scalar"))
                w.op("scalar", "activation", rexp4[:, 0:64],
                     ps_cbt[:, 448:512], AF.Exp)
                w.fence("scalar", "vector")
                for sx in range(2):
                    w.op("vector", "tensor_tensor",
                         tseg4[:, sx * 64:sx * 64 + 64],
                         ps_cbt[:, 448:512], cumT44[sx][:, 0:64],
                         op=ALU.subtract)
                w.fence("vector", "scalar")
                for sx in range(2):
                    w.op("scalar", "activation", decf44[sx][:, 0:64],
                         tseg4[:, sx * 64:sx * 64 + 64], AF.Exp)
                    w.op("scalar", "activation", ect44[sx][:, 0:64],
                         cumT44[sx][:, 0:64], AF.Exp)
                w.fence("scalar", "vector")
                if p4parts < 2.8:
                    continue
                for sx in range(2):
                    w.op("vector", "tensor_copy",
                         ect_all[:, (ch * 2 + sx) * 64:(ch * 2 + sx + 1) * 64],
                         ect44[sx][:, 0:64])
                w.op("vector", "tensor_copy", pc_all[:, ch * 64:(ch + 1) * 64],
                     pc_cur[:, 0:64])
                w.fence("vector", "vector")
                w.op("vector", "tensor_tensor", pc_cur[:, 0:64],
                     pc_cur[:, 0:64], rexp4[:, 0:64], op=ALU.mult)
                for sx in range(2):
                    w.op("vector", "tensor_tensor", sfac44[sx][:, 0:64],
                         dt2tok[:, (2 * ch + sx) * 64:(2 * ch + sx + 1) * 64],
                         decf44[sx][:, 0:64], op=ALU.mult)
                w.fence("vector", "vector")
                if p4parts < 3:
                    continue
                for sx in range(2):
                    w.op("tensor", "transpose", ps_b[:, 0:128],
                         Bm[:, sx * 128:(sx + 1) * 128], idb)
                    w.fence("tensor", "vector")
                    w.op("vector", "tensor_copy", BTt[sx], ps_b[:, 0:128])
                    w.fence("vector", "tensor")
                for sx in range(2):
                    w.op("tensor", "matmul", ps_cbt[:, 0:256],
                         Bm[:, sx * 128:(sx + 1) * 128], Cm,
                         start=True, stop=True)
                    w.fence("tensor", "vector")
                    if sx == 0:
                        w.op("vector", "tensor_tensor", CBTm4[0][:, 0:128],
                             ps_cbt[:, 0:128], trimask, op=ALU.mult)
                        w.op("vector", "tensor_copy", CBTm4[0][:, 128:256],
                             ps_cbt[:, 128:256])
                    else:
                        w.op("vector", "memset", CBTm4[1][:, 0:128], 0.0)
                        w.op("vector", "tensor_tensor", CBTm4[1][:, 128:256],
                             ps_cbt[:, 128:256], trimask, op=ALU.mult)
                    w.fence("vector", "tensor")
                if p4parts < 4:
                    continue
                for g in range(4):
                    w.fence("tensor", ("sync", "vector"))
                    w.dma("sync", ehs4[0:64, :],
                          ehseld[:, g * 2048:(g + 1) * 2048], "LD")
                    w.wait("tensor", "LD")
                    hsl = slice(g * 16, (g + 1) * 16)
                    for sx in range(2):
                        dv = dt2tok[:, (2 * ch + sx) * 64 + g * 16:
                                    (2 * ch + sx) * 64 + (g + 1) * 16]
                        sv = sfac44[sx][:, g * 16:(g + 1) * 16]
                        xtv = xT4[sx][:, g * 1024:(g + 1) * 1024].rearrange(
                            "p (h q) -> p h q", h=16)
                        w.op("vector", "tensor_tensor",
                             xdt4[sx].rearrange("p (h q) -> p h q", h=16),
                             xtv, dv[:, :, None].broadcast_to([128, 16, 64]),
                             op=ALU.mult)
                        w.op("vector", "tensor_tensor",
                             xdec4[sx].rearrange("p (h q) -> p h q", h=16),
                             xtv, sv[:, :, None].broadcast_to([128, 16, 64]),
                             op=ALU.mult)
                    w.fence("vector", "tensor")
                    for sx in range(2):
                        for gq in range(4):
                            for hh2 in range(4):
                                h = (g * 16 + gq * 4 + hh2)
                                w.op("tensor", "matmul",
                                     ps_seg[:, hh2 * 256:(hh2 + 1) * 256],
                                     ehs4[0:64, (gq * 4 + hh2) * 128:
                                          (gq * 4 + hh2 + 1) * 128],
                                     cumf4[0:64, :], start=True, stop=True)
                            w.fence("tensor", "vector")
                            cv = cumT44[sx][:, g * 16 + gq * 4:
                                            g * 16 + (gq + 1) * 4, None] \
                                .broadcast_to([128, 4, 256])
                            w.op("vector", "tensor_tensor",
                                 tseg4[:, 0:1024].rearrange(
                                     "p (h t) -> p h t", h=4),
                                 ps_seg[:, 0:1024].rearrange(
                                     "p (h t) -> p h t", h=4),
                                 cv, op=ALU.subtract)
                            w.fence("vector", "vector")
                            w.op("vector", "tensor_scalar", tseg4[:, 0:1024],
                                 tseg4[:, 0:1024], 0.0, None, op0=ALU.min)
                            w.fence("vector", ("scalar", "tensor"))
                            w.op("scalar", "activation", tseg2b[:, 0:1024],
                                 tseg4[:, 0:1024], AF.Exp)
                            w.fence("scalar", "vector")
                            cbv = CBTm4[sx][:, None, :].broadcast_to(
                                [128, 4, 256])
                            w.op("vector", "tensor_tensor",
                                 Pg4[sx][:, gq * 1024:(gq + 1) * 1024]
                                 .rearrange("p (h t) -> p h t", h=4),
                                 tseg2b[:, 0:1024].rearrange(
                                     "p (h t) -> p h t", h=4),
                                 cbv, op=ALU.mult)
                            w.fence("vector", ("tensor", "scalar"))
                    if p4parts < 5:
                        continue
                    if dbg and ch == 0 and g == 0:
                        w.fence(("vector", "scalar", "tensor"), "gpsimd")
                        w.dma("gpsimd", pdbg[3], Pg4[0], "GD")
                        w.dma("gpsimd", pdbg[4], Pg4[1], "GD")
                        w.dma("gpsimd", pdbg[5][:, 0:1024], xdt4[0], "GD")
                        w.dma("gpsimd", pdbg[5][:, 1024:2048], xdt4[1], "GD")
                        w.dma("gpsimd", pdbg[5][:, 2048:2304],
                              CBTm4[0][:, :], "GD")
                        w.dma("gpsimd", pdbg[5][:, 2304:2560],
                              CBTm4[1][:, :], "GD")
                        w.wait("tensor", "GD")
                        w.wait("vector", "GD")
                        w.wait("scalar", "GD")
                    w.op("vector", "tensor_copy", S_bf4,
                         S_in[:, g * 1024:(g + 1) * 1024])
                    w.fence("vector", "tensor")
                    for tq in range(2):
                        w.op("tensor", "matmul", ps_yo[:, 0:512],
                             Cm[:, tq * 128:(tq + 1) * 128], S_bf4[:, 0:512],
                             start=True, stop=True)
                        w.op("tensor", "matmul", ps_yo[:, 512:1024],
                             Cm[:, tq * 128:(tq + 1) * 128],
                             S_bf4[:, 512:1024], start=True, stop=True)
                        for hl in range(16):
                            for sx in range(2):
                                w.op("tensor", "matmul",
                                     ps_yd[:, hl * 64:(hl + 1) * 64],
                                     Pg4[sx][:, hl * 256 + tq * 128:
                                             hl * 256 + tq * 128 + 128],
                                     xdt4[sx][:, hl * 64:(hl + 1) * 64],
                                     start=(sx == 0), stop=(sx == 1))
                        w.fence("tensor", "vector")
                        w.wait("vector", "GD")
                        ev = ect44[tq][:, g * 16:(g + 1) * 16, None] \
                            .broadcast_to([128, 16, 64])
                        w.op("vector", "tensor_tensor",
                             yst4.rearrange("p (h q) -> p h q", h=16),
                             ps_yo[:, :].rearrange("p (h q) -> p h q", h=16),
                             ev, op=ALU.mult)
                        w.fence("vector", "vector")
                        w.op("vector", "tensor_tensor", yst4, yst4,
                             ps_yd[:, :], op=ALU.add)
                        w.fence("vector", "vector")
                        dvv = dbc44[:, g * 16:(g + 1) * 16, None] \
                            .broadcast_to([128, 16, 64])
                        w.op("vector", "tensor_tensor",
                             tseg4[:, 0:1024].rearrange(
                                 "p (h q) -> p h q", h=16),
                             xT4[tq][:, g * 1024:(g + 1) * 1024].rearrange(
                                 "p (h q) -> p h q", h=16),
                             dvv, op=ALU.mult)
                        w.fence("vector", "vector")
                        w.op("vector", "tensor_tensor", yst4, yst4,
                             tseg4[:, 0:1024], op=ALU.add)
                        w.fence("vector", ("gpsimd", "tensor"))
                        w.dma("gpsimd",
                              y_scr[t0 + tq * 128:t0 + (tq + 1) * 128,
                                    g * 1024:(g + 1) * 1024], yst4, "GD")
                    if p4parts < 6:
                        continue
                    for sx in range(2):
                        w.op("tensor", "matmul", ps_yd[:, 0:512], BTt[sx],
                             xdec4[sx][:, 0:512], start=(sx == 0),
                             stop=(sx == 1))
                    for sx in range(2):
                        w.op("tensor", "matmul", ps_yd[:, 512:1024], BTt[sx],
                             xdec4[sx][:, 512:1024], start=(sx == 0),
                             stop=(sx == 1))
                    w.fence("tensor", "vector")
                    rv = rexp4[:, g * 16:(g + 1) * 16, None].broadcast_to(
                        [128, 16, 64])
                    sl4 = S_in[:, g * 1024:(g + 1) * 1024]
                    w.op("vector", "tensor_tensor",
                         sl4.rearrange("p (h q) -> p h q", h=16),
                         sl4.rearrange("p (h q) -> p h q", h=16), rv,
                         op=ALU.mult)
                    w.fence("vector", "vector")
                    w.op("vector", "tensor_tensor", sl4, sl4, ps_yd[:, :],
                         op=ALU.add)
                    w.fence("vector", ("tensor", "gpsimd"))
            w.wait("sync", "GD")
            w.wait("gpsimd", "GD")

        def p5(w):
            w.dma("sync", mv1s[0:128, 0:3], mv1d, "LD")
            w.dma("sync", mv2s[0:128, 0:3], mv2d, "LD")
            w.fence("vector", "sync")
            w.dma("sync", sx_in.ap()[:, 0:4096], S_in[:, :], "ST")
            w.dma("sync", sx_in.ap()[:, 4096:4160], pc_cur[:, 0:64], "ST")
            w.wait("gpsimd", "ST")
            cc = w.bump("CC", 1)
            w.dirty["gpsimd"] = True
            if w.en == "gpsimd":
                ins = nc.gpsimd.collective_compute(
                    "AllGather", mybir.AluOpType.bypass,
                    replica_groups=[[0, 1, 2, 3], [4, 5, 6, 7]],
                    ins=[sx_in.ap().opt()], outs=[sx_out.ap().opt()])
                ins.then_inc(w.sems["CC"], 1)
                w.last["gpsimd"] = ins
            w.wait("sync", "CC", cc)
            w.wait("vector", "CC", cc)
            w.op("vector", "memset", S_in[:, :], 0.0)
            w.fence("vector", "vector")
            for p in range(3):
                w.fence("vector", "sync")
                w.dma("sync", xtmp5[:, :],
                      sx_out.ap()[p * 128:(p + 1) * 128, 0:4096], "LD")
                w.dma("sync", lam5[:, 0:64],
                      sx_out.ap()[p * 128:(p + 1) * 128, 4096:4160], "LD")
                w.wait("vector", "LD")
                w.op("vector", "tensor_scalar", lam5[:, 0:64], lam5[:, 0:64],
                     mv1s[:, p:p + 1], mv2s[:, p:p + 1], op0=ALU.mult,
                     op1=ALU.add)
                w.op("vector", "tensor_scalar", xtmp5[:, :], xtmp5[:, :],
                     mv1s[:, p:p + 1], None, op0=ALU.mult)
                w.fence("vector", "vector")
                w.op("vector", "tensor_tensor",
                     S_in[:, :].rearrange("p (h q) -> p h q", h=64),
                     S_in[:, :].rearrange("p (h q) -> p h q", h=64),
                     lam5[:, 0:64, None].broadcast_to([128, 64, 64]),
                     op=ALU.mult)
                w.fence("vector", "vector")
                w.op("vector", "tensor_tensor", S_in[:, :], S_in[:, :],
                     xtmp5[:, :], op=ALU.add)
                w.fence("vector", "vector")

        def p6(w):
            ps_c = ps_f[:, 512:1536]
            for ch in range(8):
                t0 = ch * 256
                w.fence(("vector", "tensor"), "vector")
                w.op("vector", "tensor_tensor",
                     sc6.rearrange("p (h q) -> p h q", h=64),
                     S_in[:, :].rearrange("p (h q) -> p h q", h=64),
                     pc_all[:, ch * 64:(ch + 1) * 64][:, :, None]
                     .broadcast_to([128, 64, 64]), op=ALU.mult)
                w.fence("vector", "tensor")
                for tq in range(2):
                    for g in range(4):
                        w.op("tensor", "matmul", ps_c[:, 0:512],
                             Cres[:, t0 + tq * 128:t0 + (tq + 1) * 128],
                             sc6[:, g * 1024:g * 1024 + 512],
                             start=True, stop=True)
                        w.op("tensor", "matmul", ps_c[:, 512:1024],
                             Cres[:, t0 + tq * 128:t0 + (tq + 1) * 128],
                             sc6[:, g * 1024 + 512:(g + 1) * 1024],
                             start=True, stop=True)
                        w.fence("tensor", ("vector", "sync"))
                        w.wait("sync", "GD")
                        w.dma("sync", yrmw,
                              y_scr[t0 + tq * 128:t0 + (tq + 1) * 128,
                                    g * 1024:(g + 1) * 1024], "LD")
                        w.wait("vector", "LD")
                        ev = ect_all[:, (ch * 2 + tq) * 64 + g * 16:
                                     (ch * 2 + tq) * 64 + (g + 1) * 16] \
                            [:, :, None].broadcast_to([128, 16, 64])
                        w.op("vector", "tensor_tensor",
                             corr6[:, :].rearrange("p (h q) -> p h q", h=16),
                             ps_c[:, 0:1024].rearrange(
                                 "p (h q) -> p h q", h=16),
                             ev, op=ALU.mult)
                        w.fence("vector", "vector")
                        w.op("vector", "tensor_tensor", yrmw, yrmw,
                             corr6[:, :], op=ALU.add)
                        w.fence("vector", ("gpsimd", "tensor"))
                        w.dma("gpsimd",
                              y_scr[t0 + tq * 128:t0 + (tq + 1) * 128,
                                    g * 1024:(g + 1) * 1024], yrmw, "GD")
            w.wait("sync", "GD")
            w.wait("gpsimd", "GD")

        def p7(w):
            for e in ("sync", "scalar", "tensor", "gpsimd"):
                w.wait(e, "PC", pc_total)
            sg7x = [sg7, sg7b]
            yb7x = [yb7, yb7b]
            w.dma("sync", nw7[:, 0:32], nwr, "LD")
            w.wait("vector", "LD")
            for tt in range(4):
                c0 = tt * TT
                srcv = [resid[c][:, c0:c0 + TT] for c in range(16)]
                hsrc = [hb7[c] for c in range(16)]
                ln(srcv, hsrc, TT)
                for j in range(32):
                    w.fence("tensor", "gpsimd")
                    w.dma("gpsimd",
                          c_wbuf[j % 2].rearrange("p (c j) -> p c j", c=16),
                          gpT_b[:, j * 128:(j + 1) * 128].rearrange(
                              "(c p) j -> p c j", p=128), "WT")
                    w.fence("vector", "sync")
                    w.dma("sync", yb7x[j % 2],
                          y_scr[c0:c0 + TT, j * 128:(j + 1) * 128],
                          "LD", transpose=True)
                    w.wait("tensor", "WT")
                    for kc in range(16):
                        w.op("tensor", "matmul", SL[j % 2],
                             c_wbuf[j % 2][:, kc * 128:(kc + 1) * 128],
                             hsrc[kc], start=(kc == 0), stop=(kc == 15))
                    w.fence("tensor", "scalar")
                    w.op("scalar", "activation", sg7, SL[j % 2], AF.Silu)
                    w.fence("scalar", ("vector", "tensor"))
                    w.wait("vector", "LD")
                    w.op("vector", "tensor_tensor", hg7[j], yb7x[j % 2], sg7,
                         op=ALU.mult)
                    w.fence("vector", ("tensor", "gpsimd", "sync"))
                for j in range(32):
                    sq = sqbuf[:, (j % 2) * 512:(j % 2) * 512 + TT]
                    if j >= 2:
                        w.fence("tensor", "scalar")
                    w.op("scalar", "activation", sq, hg7[j], AF.Square)
                    w.fence("scalar", "tensor")
                    w.op("tensor", "matmul", env["ps_st"][0:1, :TT],
                         ones_b[:, 0:1], sq, start=(j == 0), stop=(j == 31))
                w.fence("tensor", "vector")
                m = rows[0:1, 0:TT]
                w.op("vector", "tensor_scalar", m, env["ps_st"][0:1, :TT],
                     1.0 / 4096, None, op0=ALU.mult)
                w.fence("vector", "vector")
                w.op("vector", "tensor_scalar", m, m, 1e-5, None, op0=ALU.add)
                w.fence("vector", "scalar")
                w.op("scalar", "activation", m, m, AF.Sqrt)
                w.fence("scalar", "vector")
                w.op("vector", "reciprocal", m, m)
                w.fence("vector", "tensor")
                w.op("tensor", "matmul", SL[4][:, :TT], onesr_f[0:1, :128], m,
                     start=True, stop=True)
                w.fence("tensor", "scalar")
                w.op("scalar", "copy", bcR7, SL[4][:, :TT])
                w.fence("scalar", "vector")
                for j in range(32):
                    w.op("vector", "tensor_tensor", hg7[j], hg7[j], bcR7,
                         op=ALU.mult)
                    w.fence("vector", "vector")
                    w.op("vector", "tensor_scalar", hg7[j], hg7[j],
                         nw7[:, j:j + 1], None, op0=ALU.mult)
                    w.fence("vector", "vector")
                w.fence("vector", "tensor")
                def o7_load(j2):
                    w.dma("gpsimd",
                          c_wbuf2[j2 % 2].rearrange("p (c j) -> p c j", c=32),
                          opT_b[:, j2 * 128:(j2 + 1) * 128].rearrange(
                              "(c p) j -> p c j", p=128), "WT")

                def o7_mm(j2):
                    for kc in range(32):
                        w.op("tensor", "matmul", SL[j2 % 2],
                             c_wbuf2[j2 % 2][:, kc * 128:(kc + 1) * 128],
                             hg7[kc], start=(kc == 0), stop=(kc == 31))

                def o7_evac(j2):
                    w.op("scalar", "copy", o27[j2], SL[j2 % 2])

                w.pipe(16, o7_load, o7_mm, o7_evac, "scalar",
                       dma_eng=("gpsimd",))
                w.fence(("scalar", "tensor"), ("tensor", "vector", "scalar"))
                ln(o27, o27, TT)
                for c in range(16):
                    rv7 = resid[c][:, c0:c0 + TT]
                    w.op("vector", "tensor_tensor", rv7, rv7, o27[c],
                         op=ALU.add)
                w.fence("vector", ("tensor", "scalar"))

        def p8(w):
            for e in ("sync", "scalar", "tensor", "gpsimd"):
                w.wait(e, "PC", pc_total)
            ps_up = SL[0]
            ps_gt = SL[1]
            ps_h8 = SL[2]
            w.dma("sync", suugf[:, 0:16], suugd, "LD")
            w.dma("sync", sudpf[:, 0:64], sudpd, "LD")
            w.dma("sync", wsupf[:, 0:64], wsupd, "LD")
            w.dma("sync", wsgtf[:, 0:64], wsgtd, "LD")
            w.dma("sync", wsdpf[:, 0:16], wsdpd, "LD")
            w.wait("vector", "LD")
            w.wait("scalar", "LD")
            for tt in range(4):
                c0 = tt * TT
                srcv = [resid[c][:, c0:c0 + TT] for c in range(16)]
                ln(srcv, src8, TT)
                za, zb = src8, o28
                for c in range(16):
                    w.op("vector", "tensor_scalar", za[c], za[c],
                         suugf[:, c:c + 1], S2048, op0=ALU.mult, op1=ALU.mult)
                w.fence("vector", "tensor")
                def zh_mm(c):
                    w.op("tensor", "matmul", SL[2 + (c % 2)], h128b, za[c],
                         start=True, stop=True)

                def zh_evac(c):
                    w.op("scalar", "copy", zb[c], SL[2 + (c % 2)])

                w.pipe(16, None, zh_mm, zh_evac, "scalar")
                w.fence(("scalar", "tensor"), ("tensor", "scalar", "vector"))
                for b in range(4):
                    step = 1 << b
                    for i in range(16):
                        if i & step:
                            continue
                        j = i | step
                        w.op("vector", "tensor_tensor", za[i], zb[i], zb[j],
                             op=ALU.add)
                        w.op("vector", "tensor_tensor", za[j], zb[i], zb[j],
                             op=ALU.subtract)
                    w.fence("vector", "vector")
                    za, zb = zb, za
                z = zb
                w.fence("vector", ("tensor", "scalar"))
                def ug_load(j):
                    par = j % 2
                    w.dma("sync", f_wbufA[par], WupT_b[j], "LD")
                    w.dma("scalar", f_wbufG[par], WgtT_b[j], "LS")

                def ug_mm(j):
                    par = j % 2
                    sl_u = SL[par]
                    sl_g = SL[2 + par]
                    for kc in range(16):
                        w.op("tensor", "matmul", sl_u,
                             f_wbufA[par][:, kc * 128:(kc + 1) * 128], z[kc],
                             start=(kc == 0), stop=(kc == 15))
                    for kc in range(16):
                        w.op("tensor", "matmul", sl_g,
                             f_wbufG[par][:, kc * 128:(kc + 1) * 128], z[kc],
                             start=(kc == 0), stop=(kc == 15))

                def ug_evac(j):
                    par = j % 2
                    sgv = sgb8x[par]
                    w.op("scalar", "activation", sgv, SL[2 + par], AF.Silu,
                         scale=wsgtf[:, j:j + 1])
                    w.fence("scalar", "vector")
                    uv = ub8[:, j * TT:(j + 1) * TT]
                    w.op("vector", "scalar_tensor_tensor", uv, SL[par],
                         wsupf[:, j:j + 1], sgv, op0=ALU.mult, op1=ALU.mult)
                    w.fence("vector", "vector")
                    w.op("vector", "tensor_scalar", uv, uv,
                         sudpf[:, j:j + 1], S8192, op0=ALU.mult, op1=ALU.mult)

                sgb8x = [sgb8, sgb8b]
                if p8f & 1:
                    w.pipe(64, ug_load, ug_mm, ug_evac, "vector", npsum=2,
                        dma_eng=("sync", "scalar"))
                else:
                    for j in range(64):
                        w.fence("tensor", "gpsimd")
                        ug_load(j)
                        w.wait("tensor", "WT")
                        ug_mm(j)
                        w.fence("tensor", ("scalar", "vector"))
                        ug_evac(j)
                        w.fence("vector", ("tensor", "scalar"))
                w.fence(("vector", "tensor", "scalar"),
                        ("tensor", "scalar", "vector"))
                def h2_mm(c):
                    uv = ub8[:, c * TT:(c + 1) * TT]
                    w.op("tensor", "matmul", SL[c % 2], h128b, uv,
                         start=True, stop=True)

                def h2_evac(c):
                    uv = ub8[:, c * TT:(c + 1) * TT]
                    w.op("scalar", "copy", uv, SL[c % 2])

                if dbg and tt == 0:
                    w.fence(("vector", "scalar", "tensor"), "gpsimd")
                    w.dma("gpsimd", u8dbg.rearrange("(a p) t -> p a t", p=128),
                          ub8.rearrange("p (a t) -> p a t", a=64), "GD")
                    w.wait("tensor", "GD")
                    w.wait("scalar", "GD")
                    w.wait("vector", "GD")
                if p8f & 2:
                    w.pipe(64, None, h2_mm, h2_evac, "scalar")
                else:
                    for c in range(64):
                        h2_mm(c)
                        w.fence("tensor", "scalar")
                        h2_evac(c)
                        w.fence("scalar", ("tensor", "vector"))
                w.fence(("scalar", "tensor"), ("tensor", "vector", "scalar"))
                import math
                for b in range(6):
                    step = 1 << b
                    nhi = 64 // (2 * step)
                    va = ub8[:, :].rearrange(
                        "p (hi two lo t) -> p hi two lo t", hi=nhi, two=2,
                        lo=step)
                    a0 = va[:, :, 0, :, :]
                    a1 = va[:, :, 1, :, :]
                    sv = uscr8[:, :].rearrange(
                        "p (hi lo t) -> p hi lo t", hi=nhi, lo=step)
                    w.op("vector", "tensor_tensor", sv, a0, a1,
                         op=ALU.subtract)
                    w.fence("vector", "vector")
                    w.op("vector", "tensor_tensor", a0, a0, a1, op=ALU.add)
                    w.fence("vector", "vector")
                    w.op("vector", "tensor_copy", a1, sv)
                    w.fence("vector", "vector")
                w.fence("vector", "tensor")
                def dn_load(it):
                    j2, hf = it // 2, it % 2
                    w.dma("sync", f_wbuf2[it % 2], WdnT_b[j2, hf], "LD")

                def dn_mm(it):
                    j2, hf = it // 2, it % 2
                    sl = SL[(j2 % 2)]
                    for kc in range(32):
                        w.op("tensor", "matmul", sl,
                             f_wbuf2[it % 2][:, kc * 128:(kc + 1) * 128],
                             ub8[:, (hf * 32 + kc) * TT:
                                 (hf * 32 + kc + 1) * TT],
                             start=(kc == 0 and hf == 0),
                             stop=(kc == 31 and hf == 1))

                def dn_evac(it):
                    j2, hf = it // 2, it % 2
                    if hf == 1:
                        w.op("vector", "tensor_scalar", o28[j2],
                             SL[(j2 % 2)], wsdpf[:, j2:j2 + 1], None,
                             op0=ALU.mult)
                    else:
                        w.op("vector", "tensor_copy", rows[0:1, 0:1],
                             rows[0:1, 1:2])

                if p8f & 4:
                    w.pipe(32, dn_load, dn_mm, dn_evac, "vector", npsum=3,
                        dma_eng=("sync",))
                else:
                    for it in range(32):
                        w.fence("tensor", "gpsimd")
                        dn_load(it)
                        w.wait("tensor", "WT")
                        dn_mm(it)
                        w.fence("tensor", "vector")
                        dn_evac(it)
                        w.fence("vector", ("tensor", "scalar"))
                w.fence(("vector", "tensor"), ("tensor", "scalar", "vector"))
                ln(o28, o28, TT)
                for c in range(16):
                    w.wait("vector", "ST")
                    w.op("vector", "tensor_tensor", outst8,
                         resid[c][:, c0:c0 + TT], o28[c], op=ALU.add)
                    w.fence("vector", "sync")
                    w.dma("sync", outT[c * 128:(c + 1) * 128, c0:c0 + TT],
                          outst8, "ST")
        p1(w)
        phase_barrier()
        if upto >= 2:
            p2(w)
            phase_barrier()
        if upto >= 3:
            p3(w)
            phase_barrier()
        if upto >= 4:
            p4(w)
            phase_barrier()
        if upto >= 5:
            p5(w)
            phase_barrier()
        if upto >= 6:
            p6(w)
            phase_barrier()
        if upto >= 7:
            p7(w)
            phase_barrier()
        if upto >= 8:
            p8(w)
            phase_barrier()
        if dbg:
            for e in ("sync", "gpsimd"):
                for sem in ("LD", "ST", "WT", "GD"):
                    w.wait(e, sem)
            for r0 in range(16):
                w.dma("gpsimd", ydbg[r0 * 128:(r0 + 1) * 128, :],
                      y_scr[r0 * 128:(r0 + 1) * 128, :], "GD")
            for r0 in range(32):
                w.dma("gpsimd", xsdbg[r0 * 128:(r0 + 1) * 128, :],
                      xs_scr[r0 * 128:(r0 + 1) * 128, :], "GD")
            if upto in (4, 5, 6):
                w.dma("gpsimd", pdbg[6], xT4[0], "GD")
                w.dma("gpsimd", pdbg[7][:, 0:64], cumT44[0][:, 0:64], "GD")
                w.dma("gpsimd", pdbg[7][:, 64:1088],
                      Pg4[0][:, 0:1024], "GD")
                w.dma("gpsimd", pdbg[7][:, 1088:2112], S_in[:, 0:1024], "GD")
        if upto < 8:
            w.fence(("vector", "tensor", "scalar"), "gpsimd")
            for c in range(16):
                w.dma("gpsimd", outT[c * 128:(c + 1) * 128, :], resid[c],
                      "ST")
        w.wait("sync", "ST")
        w.wait("sync", "LD")
        w.wait("sync", "WT")
        w.wait("sync", "GD")
        w.wait("gpsimd", "ST")

    run_walks(nc, st, emit)
    st.close()
    return nc


def fused_inputs(inputs):
    x = np.asarray(inputs["L0_attn_x"])
    perm = rope_perm()
    permQ = np.concatenate([h * 128 + perm for h in range(16)])
    permK = np.concatenate([h * 128 + perm for h in range(4)])
    wqP = np.ascontiguousarray(np.asarray(inputs["L0_attn_wq"])[:, permQ])
    wkP = np.ascontiguousarray(np.asarray(inputs["L0_attn_wk"])[:, permK])
    wvv = np.ascontiguousarray(np.asarray(inputs["L0_attn_wv"]))
    wov = np.ascontiguousarray(np.asarray(inputs["L0_attn_wo"]))
    cs = cs_table()
    ident = np.eye(128)
    U_T = np.ascontiguousarray(np.asarray(inputs["L0_ffn_U"]).transpose(0, 2, 1))
    V_T = np.ascontiguousarray(np.asarray(inputs["L0_ffn_V"]).transpose(0, 2, 1))
    VP = np.stack([np.concatenate([V_T[2 * m], V_T[2 * m + 1]], axis=1)
                   for m in range(4)]).astype(np.float32)
    UP = np.stack([np.concatenate([U_T[2 * m], U_T[2 * m + 1]], axis=0)
                   for m in range(4)]).astype(np.float32)
    b1r_ = np.ascontiguousarray(
        np.asarray(inputs["L0_ffn_b1"]).reshape(16, 128).T)
    b2c_ = np.asarray(inputs["L0_ffn_W2"])[0:0]
    b2c_ = np.asarray(inputs["L0_ffn_b2"]).reshape(8, 1).astype(np.float32)
    SEL = np.zeros((8, 512), dtype=np.float32)
    for m in range(4):
        SEL[2 * m, m * 128:m * 128 + 64] = 1.0
        SEL[2 * m + 1, m * 128 + 64:(m + 1) * 128] = 1.0
    ipw = np.asarray(inputs["L1_attn_in_proj_w"])
    wpT_ = np.ascontiguousarray(ipw.T[:, 4096:8512])
    gpT_ = np.ascontiguousarray(ipw.T[:, 0:4096])
    cw = np.asarray(inputs["L1_attn_conv_w"])
    cwr_ = np.ascontiguousarray(
        cw.reshape(34, 128, 4).transpose(1, 0, 2).reshape(128, 136))
    cbr_ = np.ascontiguousarray(
        np.asarray(inputs["L1_attn_conv_b"]).reshape(34, 128).T)
    dtb_ = np.asarray(inputs["L1_attn_dt_bias"]).reshape(64, 1).astype(np.float32)
    alog_ = np.asarray(inputs["L1_attn_A_log"]).reshape(64, 1).astype(np.float32)
    dbc_ = np.tile(np.asarray(inputs["L1_attn_D"]).astype(np.float32), (128, 1))
    ehsel_ = np.zeros((64, 8192), dtype=np.float32)
    for h in range(64):
        ehsel_[h % 16 if False else h, h * 128:(h + 1) * 128] = 1.0
    ehsel_ = np.zeros((64, 8192), dtype=np.float32)
    for g in range(4):
        for l in range(16):
            ehsel_[g * 16 + l, g * 2048 + l * 128:g * 2048 + (l + 1) * 128] = 1.0
    tri = np.tril(np.ones((128, 128), dtype=np.float32)).T
    opT_ = np.ascontiguousarray(np.asarray(inputs["L1_attn_out_proj_w"]).T)
    nwr_ = np.ascontiguousarray(
        np.asarray(inputs["L1_attn_norm_w"]).reshape(32, 128).T)
    WupT_ = np.ascontiguousarray(np.asarray(inputs["L1_ffn_W_up"]).T)
    WgtT_ = np.ascontiguousarray(np.asarray(inputs["L1_ffn_W_gate"]).T)
    WdnT_ = np.ascontiguousarray(np.asarray(inputs["L1_ffn_W_down"]).T)
    pack = lambda v, n: np.ascontiguousarray(
        np.asarray(v).reshape(n, 128).T.astype(np.float32))
    wsu = np.asarray(inputs["L1_ffn_Wscale_ug"])
    Hm = np.array([[1.0]])
    while Hm.shape[0] < 128:
        Hm = np.block([[Hm, Hm], [Hm, -Hm]])
    ins = []
    for c in range(NCORES):
        b, q = c // 4, c % 4
        tok0 = q * NTOK
        lo = max(0, tok0 - 640)
        xb = np.zeros((D, BAND2), dtype=np.float32)
        xb[:, 640 - (tok0 - lo):] = x[b, lo:tok0 + NTOK].T
        csb = np.zeros((128, BAND2), dtype=np.float32)
        csb[:, 640 - (tok0 - lo):] = cs[:, lo:tok0 + NTOK]
        csb[0:64, :640 - (tok0 - lo)] = 1.0
        loq = max(0, tok0 - 128)
        csq = np.zeros((128, 2176), dtype=np.float32)
        csq[:, 128 - (tok0 - loq):] = cs[:, loq:tok0 + NTOK]
        csq[0:64, :128 - (tok0 - loq)] = 1.0
        sw = lambda a: np.ascontiguousarray(
            np.concatenate([a[64:128], a[0:64]], axis=0))
        M = np.empty((17, 128, 896), dtype=np.float32)
        ii = np.arange(128)
        for Qg in range(17):
            iabs = tok0 - 128 + Qg * 128 + ii[:, None]
            for s7 in range(7):
                if s7 < 5:
                    jabs = tok0 - 640 + (Qg + s7) * 128 + ii[None, :]
                else:
                    jabs = (s7 - 5) * 128 + ii[None, :]
                allowed = ((jabs <= iabs) & (jabs > iabs - 512)) | \
                          ((jabs < 256) & (jabs <= iabs - 512))
                allowed &= (jabs >= 0)
                if s7 >= 5:
                    fs0 = (s7 - 5) * 128
                    bandlo = tok0 - 640 + Qg * 128
                    bandhi = bandlo + 640
                    if bandlo <= fs0 and bandhi >= fs0 + 128:
                        allowed = np.zeros_like(allowed)
                M[Qg, :, s7 * 128:(s7 + 1) * 128] = np.where(
                    allowed, 0.0, -1e30)
        mv1_ = np.zeros((128, 3), dtype=np.float32)
        mv2_ = np.ones((128, 3), dtype=np.float32)
        for p in range(3):
            if p < q:
                mv1_[:, p] = 1.0
                mv2_[:, p] = 0.0
        ins.append({
            "x_band": np.ascontiguousarray(xb),
            "x_front": np.ascontiguousarray(x[b, 0:FRONT].T),
            "wq": wqP, "wk": wkP, "wv": wvv, "wo": wov,
            "cs_band": np.ascontiguousarray(csb), "cs2_band": sw(csb),
            "cs_front": np.ascontiguousarray(cs[:, 0:FRONT]),
            "cs2_front": sw(np.ascontiguousarray(cs[:, 0:FRONT])),
            "cs_q": np.ascontiguousarray(csq), "cs2_q": sw(csq),
            "masks": M.astype(BF),
            "identb": ident.astype(BF), "identf": ident.astype(np.float32),
            "W1": np.ascontiguousarray(np.asarray(inputs["L0_ffn_W1"])),
            "b1r": b1r_, "W2": np.ascontiguousarray(
                np.asarray(inputs["L0_ffn_W2"])), "b2c": b2c_,
            "VP": VP, "UP": UP, "SEL": SEL,
            "wpT": wpT_, "cwr": cwr_, "cbr": cbr_, "dtb": dtb_,
            "hmask": np.full((128, 1), 0.0 if q == 0 else 1.0, np.float32),
            "alog": alog_, "dbc_in": dbc_, "ehsel": ehsel_,
            "trim": tri.astype(BF), "mv1": mv1_, "mv2": mv2_,
            "gpT": gpT_, "opT": opT_, "nwr": nwr_,
            "WupT": WupT_, "WgtT": WgtT_, "WdnT": WdnT_,
            "suug": pack(inputs["L1_ffn_SU_ug"], 16),
            "sudp": pack(inputs["L1_ffn_SU_dp"], 64),
            "wsup": pack(wsu[:8192], 64), "wsgt": pack(wsu[8192:], 64),
            "wsdp": pack(inputs["L1_ffn_Wscale_dp"], 16),
            "h128": Hm.astype(BF),
        })
    return ins


def kernel(**inputs):
    nc = build_fused()
    ins = fused_inputs(inputs)
    res = run_bass_kernel_spmd(nc, ins, core_ids=list(range(NCORES)))
    out = np.empty((2, 8192, 2048), dtype=np.float32)
    for c in range(NCORES):
        b, q = c // 4, c % 4
        out[b, q * 2048:(q + 1) * 2048] = res.results[c]["outT"].T
    return out
